# version 05
# baseline (speedup 1.0000x reference)
import sys
import numpy as np
import ml_dtypes
import os
from contextlib import ExitStack
import concourse.bass as bass
import concourse.mybir as mybir
from concourse.bass_utils import run_bass_kernel_spmd

F32 = mybir.dt.float32
BF16 = mybir.dt.bfloat16
AF = mybir.ActivationFunctionType
ALU = mybir.AluOpType
AX = mybir.AxisListType

NCORES = 8
S = 256
B = 8
T = 2048
D = 1024
DC = 8
MAGIC = 12582912.0
NEG = -3.0e38
EPS_LN = 1e-5
I_FF = 2816
IC = 22
RG = [list(range(NCORES))]
bfd = ml_dtypes.bfloat16


class _EngProxy:
    def __init__(self, e, plan):
        self._e = e
        self._plan = plan

    def __getattr__(self, name):
        real = getattr(self._e, name)
        if not callable(real):
            return real
        plan = self._plan

        def wrap(*args, **kwargs):
            for o in list(args) + list(kwargs.values()):
                for item in (o if isinstance(o, (list, tuple)) else (o,)):
                    sp = getattr(item, "space", None)
                    if sp is not None and getattr(sp, "value", None) == "psum":
                        plan.note_psum(item)
            return real(*args, **kwargs)

        return wrap


class Plan:
    ENGS = ("tensor", "vector", "scalar", "gpsimd", "sync")
    OWN = {"tensor": "T", "vector": "V", "scalar": "A", "gpsimd": "G", "sync": "SY"}
    RING = {"sync": 6, "scalar": 2, "gpsimd": 4, "vector": 2}

    def __init__(self, nc):
        self.nc = nc
        self.streams = {e: [] for e in self.ENGS}
        self.sems = {}
        self.cnt = {}
        self.hwm = {e: {} for e in self.ENGS}
        for s in ("T", "V", "A", "G", "SY"):
            self.sems[s] = nc.alloc_semaphore("sem_" + s)
            self.cnt[s] = 0
        self.dma_use = {}
        for e, r in self.RING.items():
            self.dma_use[e] = 0
            for j in range(r):
                s = f"d{e}{j}"
                self.sems[s] = nc.alloc_semaphore("sem_" + s)
                self.cnt[s] = 0
        self.groups = []
        self.sem_hist = {}
        self.cur_gid = None
        self.bank_acc = {}
        self.reach = []
        self.eng_last = {}
        self.violations = []

    def _filter(self, eng, waits):
        out, h, own = [], self.hwm[eng], self.OWN[eng]
        for ev in waits:
            if ev is None:
                continue
            s, v = ev
            if v <= 0 or h.get(s, 0) >= v:
                continue
            h[s] = v
            out.append((s, v))
        return out

    def emit(self, eng, fns, waits=(), dma=False):
        if callable(fns):
            fns = [fns]
        fns = [f for f in fns if f is not None]
        assert fns, "empty emit"
        if dma:
            idx = self.dma_use[eng]
            self.dma_use[eng] += 1
            sem = f"d{eng}{idx % self.RING[eng]}"
            waits = list(waits) + [(sem, self.cnt[sem])]
            self.cnt[sem] += 16 * len(fns)
        else:
            sem = self.OWN[eng]
            self.cnt[sem] += 1
        val = self.cnt[sem]
        w = tuple(self._filter(eng, waits))
        semh = self.sems[sem]
        plan = self

        src_line = sys._getframe(1).f_lineno
        gid = len(self.groups)
        self.groups.append((eng, sem, val, w, src_line))
        self.sem_hist.setdefault(sem, []).append((val, gid))
        r = 1 << gid
        if eng in self.eng_last:
            r |= self.reach[self.eng_last[eng]]
        for (s, v) in w:
            tg = self._sem_gid(s, v)
            if tg is not None:
                r |= self.reach[tg]
        self.reach.append(r)
        self.eng_last[eng] = gid

        def closure(e, fns=tuple(fns), w=w, semh=semh, dma=dma, src_line=src_line,
                    gid=gid):
            plan.cur_gid = gid
            ep = _EngProxy(e, plan)
            try:
                for (s, v) in w:
                    e.wait_ge(plan.sems[s], v)
                if dma:
                    for f in fns:
                        f(ep).then_inc(semh, 16)
                else:
                    ins = None
                    for f in fns:
                        ins = f(ep)
                    ins.then_inc(semh, 1)
            except Exception:
                print(f"[plan] failure in group emitted at kernel.py:{src_line}")
                raise

        self.streams[eng].append(closure)
        return (sem, val)

    def _sem_gid(self, s, v):
        hist = self.sem_hist.get(s)
        if not hist:
            return None
        lo, hi = 0, len(hist)
        while lo < hi:
            mid = (lo + hi) // 2
            if hist[mid][0] >= v:
                hi = mid
            else:
                lo = mid + 1
        return hist[lo][1] if lo < len(hist) else None

    def note_psum(self, ap):
        g = self.cur_gid
        eng = self.groups[g][0]
        t = ap.tensor
        esz = 2 if t.dtype == BF16 else 4
        pairs = list(ap.ap)
        stride0 = pairs[0][0] if pairs else 1
        off = ap.offset % stride0 if stride0 > 0 else ap.offset
        span = 1
        for (st, n) in pairs[1:]:
            span += (n - 1) * abs(st)
        lo_b = (off * esz) // 2048
        hi_b = ((off + span) * esz - 1) // 2048
        for b in range(lo_b, hi_b + 1):
            key = (t.name, b)
            acc = self.bank_acc.setdefault(key, [])
            myr = self.reach[g]
            for (pg, peng) in acc:
                if peng != eng and not (myr >> pg) & 1 and not (self.reach[pg] >> g) & 1:
                    self.violations.append(
                        (t.name, b, peng, self.groups[pg][4], eng, self.groups[g][4],
                         pg, g))
            if not acc or acc[-1][0] != g:
                acc.append((g, eng))

    def psum_report(self):
        seen = set()
        import os as _os
        verbose = _os.environ.get("KPSUM_VERBOSE")
        for v in self.violations:
            if v[:6] in seen:
                continue
            seen.add(v[:6])
            print(f"[psumchk] UNORDERED: {v[0]} bank{v[1]}: "
                  f"{v[2]}@L{v[3]} vs {v[4]}@L{v[5]}")
            if verbose:
                for gg in (v[6], v[7]):
                    eng, sem, val, w, ln = self.groups[gg]
                    print(f"    gid{gg}: {eng} {sem}={val} L{ln} waits={list(w)}")
        return len(seen)

    def play(self):
        with self.nc.Block() as block:
            @block.tensor
            def _(e):
                for f in self.streams["tensor"]:
                    f(e)

            @block.vector
            def _(e):
                for f in self.streams["vector"]:
                    f(e)

            @block.scalar
            def _(e):
                for f in self.streams["scalar"]:
                    f(e)

            @block.gpsimd
            def _(e):
                for f in self.streams["gpsimd"]:
                    f(e)

            @block.sync
            def _(e):
                for f in self.streams["sync"]:
                    f(e)


class StopPlan(Exception):
    pass


def build_program(Dr, dbg=(), stop_after=None):
    nc = bass.Bass("TRN2", target_bir_lowering=False, debug=False,
                   dynamic_dma_scratch_size=8192)
    p = Plan(nc)

    def cut(tag):
        if stop_after is not None and tag == stop_after:
            _finish(nc, p, None, None)
            err = StopPlan()
            err.nc = nc
            raise err
    TE, V, A, G, SY = "tensor", "vector", "scalar", "gpsimd", "sync"

    def di(name, shape, dt=F32):
        return nc.dram_tensor(name, list(shape), dt, kind="ExternalInput").ap()

    def dint(name, shape, dt=F32):
        return nc.dram_tensor(name, list(shape), dt, kind="Internal").ap()

    dbg_t = {}
    dbg_ev = []

    def dbg_out(name, shape, dt=F32):
        if name in dbg:
            dbg_t[name] = nc.dram_tensor("dbg_" + name, list(shape), dt, kind="ExternalOutput").ap()
            return dbg_t[name]
        return None

    x_fm = di("x_fm", [D, T])
    x_halo = di("x_halo", [D, 32])
    qkv_w = di("qkv_w", [D, 1536])
    qkvw_head = di("qkvw_head", [D, 128])
    wv_w = di("wv_w", [D, D])
    wv_b = di("wv_b", [128, 8])
    out_w = di("out_w", [512, D])
    out_b = di("out_b", [128, 8])
    wq_g = di("wq_g", [D, 256])
    wk_g = di("wk_g", [D, 256])
    conv_taps = di("conv_taps", [512, 33])
    sel_head = di("sel_head", [512, 64])
    moe_w1 = di("moe_w1", [8, D, 512])
    moe_b1 = di("moe_b1", [128, 32])
    moe_w2 = di("moe_w2", [8, 512, D])
    moe_b2 = di("moe_b2", [8, D])
    cent = di("cent", [8, D])
    cent_T = di("cent_T", [D, 8])
    route_T = di("route_T", [D, 8])
    wt_can = [di(f"wt_can{d}", [T, S]) for d in range(Dr)]
    l1_w_T = di("l1_w_T", [3, D, D])
    l1_bqk = di("l1_bqk", [128, 16])
    l1_bv_row = di("l1_bv_row", [1, D])
    l1_clip = di("l1_clip", [1, 2])
    wg_nat = di("wg_nat", [I_FF, D])
    wu_nat = di("wu_nat", [I_FF, D])
    wd_nat = di("wd_nat", [D, I_FF])
    wg_T = di("wg_T", [D, I_FF])
    wu_T = di("wu_T", [D, I_FF])
    wd_T = di("wd_T", [I_FF, D])
    out_t = nc.dram_tensor("out_il", [D, T], F32, kind="ExternalOutput").ap()

    ag_lm_in = dint("ag_lm_in", [D, S])
    ag_lm_out = dint("ag_lm_out", [NCORES, D, S])
    ag_kv_in = dint("ag_kv_in", [1024, S], BF16)
    ag_kv_out = dint("ag_kv_out", [NCORES, 1024, S], BF16)
    ar2_in = dint("ar2_in", [1, 2])
    ar2_out = dint("ar2_out", [1, 2])
    ag_u_in = dint("ag_u_in", [S, 64])
    ag_u_out = dint("ag_u_out", [NCORES, S, 64])
    ag_kg_in = dint("ag_kg_in", [S, 256])
    ag_kg_out = dint("ag_kg_out", [NCORES, S, 256])
    ar_a_in = dint("ar_a_in", [1, 1])
    ar_a_out = dint("ar_a_out", [1, 1])
    ar_s_in = dint("ar_s_in", [1, 1])
    ar_s_out = dint("ar_s_out", [1, 1])
    ar_x_in = dint("ar_x_in", [1, 1])
    ar_x_out = dint("ar_x_out", [1, 1])
    ar_h_in = dint("ar_h_in", [1, 1])
    ar_h_out = dint("ar_h_out", [1, 1])
    g_dram = dint("g_dram", [D, T])
    xnew_dram = dint("xnew_dram", [D, T])
    w1bf_dram = dint("w1bf_dram", [8, DC, 128, 512], BF16)
    w2bf_dram = dint("w2bf_dram", [8, 8, 4, 128, 128], BF16)
    o_dram = dint("o_dram", [16, 128, D], BF16)
    grow_dram = dint("grow_dram", [1, 8 * T], BF16)
    wgq_dram = dint("wgq_dram", [DC, 128, I_FF], BF16)
    wuq_dram = dint("wuq_dram", [DC, 128, I_FF], BF16)

    identb_d = nc.inline_tensor(np.eye(128, dtype=bfd), name="c_identb").ap()
    identf_d = nc.inline_tensor(np.eye(128, dtype=np.float32), name="c_identf").ap()
    mask_np = np.where(np.kron(np.eye(16), np.ones((8, 8))) > 0, 0.0, NEG).astype(np.float32)
    maskb_d = nc.inline_tensor(mask_np, name="c_maskb").ap()
    dg = np.zeros((128, 512), np.float32)
    dg[np.arange(128), np.arange(128)] = 1.0
    dg[np.arange(128), 256 + 128 + np.arange(128)] = 1.0
    diagI_d = nc.inline_tensor(dg, name="c_diagI").ap()

    xring = [nc.alloc_sbuf_tensor(f"xring{j}", [128, T], F32) for j in range(2)]
    sq_s = nc.alloc_sbuf_tensor("sq_s", [128, T], F32)
    ones1 = nc.alloc_sbuf_tensor("ones1", [1, 128], F32)
    ones128 = nc.alloc_sbuf_tensor("ones128", [128, 1], F32)
    ones1b = nc.alloc_sbuf_tensor("ones1b", [1, 128], BF16)
    identb = nc.alloc_sbuf_tensor("identb", [128, 128], BF16)
    identf = nc.alloc_sbuf_tensor("identf", [128, 128], F32)
    maskb = nc.alloc_sbuf_tensor("maskb", [128, 128], F32)
    diagI = nc.alloc_sbuf_tensor("diagI", [128, 512], F32)
    cconst = nc.alloc_sbuf_tensor("cconst", [128, 5], F32)
    srow = nc.alloc_sbuf_tensor("srow", [1, 512], F32)
    scol = nc.alloc_sbuf_tensor("scol", [128, 16], F32)
    misc = nc.alloc_sbuf_tensor("misc", [128, 512], F32)

    ev_const = p.emit(V, [
        lambda e: e.memset(ones1[:], 1.0),
        lambda e: e.memset(ones128[:], 1.0),
        lambda e: e.memset(ones1b[:], 1.0),
        lambda e: e.memset(scol[0:64, 0:1], 0.125 / 8.0),
        lambda e: e.memset(scol[64:128, 0:1], 1.0 / 8.0),
        lambda e: e.memset(cconst[:, 0:1], EPS_LN),
        lambda e: e.memset(cconst[:, 1:2], 0.125),
        lambda e: e.memset(cconst[:, 2:3], 0.125 / 8.0),
        lambda e: e.memset(cconst[:, 3:4], -3.25),
        lambda e: e.memset(cconst[:, 4:5], -1.0),
    ])
    ev_cld = p.emit(SY, [
        lambda e: e.dma_start(out=identb[:], in_=identb_d[:, :]),
        lambda e: e.dma_start(out=identf[:], in_=identf_d[:, :]),
        lambda e: e.dma_start(out=maskb[:], in_=maskb_d[:, :]),
        lambda e: e.dma_start(out=diagI[:], in_=diagI_d[:, :]),
    ], dma=True)

    def mm(out_ap, lhsT_ap, rhs_ap, start, stop):
        return lambda e: e.matmul(out=out_ap, lhsT=lhsT_ap, rhs=rhs_ap, start=start, stop=stop)

    def act(out_ap, in_ap, func=AF.Copy, bias=0.0, scale=1.0):
        return lambda e: e.activation(out=out_ap, in_=in_ap, func=func, bias=bias, scale=scale)

    def tt(out_ap, a_ap, b_ap, op):
        return lambda e: e.tensor_tensor(out=out_ap, in0=a_ap, in1=b_ap, op=op)

    def ts(out_ap, in_ap, s1, s2=None, op0=ALU.mult, op1=ALU.bypass):
        return lambda e: e.tensor_scalar(out=out_ap, in0=in_ap, scalar1=s1, scalar2=s2,
                                         op0=op0, op1=op1)

    def stt(out_ap, in0_ap, scalar, in1_ap, op0, op1):
        return lambda e: e.scalar_tensor_tensor(out=out_ap, in0=in0_ap, scalar=scalar,
                                                in1=in1_ap, op0=op0, op1=op1)

    def red(out_ap, in_ap, op=ALU.add, absv=None, neg=None):
        return lambda e: e.tensor_reduce(out=out_ap, in_=in_ap, axis=AX.X, op=op,
                                         apply_absolute_value=absv, negate=neg)

    def recip(out_ap, in_ap):
        return lambda e: e.reciprocal(out=out_ap, in_=in_ap)

    def cp(out_ap, in_ap):
        return lambda e: e.tensor_copy(out=out_ap, in_=in_ap)

    def dmaf(dst, src):
        return lambda e: e.dma_start(out=dst, in_=src)

    def trb(out_psum_ap, in_ap):
        np_ = in_ap.shape[0]
        return lambda e: e.transpose(out_psum_ap, in_ap, identb[0:np_, 0:np_])

    def trf(out_psum_ap, in_ap):
        np_ = in_ap.shape[0]
        return lambda e: e.transpose(out_psum_ap, in_ap, identf[0:np_, 0:np_])

    def coll(kind, op, in_ap, out_ap, waits):
        return p.emit(G, lambda e: e.collective_compute(
            kind, op, replica_groups=RG, ins=[in_ap], outs=[out_ap]), waits=waits)

    def b0ap(tile):
        return tile[:].rearrange("p (s b) -> p s b", b=8)[:, :, 0]

    def bc(col_ap, n):
        return col_ap.broadcast_to((col_ap.shape[0], n))

    def dump(name, src_ap, waits, shape=None):
        t = dbg_t.get(name)
        if t is None:
            return None
        ev = p.emit(SY, dmaf(t[:, :] if t.ndim == 2 else t[:, :, :], src_ap), waits=waits, dma=True)
        dbg_ev.append(ev)
        return ev

    L0 = ExitStack()

    def sb0(name, shape, dt=F32):
        return L0.enter_context(nc.sbuf_tensor(name, list(shape), dt))

    xn_bf = [sb0(f"xn_bf{k}", [128, T], BF16) for k in range(DC)]
    m_b = sb0("m_b", [128, T])
    istd_b = sb0("istd_b", [128, T])
    var_b = sb0("var_b", [128, T])
    xg0 = [sb0(f"xg0_{k}", [128, S]) for k in range(DC)]
    alpha_b = sb0("alpha_b", [128, S])
    qkvw_b = [sb0(f"qkvw{k}", [128, 1536], BF16) for k in range(DC)]
    biasc = sb0("biasc", [128, 16])
    bc_ps = None

    bc_last = [None]

    def bcast_row(dst_tile, row_ap, n, waits, scale=None):
        evs = []
        nt = (n + 511) // 512
        for j in range(nt):
            w = min(512, n - 512 * j)
            ev_m = p.emit(TE, mm(bc_ps[:, 0:w], ones1[:, 0:128], row_ap[0:1, 512 * j:512 * j + w],
                                 True, True),
                          waits=list(waits) + [evs[-1] if evs else bc_last[0]])
            if scale is None:
                ev_c = p.emit(V, cp(dst_tile[:, 512 * j:512 * j + w], bc_ps[:, 0:w]), waits=[ev_m])
            else:
                ev_c = p.emit(V, ts(dst_tile[:, 512 * j:512 * j + w], bc_ps[:, 0:w], scale), waits=[ev_m])
            evs.append(ev_c)
        bc_last[0] = evs[-1]
        return evs[-1]

    ev_bias = p.emit(SY, [dmaf(biasc[:, 0:8], wv_b[:, :]), dmaf(biasc[:, 8:16], out_b[:, :])],
                     dma=True)
    ev_qkv_cast = []
    for k in range(DC):
        evl = p.emit(SY, dmaf(xring[0][:, 0:1536], qkv_w[128 * k:128 * (k + 1), :]),
                     waits=[ev_qkv_cast[-1]] if ev_qkv_cast else [], dma=True)
        evc = p.emit(V, cp(qkvw_b[k][:], xring[0][:, 0:1536]), waits=[evl])
        ev_qkv_cast.append(evc)
    ev_qkvw = ev_qkv_cast[-1]

    MW = ExitStack()
    wstg = MW.enter_context(nc.sbuf_tensor("wstg", [128, 512], BF16))
    ev_w_cast = ev_qkvw
    ev_st_prev = None
    for e_i in range(8):
        for k in range(DC):
            evl = p.emit(SY, dmaf(xring[0][:, 0:512], moe_w1[e_i, 128 * k:128 * (k + 1), :]),
                         waits=[ev_w_cast], dma=True)
            evc = p.emit(V, cp(wstg[:], xring[0][:, 0:512]), waits=[evl, ev_st_prev])
            evs = p.emit(SY, dmaf(w1bf_dram[e_i, k, :, :], wstg[:]), waits=[evc], dma=True)
            ev_w_cast, ev_st_prev = evc, evs
    for e_i in range(8):
        for kc in range(4):
            evl = p.emit(SY, dmaf(xring[0][:, 0:1024], moe_w2[e_i, 128 * kc:128 * (kc + 1), :]),
                         waits=[ev_w_cast], dma=True)
            evc = p.emit(V, cp(wstg[:, 0:512], xring[0][:, 0:512]), waits=[evl, ev_st_prev])
            evc2 = p.emit(V, cp(sq_s[:, 0:512].bitcast(BF16)[:, 0:512] if False else wstg[:, 0:512],
                                xring[0][:, 512:1024]), waits=[evc]) if False else None
            evs = p.emit(SY, [dmaf(w2bf_dram[m, e_i, kc, :, :], wstg[:, 128 * (m % 4):128 * (m % 4) + 128])
                              for m in range(4)], waits=[evc], dma=True)
            evc3 = p.emit(V, cp(wstg[:], xring[0][:, 512:1024]), waits=[evs])
            evs2 = p.emit(SY, [dmaf(w2bf_dram[4 + m, e_i, kc, :, :], wstg[:, 128 * m:128 * m + 128])
                               for m in range(4)], waits=[evc3], dma=True)
            ev_w_cast, ev_st_prev = evc3, evs2
    ev_moe_w = (ev_w_cast, ev_st_prev)
    MW.close()
    cut("mw")

    RWD = ExitStack()
    rows = RWD.enter_context(nc.sbuf_tensor("rows", [1, 2 * T], F32))
    LNs = ExitStack()
    ps_sx = LNs.enter_context(nc.psum_tensor("ps_sx", [1, T], F32))
    ps_sq = LNs.enter_context(nc.psum_tensor("ps_sq", [1, T], F32))
    ev_x_ok = [[ev_moe_w[0]], [None]]
    mm_sx, mm_sq = [], []
    for k in range(DC):
        r = k % 2
        evl = p.emit(SY, dmaf(xring[r][:], x_fm[128 * k:128 * (k + 1), :]),
                     waits=ev_x_ok[r] + [ev_moe_w[1]], dma=True)
        evsq = p.emit(A, act(sq_s[:], xring[r][:], AF.Square),
                      waits=[evl] + ([mm_sq[-1]] if mm_sq else [ev_const]))

        def mk(dst, src, k=k):
            return [mm(dst[:, 512 * j:512 * (j + 1)], ones128[:, 0:1],
                       src[:, 512 * j:512 * (j + 1)], k == 0, k == DC - 1) for j in range(4)]

        m1 = p.emit(TE, mk(ps_sx, xring[r]), waits=[evl])
        mm_sx.append(m1)
        m2 = p.emit(TE, mk(ps_sq, sq_s), waits=[evsq])
        mm_sq.append(m2)
        ev_x_ok[r] = [m1, evsq]
    ev_mrow = p.emit(V, ts(rows[0:1, 0:T], ps_sx[:, :], 1.0 / D), waits=[mm_sx[-1]])
    ev_qrow = p.emit(V, ts(rows[0:1, T:2 * T], ps_sq[:, :], 1.0 / D), waits=[mm_sq[-1]])
    LNs.close()
    bc_ps = L0.enter_context(nc.psum_tensor("bc_ps", [128, 512], F32))
    ev_mb = bcast_row(m_b, rows[0:1, 0:T], T, [ev_mrow])
    ev_vb0 = bcast_row(var_b, rows[0:1, T:2 * T], T, [ev_qrow])
    RWD.close()
    cut("ln0")
    ev_m2 = p.emit(V, tt(sq_s[:], m_b[:], m_b[:], ALU.mult), waits=[ev_mb, ev_qrow])
    ev_var = p.emit(V, tt(var_b[:], var_b[:], sq_s[:], ALU.subtract), waits=[ev_m2, ev_vb0])
    ev_sd = p.emit(A, act(sq_s[:], var_b[:], AF.Sqrt, bias=cconst[:, 0:1]), waits=[ev_var])
    ev_istd = p.emit(V, recip(istd_b[:], sq_s[:]), waits=[ev_sd])

    PL = ExitStack()
    pool_t = [PL.enter_context(nc.sbuf_tensor(f"pool{j}", [128, S], F32)) for j in range(2)]
    ev_xn = []
    ev_x2_ok = [ev_istd, ev_istd]
    ev_pool_ok = [None, None]
    ev_pool_st = []
    for k in range(DC):
        r = k % 2
        evl = p.emit(SY, dmaf(xring[r][:], x_fm[128 * k:128 * (k + 1), :]),
                     waits=[ev_x2_ok[r]], dma=True)
        evs = p.emit(V, tt(sq_s[:], xring[r][:], m_b[:], ALU.subtract), waits=[evl, ev_istd])
        evn = p.emit(V, tt(xn_bf[k][:], sq_s[:], istd_b[:], ALU.mult), waits=[evs])
        ev_xn.append(evn)
        ev_x2_ok[r] = evn
        evp = p.emit(V, red(pool_t[r][:].rearrange("p (g b) -> p g b", b=8),
                            xn_bf[k][:].rearrange("p (g j b) -> p g b j", g=32, j=8),
                            ALU.add),
                     waits=[evn, ev_pool_ok[r]])
        evst = p.emit(SY, dmaf(ag_lm_in[128 * k:128 * (k + 1), :], pool_t[r][:]),
                      waits=[evp], dma=True)
        ev_pool_st.append(evst)
        ev_pool_ok[r] = evst
    ev_ag_lm = coll("AllGather", ALU.bypass, ag_lm_in[:, :], ag_lm_out[:, :, :],
                    list(ev_pool_st))
    PL.close()
    cut("aglm")
    dump("xn", xn_bf[0][:], [ev_xn[0]])

    NY = ExitStack()

    def sbn(name, shape, dt=F32):
        return NY.enter_context(nc.sbuf_tensor(name, list(shape), dt))

    q0 = [sbn(f"q0_{j}", [128, S], BF16) for j in range(4)]
    k_l0 = [sbn(f"kl0_{j}", [128, S], BF16) for j in range(4)]
    ql_c = sbn("ql_c", [64, T], BF16)
    kl_c = sbn("kl_c", [64, T], BF16)
    v0ext = [sbn(f"v0e{j}", [128, S + 32], BF16) for j in range(4)]
    res_fm = [sbn(f"res{j}", [128, S]) for j in range(4)]
    convtmp = sbn("convtmp", [128, S])
    taps = [sbn(f"taps{j}", [128, 33]) for j in range(4)]
    qh_w = [sbn(f"qhw{k}", [128, 128], BF16) for k in range(DC)]
    enc0b = [sbn(f"enc0b_{k}", [128, S], BF16) for k in range(DC)]
    halo_x = sbn("halo_x", [128, 256])
    halo_n = sbn("halo_n", [128, 256], BF16)
    kst = sbn("kst", [128, S], BF16)
    lmr0 = sbn("lmr0", [128, T], BF16)
    lmr = [lmr0, lmr0]
    a2e = [sbn(f"a2e_{j}", [128, 256]) for j in range(2)]
    a20b = [sbn(f"a20b_{j}", [128, 256], BF16) for j in range(2)]
    racc = sbn("racc", [128, 2])
    a2cols = sbn("a2cols", [128, 16])
    k0c = sbn("k0c", [64, T], BF16)
    v0c = sbn("v0c", [64, T], BF16)
    v0ctm = sbn("v0ctm", [128, 16 * 64], BF16)
    selb = sbn("selb", [128, 4 * 64], BF16)
    kvring = [lmr0, lmr0]
    at3e0 = sbn("at3e0", [128, T], BF16)
    att3e = [at3e0, at3e0]
    att3rc = sbn("at3rc", [128, 2])
    attn3T = sbn("at3T", [128, 16 * 256], BF16)
    avb = [sbn(f"av{j}", [128, 64], BF16) for j in range(2)]
    a2T = [sbn(f"a2T{j}", [128, 256], BF16) for j in range(2)]
    z_t = [sbn(f"z{j}", [128, 256], BF16) for j in range(2)]
    zT_t = [sbn(f"zT{j}", [128, 256], BF16) for j in range(2)]
    xz_t = [sbn(f"xz{j}", [128, 256], BF16) for j in range(2)]
    xzT_t = [sbn(f"xzT{j}", [128, 256], BF16) for j in range(2)]
    tm_t = [sbn(f"tm{j}", [128, 256], BF16) for j in range(2)]
    tm2_t = tm_t
    U_bf = [sbn(f"U_{j}", [128, 8 * 64], BF16) for j in range(2)]
    Uloc = [sbn(f"Ul{j}", [128, 64]) for j in range(2)]
    at1_0 = sbn("at1_0", [128, 8 * S], BF16)
    attn1 = [at1_0, at1_0]
    attn1T = [sbn(f"at1T_{j}", [128, 8 * S], BF16) for j in range(2)]
    xgp = [sbn(f"xgp{j}", [128, S], BF16) for j in range(4)]
    oww = [sbn(f"oww{j}", [128, 512], BF16) for j in range(2)]
    wqg_b = [sbn(f"wqg{k}", [128, 256], BF16) for k in range(DC)]
    wkg_b = [sbn(f"wkg{k}", [128, 256], BF16) for k in range(DC)]
    q_g = [sbn(f"qg{j}", [128, S]) for j in range(2)]
    kgr0 = sbn("kgr0", [128, S])
    kgring = [kgr0, kgr0]
    canr0 = sbn("canr0", [128, S])
    canring = [canr0, canr0]
    stage = misc
    arow = srow

    evl = p.emit(SY, [dmaf(taps[j][:], conv_taps[128 * j:128 * (j + 1), :]) for j in range(4)],
                 dma=True)
    ev_taps = evl
    ev_qhw = ev_qkvw
    for g2 in range(2):
        evl = p.emit(SY, [dmaf(stage[:, 128 * k:128 * (k + 1)],
                               qkvw_head[128 * (4 * g2 + k):128 * (4 * g2 + k + 1), :])
                          for k in range(4)], waits=[ev_qhw], dma=True)
        ev_qhw = p.emit(V, [cp(qh_w[4 * g2 + k][:], stage[:, 128 * k:128 * (k + 1)])
                            for k in range(4)], waits=[evl])
    evl = p.emit(SY, [dmaf(stage[:, 64 * j:64 * (j + 1)], sel_head[128 * j:128 * (j + 1), :])
                      for j in range(4)], waits=[ev_qhw], dma=True)
    ev_sel = p.emit(V, [cp(selb[:, 64 * j:64 * (j + 1)], stage[:, 64 * j:64 * (j + 1)])
                        for j in range(4)], waits=[evl])
    ev_outw = ev_sel
    ev_wg = ev_outw
    for k in range(DC):
        evl = p.emit(SY, [dmaf(stage[:, 0:256], wq_g[128 * k:128 * (k + 1), :]),
                          dmaf(stage[:, 256:512], wk_g[128 * k:128 * (k + 1), :])],
                     waits=[ev_wg], dma=True)
        ev_wg = p.emit(V, [cp(wqg_b[k][:], stage[:, 0:256]),
                           cp(wkg_b[k][:], stage[:, 256:512])], waits=[evl])

    HL = ExitStack()
    ps_h = HL.enter_context(nc.psum_tensor("ps_h", [1, 64], F32))
    ev_hl = p.emit(SY, [dmaf(halo_x[:, 32 * k:32 * (k + 1)], x_halo[128 * k:128 * (k + 1), :])
                        for k in range(DC)], waits=[ev_sel], dma=True)
    ev_hsq = p.emit(A, act(misc[:, 0:256], halo_x[:, :], AF.Square), waits=[ev_hl, ev_wg])
    ev_hmm = p.emit(TE, [mm(ps_h[:, 0:32], ones128[:, 0:1], halo_x[:, 32 * k:32 * (k + 1)],
                            k == 0, k == DC - 1) for k in range(DC)] +
                    [mm(ps_h[:, 32:64], ones128[:, 0:1], misc[:, 32 * k:32 * (k + 1)],
                        k == 0, k == DC - 1) for k in range(DC)],
                    waits=[ev_hsq, ev_const])
    ev_hm = p.emit(V, ts(srow[0:1, 0:32], ps_h[:, 0:32], 1.0 / D), waits=[ev_hmm])
    ev_hq = p.emit(V, ts(srow[0:1, 32:64], ps_h[:, 32:64], 1.0 / D), waits=[ev_hmm])
    HL.close()
    ev_hm2 = p.emit(V, tt(srow[0:1, 64:96], srow[0:1, 0:32], srow[0:1, 0:32], ALU.mult),
                    waits=[ev_hm])
    ev_hvar = p.emit(V, tt(srow[0:1, 32:64], srow[0:1, 32:64], srow[0:1, 64:96], ALU.subtract),
                     waits=[ev_hm2, ev_hq])
    ev_hsd = p.emit(A, act(srow[0:1, 64:96], srow[0:1, 32:64], AF.Sqrt, bias=cconst[0:1, 0:1]),
                    waits=[ev_hvar])
    ev_histd = p.emit(V, recip(srow[0:1, 32:64], srow[0:1, 64:96]), waits=[ev_hsd])
    ev_hmb = bcast_row(convtmp[:, 0:32], srow[0:1, 0:32], 32, [ev_hm])
    ev_hib = bcast_row(convtmp[:, 32:64], srow[0:1, 32:64], 32, [ev_histd])
    ev_hxn = []
    for k in range(DC):
        e1 = p.emit(V, tt(misc[:, 0:32], halo_x[:, 32 * k:32 * (k + 1)], convtmp[:, 0:32],
                          ALU.subtract), waits=[ev_hmb] + ([ev_hxn[-1]] if ev_hxn else []))
        e2 = p.emit(V, tt(halo_n[:, 32 * k:32 * (k + 1)], misc[:, 0:32], convtmp[:, 32:64],
                          ALU.mult), waits=[e1, ev_hib])
        ev_hxn.append(e2)

    QP = ExitStack()
    ps_q = QP.enter_context(nc.psum_tensor("ps_q", [128, S], F32))
    ps_k = QP.enter_context(nc.psum_tensor("ps_k", [128, S], F32))
    ps_v = QP.enter_context(nc.psum_tensor("ps_v", [128, S], F32))
    ps_vh = QP.enter_context(nc.psum_tensor("ps_vh", [128, 32], F32))
    ev_kv_st = []
    prev = [ev_qkvw, ev_xn[-1], ev_xn[-2]]
    for m2 in range(4):
        e_q = p.emit(TE, [mm(ps_q[:, :], qkvw_b[k][:, 128 * m2:128 * (m2 + 1)], b0ap(xn_bf[k]),
                             k == 0, k == DC - 1) for k in range(DC)], waits=prev)
        c_q = p.emit(A, act(q0[m2][:], ps_q[:, :], AF.Copy, scale=cconst[:, 1:2]), waits=[e_q])
        e_k = p.emit(TE, [mm(ps_k[:, :], qkvw_b[k][:, 512 + 128 * m2:512 + 128 * (m2 + 1)],
                             b0ap(xn_bf[k]), k == 0, k == DC - 1) for k in range(DC)], waits=[c_q])
        c_k = p.emit(A, act(kst[:, :], ps_k[:, :], AF.Copy),
                     waits=[e_k] + ([ev_kv_st[-2]] if len(ev_kv_st) >= 2 else []))
        s_k = p.emit(SY, dmaf(ag_kv_in[128 * m2:128 * (m2 + 1), :], kst[:, :]), waits=[c_k], dma=True)
        e_v = p.emit(TE, [mm(ps_v[:, :], qkvw_b[k][:, 1024 + 128 * m2:1024 + 128 * (m2 + 1)],
                             b0ap(xn_bf[k]), k == 0, k == DC - 1) for k in range(DC)], waits=[c_k])
        c_v = p.emit(A, act(v0ext[m2][:, 16:16 + S], ps_v[:, :], AF.Copy), waits=[e_v])
        e_vh = p.emit(TE, [mm(ps_vh[:, :], qkvw_b[k][:, 1024 + 128 * m2:1024 + 128 * (m2 + 1)],
                              halo_n[:, 32 * k:32 * (k + 1)], k == 0, k == DC - 1) for k in range(DC)],
                      waits=[ev_hxn[-1], c_v])
        c_vh = p.emit(A, [act(v0ext[m2][:, 0:16], ps_vh[:, 0:16], AF.Copy),
                          act(v0ext[m2][:, S + 16:S + 32], ps_vh[:, 16:32], AF.Copy)], waits=[e_vh])
        s_v = p.emit(SY, dmaf(ag_kv_in[512 + 128 * m2:512 + 128 * (m2 + 1), :],
                              v0ext[m2][:, 16:16 + S]), waits=[c_v], dma=True)
        ev_kv_st += [s_k, s_v]
        prev = [c_vh]
    ev_ag_kv = coll("AllGather", ALU.bypass, ag_kv_in[:, :], ag_kv_out[:, :, :],
                    list(ev_kv_st))
    QP.close()
    cut("qkv")

    ev_conv = []
    for m2 in range(4):
        evs = [p.emit(V, ts(res_fm[m2][:], v0ext[m2][:, 0:S], taps[m2][:, 0:1]),
                      waits=[ev_taps, prev[0]])]
        for j in range(1, 33):
            src, dst = (res_fm[m2], convtmp) if j % 2 == 1 else (convtmp, res_fm[m2])
            evs.append(p.emit(V, stt(dst[:], v0ext[m2][:, j:j + S], taps[m2][:, j:j + 1], src[:],
                                     ALU.mult, ALU.add), waits=[evs[-1]]))
        ev_conv.append(evs[-1])

    LM = ExitStack()
    ps_ql = LM.enter_context(nc.psum_tensor("ps_ql", [64, T], F32))
    ev_lm_mm = []
    ev_lml = []
    for k in range(DC):
        r = k % 2
        evl = p.emit(SY, dmaf(xring[r][:].rearrange("p (r s) -> p r s", r=8),
                              ag_lm_out[:, 128 * k:128 * (k + 1), :].transpose([1, 0, 2])),
                     waits=[ev_ag_lm] + ([ev_lml[-2]] if len(ev_lml) >= 2 else [ev_x2_ok[r]]),
                     dma=True)
        evc = p.emit(V, cp(lmr[r][:], xring[r][:]),
                     waits=[evl] + ([ev_lm_mm[-1]] if ev_lm_mm else []))
        ev_lml.append(evc)
        ev_m = p.emit(TE, [mm(ps_ql[:, 512 * j:512 * (j + 1)], qh_w[k][:, 0:64],
                              lmr[r][:, 512 * j:512 * (j + 1)], k == 0, k == DC - 1)
                           for j in range(4)],
                      waits=[evc, ev_qhw])
        ev_lm_mm.append(ev_m)
    ev_qlkl = p.emit(A, [act(ql_c[:, 512 * j:512 * (j + 1)], ps_ql[:, 512 * j:512 * (j + 1)],
                             AF.Copy, scale=cconst[0:64, 2:3]) for j in range(4)],
                     waits=[ev_lm_mm[-1]])
    LM.close()
    LM2 = ExitStack()
    ps_klc = LM2.enter_context(nc.psum_tensor("ps_klc", [64, T], F32))
    ps_kl0 = LM2.enter_context(nc.psum_tensor("ps_kl0", [128, 4 * S], F32))
    ev_kl0mm = []
    for k in range(DC):
        r = k % 2
        evl = p.emit(SY, dmaf(xring[r][:].rearrange("p (r s) -> p r s", r=8),
                              ag_lm_out[:, 128 * k:128 * (k + 1), :].transpose([1, 0, 2])),
                     waits=([ev_kl0mm[-2]] if len(ev_kl0mm) >= 2 else [ev_lm_mm[-1], ev_qlkl]),
                     dma=True)
        evc = p.emit(V, cp(lmr[r][:], xring[r][:]),
                     waits=[evl] + ([ev_kl0mm[-1]] if ev_kl0mm else []))
        ev_m = p.emit(TE, [mm(ps_klc[:, 512 * j:512 * (j + 1)], qh_w[k][:, 64:128],
                              lmr[r][:, 512 * j:512 * (j + 1)], k == 0, k == DC - 1)
                           for j in range(4)] +
                      [mm(ps_kl0[:, S * m2:S * (m2 + 1)],
                          qkvw_b[k][:, 512 + 128 * m2:512 + 128 * (m2 + 1)],
                          lmr[r][:].rearrange("p (r g b) -> p (r g) b", r=8, b=8)[:, :, 0],
                          k == 0 and m2 % 2 == 0, k == DC - 1 and m2 % 2 == 1)
                       for m2 in range(4)],
                      waits=[evc])
        ev_kl0mm.append(ev_m)
    ev_klc = p.emit(A, [act(kl_c[:, 512 * j:512 * (j + 1)], ps_klc[:, 512 * j:512 * (j + 1)],
                            AF.Copy, scale=cconst[0:64, 1:2]) for j in range(4)],
                    waits=[ev_kl0mm[-1]])
    ev_kl0 = p.emit(A, [act(k_l0[m2][:], ps_kl0[:, S * m2:S * (m2 + 1)], AF.Copy, scale=cconst[:, 1:2])
                        for m2 in range(4)], waits=[ev_kl0mm[-1], ev_klc])
    LM2.close()
    cut("conv_lm")
    dump("qlkl", ql_c[:], [ev_qlkl])
    dump("kl0", k_l0[0][:], [ev_kl0])

    A2S = ExitStack()
    ps_a2 = A2S.enter_context(nc.psum_tensor("ps_a2", [128, 512], F32))
    ps_cs = A2S.enter_context(nc.psum_tensor("ps_cs", [128, 16], F32))
    ev_a2 = []
    ev_racc = None
    ev_a20 = None

    def lmslice(which, b, m2=None):
        src = ql_c if which == "q" else kl_c
        v = src[:, :].rearrange("p (r g bb) -> p (r g) bb", r=8, bb=8)[:, :, b]
        if m2 is None:
            return v
        return v[:, 128 * m2:128 * (m2 + 1)]

    for b in range(B):
        e_mm = p.emit(TE, [mm(ps_a2[:, 256 * m2:256 * (m2 + 1)], lmslice("q", b, m2),
                              lmslice("k", b), m2 == 0, m2 == 1) for m2 in range(2)],
                      waits=[ev_qlkl, ev_klc] + ([ev_a2[-1]] if ev_a2 else []))
        e_nm = p.emit(V, [red(scol[:, 2 + m2:3 + m2], ps_a2[:, 256 * m2:256 * (m2 + 1)],
                              ALU.max, neg=True) for m2 in range(2)], waits=[e_mm])
        e_ex = p.emit(A, [act(a2e[m2][:], ps_a2[:, 256 * m2:256 * (m2 + 1)], AF.Exp,
                              bias=scol[:, 2 + m2:3 + m2]) for m2 in range(2)], waits=[e_nm])
        e_rs = p.emit(V, [red(scol[:, 4 + m2:5 + m2], a2e[m2][:], ALU.add) for m2 in range(2)],
                      waits=[e_ex])
        e_rc = p.emit(V, [recip(scol[:, 6 + m2:7 + m2], scol[:, 4 + m2:5 + m2]) for m2 in range(2)],
                      waits=[e_rs])
        e_ra = p.emit(V, [tt(scol[:, 8:9], scol[:, 4:5], scol[:, 6:7], ALU.mult),
                          tt(scol[:, 9:10], scol[:, 5:6], scol[:, 7:8], ALU.mult)], waits=[e_rc])
        if b == 0:
            ev_racc = p.emit(V, tt(racc[:, 0:1], scol[:, 8:9], scol[:, 9:10], ALU.max),
                             waits=[e_ra])
            ev_a20 = p.emit(V, [ts(a20b[m2][:], a2e[m2][:], scol[:, 6 + m2:7 + m2])
                                for m2 in range(2)], waits=[e_rc])
        else:
            e_u1 = p.emit(V, tt(scol[:, 10:11], scol[:, 8:9], scol[:, 9:10], ALU.max), waits=[e_ra])
            ev_racc = p.emit(V, tt(racc[:, 0:1], racc[:, 0:1], scol[:, 10:11], ALU.max),
                             waits=[e_u1, ev_racc])
        e_cs = p.emit(TE, [mm(ps_cs[:, 2 * b + m2:2 * b + m2 + 1],
                              a2e[kc][:, 128 * m2:128 * (m2 + 1)], scol[:, 6 + kc:7 + kc],
                              kc == 0, kc == 1)
                           for m2 in range(2) for kc in range(2)],
                      waits=[e_rc] + ([ev_a20] if b == 0 else []))
        ev_a2.append(e_cs)
    e_cse = p.emit(V, cp(a2cols[:, :], ps_cs[:, :]), waits=[ev_a2[-1]])
    e_cmax = p.emit(V, red(racc[:, 1:2], a2cols[:, :], ALU.max), waits=[e_cse])
    ps_tr = A2S.enter_context(nc.psum_tensor("ps_tr", [128, 128], F32))
    e_tr = p.emit(TE, trf(ps_tr[0:2, :], racc[:, 0:2]), waits=[ev_racc, e_cmax, ev_cld])
    e_tre = p.emit(V, cp(convtmp[0:2, 0:128], ps_tr[0:2, :]), waits=[e_tr, ev_conv[-1]])
    e_mx = p.emit(V, red(scol[0:2, 12:13], convtmp[0:2, 0:128], ALU.max), waits=[e_tre])
    e_st2 = p.emit(G, dmaf(ar2_in[:, :], scol[0:2, 12:13].rearrange("p one -> one p") if False
                           else scol[0:2, 12:13]), waits=[e_mx], dma=True)
    ev_ar2 = coll("AllReduce", ALU.max, ar2_in[:, :], ar2_out[:, :], [e_st2])
    e_ld2 = p.emit(G, dmaf(srow[0:1, 100:102], ar2_out[:, :]), waits=[ev_ar2], dma=True)
    e_sc = p.emit(V, tt(srow[0:1, 102:103], srow[0:1, 100:101], srow[0:1, 101:102], ALU.mult),
                  waits=[e_ld2])
    e_rs2 = p.emit(V, recip(srow[0:1, 103:104], srow[0:1, 102:103]), waits=[e_sc])
    e_mmb = p.emit(TE, mm(ps_tr[:, 0:1], ones1[:, 0:128], srow[0:1, 103:104], True, True),
                   waits=[e_rs2])
    ev_rsc = p.emit(V, cp(scol[:, 11:12], ps_tr[:, 0:1]), waits=[e_mmb])
    A2S.close()
    cut("attn2")
    dump("a20", a20b[0][:], [ev_a20])

    SL = ExitStack()
    ps_sl = SL.enter_context(nc.psum_tensor("ps_sl", [64, T], F32))
    for ti, dst in ((0, k0c), (1, v0c)):
        ev_mm_l = []
        for kc in range(4):
            r = kc % 2
            evl = p.emit(SY, dmaf(kvring[r][:].rearrange("p (r s) -> p r s", r=8),
                                  ag_kv_out[:, 512 * ti + 128 * kc:512 * ti + 128 * (kc + 1), :]
                                  .transpose([1, 0, 2])),
                         waits=[ev_ag_kv, ev_kl0mm[-1]] + ([ev_mm_l[-1]] if ev_mm_l else
                                             ([ev_k0c] if ti == 1 else [])), dma=True)
            e_m = p.emit(TE, [mm(ps_sl[:, 512 * j:512 * (j + 1)], selb[:, 64 * kc:64 * (kc + 1)],
                                 kvring[r][:, 512 * j:512 * (j + 1)], kc == 0, kc == 3)
                              for j in range(4)],
                         waits=[evl, ev_sel, ev_rsc, ev_a20])
            ev_mm_l.append(e_m)
        ev_slc = p.emit(A, [act(dst[:, 512 * j:512 * (j + 1)], ps_sl[:, 512 * j:512 * (j + 1)],
                                AF.Copy) for j in range(4)], waits=[ev_mm_l[-1]])
        if ti == 0:
            ev_k0c = ev_slc
        else:
            ev_v0c = ev_slc
    SL.close()

    VT = ExitStack()
    ps_vt = VT.enter_context(nc.psum_tensor("ps_vt", [128, 1024], BF16))
    ev_v0tm = []
    for tgrp in range(2):
        e_t = p.emit(TE, [trb(ps_vt[:, 128 * j:128 * j + 64], v0c[:, 128 * (8 * tgrp + j):128 * (8 * tgrp + j + 1)])
                          for j in range(8)],
                     waits=[ev_v0c] + ([ev_v0tm[-1]] if ev_v0tm else []))
        e_e = p.emit(A, act(v0ctm[:, 512 * tgrp:512 * (tgrp + 1)],
                            ps_vt[:, :].rearrange("p (j c) -> p j c", j=8)[:, :, 0:64], AF.Copy),
                     waits=[e_t])
        ev_v0tm.append(e_e)
    VT.close()
    cut("slices")

    A3S = ExitStack()
    ps_a3 = A3S.enter_context(nc.psum_tensor("ps_a3", [128, T], F32))
    ev_at3 = []
    A3T = ExitStack()
    ps_3t = A3T.enter_context(nc.psum_tensor("ps_3t", [128, 1024], BF16))
    ev_3t = []
    for m2 in range(2):
        e_m = p.emit(TE, [mm(ps_a3[:, 512 * j:512 * (j + 1)], lmslice("q", 0, m2),
                             k0c[:, 512 * j:512 * (j + 1)], True, True) for j in range(4)],
                     waits=[ev_k0c, ev_qlkl, ev_v0tm[-1]] + ([ev_at3[-1]] if ev_at3 else []))
        e_nm = p.emit(V, red(scol[:, 2 + m2:3 + m2], ps_a3[:, :], ALU.max, neg=True), waits=[e_m])
        e_ex = p.emit(A, act(att3e[m2][:], ps_a3[:, :], AF.Exp, bias=scol[:, 2 + m2:3 + m2]),
                      waits=[e_nm] + ([ev_3t[-1]] if m2 == 1 else []))
        e_rs = p.emit(V, red(scol[:, 4 + m2:5 + m2], att3e[m2][:], ALU.add), waits=[e_ex])
        e_rc = p.emit(V, recip(att3rc[:, m2:m2 + 1], scol[:, 4 + m2:5 + m2]), waits=[e_rs])
        ev_at3.append(e_rc)
        for tq in range(2):
            fns = []
            for tj in range(8):
                t = 8 * tq + tj
                fns.append(trb(ps_3t[:, 128 * tj:128 * (tj + 1)],
                               att3e[m2][:, 128 * t:128 * (t + 1)]))
            e_t = p.emit(TE, fns, waits=[e_rc] + ([ev_3t[-1]] if ev_3t else []))
            e_e = p.emit(A, [act(attn3T[:, 256 * (8 * tq + tj) + 128 * m2:
                                        256 * (8 * tq + tj) + 128 * (m2 + 1)],
                                 ps_3t[:, 128 * tj:128 * (tj + 1)], AF.Copy)
                             for tj in range(8)],
                         waits=[e_t])
            ev_3t.append(e_e)
    A3T.close()
    AVS = ExitStack()
    ps_av = AVS.enter_context(nc.psum_tensor("ps_av", [128, 128], F32))
    ev_av = []
    for m2 in range(2):
        e_m = p.emit(TE, [mm(ps_av[:, 64 * m2:64 * (m2 + 1)],
                             attn3T[:, 256 * t + 128 * m2:256 * t + 128 * (m2 + 1)],
                             v0ctm[:, 64 * t:64 * (t + 1)], t == 0, t == 15) for t in range(16)],
                     waits=[ev_3t[-1], ev_v0tm[-1]] + ([ev_av[-1]] if ev_av else []))
        e_e = p.emit(A, act(avb[m2][:], ps_av[:, 64 * m2:64 * (m2 + 1)], AF.Copy,
                            scale=att3rc[:, m2:m2 + 1]), waits=[e_m, ev_at3[-1]])
        ev_av.append(e_e)
    AVS.close()
    A3S.close()
    cut("attn3")

    PV = ExitStack()
    ps_p1 = PV.enter_context(nc.psum_tensor("ps_p1", [128, 512], F32))
    ps_pt = PV.enter_context(nc.psum_tensor("ps_pt", [128, 1024], BF16))

    def tr2(dst_tiles, src_tiles, waits):
        fns = []
        for i in range(2):
            for j in range(2):
                fns.append(trb(ps_pt[:, 256 * i + 128 * j:256 * i + 128 * (j + 1)],
                               src_tiles[j][:, 128 * i:128 * (i + 1)]))
        e_t = p.emit(TE, fns, waits=waits)
        e_e = p.emit(A, [act(dst_tiles[i][:], ps_pt[:, 256 * i:256 * (i + 1)], AF.Copy)
                         for i in range(2)], waits=[e_t])
        return e_e

    ev_a2T = tr2(a2T, a20b, [ev_a20, ev_cld])
    ev_z = p.emit(V, [ts(z_t[i][:], a2T[i][:], scol[:, 11:12]) for i in range(2)],
                  waits=[ev_a2T, ev_rsc])
    ev_zT = p.emit(V, [ts(zT_t[i][:], a20b[i][:], scol[:, 11:12]) for i in range(2)],
                   waits=[ev_a20, ev_rsc])
    cut("pv_z")

    def mm256(ps, lhsT_tiles, rhs_tiles, waits):
        fns = []
        for m2 in range(2):
            for kc in range(2):
                fns.append(mm(ps[:, 256 * m2:256 * (m2 + 1)], lhsT_tiles[kc][:, 128 * m2:128 * (m2 + 1)],
                              rhs_tiles[kc][:], kc == 0, kc == 1))
        return p.emit(TE, fns, waits=waits)

    for it in range(6):
        e1 = mm256(ps_p1, a2T, z_t, [ev_z, ev_a2T])
        if it == 0:
            cut("pv_m")
        e_xz = p.emit(A, [act(xz_t[i][:], ps_p1[:, 256 * i:256 * (i + 1)], AF.Copy)
                          for i in range(2)], waits=[e1])
        if it == 0:
            cut("pv_x")
        e_t1 = p.emit(V, [stt(tm_t[i][:], ps_p1[:, 256 * i:256 * (i + 1)], 1.0 / 7.0,
                              diagI[:, 256 * i:256 * (i + 1)], ALU.mult, ALU.subtract)
                          for i in range(2)], waits=[e1, e_xz])
        if it == 0:
            cut("pv_a")
        e_xzT = tr2(xzT_t, xz_t, [e_xz])
        if it == 0:
            cut("pv_b")
        e2 = mm256(ps_p1, xzT_t, tm_t, [e_xzT, e_t1])
        e_t3 = p.emit(V, [stt(tm2_t[i][:], ps_p1[:, 256 * i:256 * (i + 1)], -7.0 / 15.0,
                              diagI[:, 256 * i:256 * (i + 1)], ALU.mult, ALU.subtract)
                          for i in range(2)], waits=[e2])
        if it == 0:
            cut("pv_c")
        e3 = mm256(ps_p1, xzT_t, tm2_t, [e_t3])
        e_t5 = p.emit(V, [stt(tm_t[i][:], ps_p1[:, 256 * i:256 * (i + 1)], -15.0 / 13.0,
                              diagI[:, 256 * i:256 * (i + 1)], ALU.mult, ALU.subtract)
                          for i in range(2)], waits=[e3])
        e4 = mm256(ps_p1, zT_t, tm_t, [e_t5, ev_zT])
        if it == 0:
            cut("pv_d")
        ev_z = p.emit(A, [act(z_t[i][:], ps_p1[:, 256 * i:256 * (i + 1)], AF.Copy, scale=cconst[:, 3:4])
                          for i in range(2)], waits=[e4])
        if it == 0:
            cut("pv_e")
        ev_zT = tr2(zT_t, z_t, [ev_z])
        if it == 0:
            cut("pv_it0")
    ev_u_mm = p.emit(TE, [mm(ps_p1[:, 64 * m2:64 * (m2 + 1)], zT_t[kc][:, 128 * m2:128 * (m2 + 1)],
                             avb[kc][:], kc == 0, kc == 1)
                          for m2 in range(2) for kc in range(2)],
                     waits=[ev_zT, ev_av[-1]])
    ev_u_e = p.emit(V, [cp(Uloc[m2][:], ps_p1[:, 64 * m2:64 * (m2 + 1)]) for m2 in range(2)],
                    waits=[ev_u_mm])
    ev_u_st = p.emit(SY, [dmaf(ag_u_in[128 * m2:128 * (m2 + 1), :], Uloc[m2][:]) for m2 in range(2)],
                     waits=[ev_u_e], dma=True)
    cut("pv_u")
    ev_ag_u = coll("AllGather", ALU.bypass, ag_u_in[:, :], ag_u_out[:, :, :], [ev_u_st])
    PV.close()
    cut("pinv")
    dump("av", avb[0][:], [ev_av[-1]])
    dump("U", Uloc[0][:], [ev_u_e])

    A1S = ExitStack()
    ps_a1 = A1S.enter_context(nc.psum_tensor("ps_a1", [128, T], F32))
    a1max = A1S.enter_context(nc.sbuf_tensor("a1max", [128, 8], F32))
    a1sum = A1S.enter_context(nc.sbuf_tensor("a1sum", [128, 8], F32))
    ev_a1 = []
    for mt in range(2):
        e_m0 = p.emit(TE, [mm(ps_a1[:, 256 * h:256 * (h + 1)],
                              q0[h // 2][0:64, 128 * mt:128 * (mt + 1)],
                              k_l0[h // 2][0:64, :],
                              True, True) for h in (0, 2, 4, 6)],
                      waits=[ev_kl0, ev_u_e] + ([ev_a1[-1]] if ev_a1 else []))
        e_m = p.emit(TE, [mm(ps_a1[:, 256 * h:256 * (h + 1)],
                             q0[h // 2][64:128, 128 * mt:128 * (mt + 1)],
                             k_l0[h // 2][64:128, :],
                             True, True) for h in (1, 3, 5, 7)],
                     waits=[e_m0])
        if mt == 0:
            cut("a1_m")
        e_nm = p.emit(V, red(a1max[:, :], ps_a1[:, :].rearrange("p (h l) -> p h l", h=8),
                             ALU.max, neg=True), waits=[e_m])
        if mt == 0:
            cut("a1_r")
        e_sub = p.emit(V, tt(sq_s[:], ps_a1[:, :],
                             a1max[:, :].unsqueeze(2).broadcast_to((128, 8, 256)).rearrange("p h l -> p (h l)")
                             if False else a1max[:, :], ALU.add), waits=[e_nm]) if False else None
        e_sub = p.emit(V, tt(sq_s[:].rearrange("p (h l) -> p h l", h=8),
                             ps_a1[:, :].rearrange("p (h l) -> p h l", h=8),
                             a1max[:].unsqueeze(2).broadcast_to([128, 8, 256]),
                             ALU.add), waits=[e_nm])
        if mt == 0:
            cut("a1_s")
        e_ex = p.emit(A, act(attn1[mt][:], sq_s[:], AF.Exp),
                      waits=[e_sub] + ([ev_1t[-1]] if mt == 1 else []))
        if mt == 0:
            cut("a1_e")
        e_rs = p.emit(V, red(a1sum[:, :], attn1[mt][:].rearrange("p (h l) -> p h l", h=8),
                             ALU.add), waits=[e_ex])
        e_rc = p.emit(V, recip(a1sum[:, :], a1sum[:, :]), waits=[e_rs])
        e_n = p.emit(V, tt(attn1[mt][:].rearrange("p (h l) -> p h l", h=8),
                           attn1[mt][:].rearrange("p (h l) -> p h l", h=8),
                           a1sum[:].unsqueeze(2).broadcast_to([128, 8, 256]), ALU.mult),
                     waits=[e_rc])
        if mt == 0:
            cut("a1_n")
        ev_a1.append(e_n)
        if mt == 0:
            A1T = ExitStack()
            ps_1t = A1T.enter_context(nc.psum_tensor("ps_1t", [128, 1024], BF16))
            ev_1t = []
        for grp in range(2):
            fns = []
            hs = range(grp * 4, grp * 4 + 4)
            for hi, h in enumerate(hs):
                for j in range(2):
                    fns.append(trb(ps_1t[:, 256 * hi + 128 * j:256 * hi + 128 * (j + 1)],
                                   attn1[mt][:, 256 * h + 128 * j:256 * h + 128 * (j + 1)]))
            e_t = p.emit(TE, fns, waits=[e_n] + ([ev_1t[-1]] if ev_1t else []))
            e_e = p.emit(A, [act(attn1T[j][:, 256 * h + 128 * mt:256 * h + 128 * (mt + 1)],
                                 ps_1t[:, 256 * hi + 128 * j:256 * hi + 128 * (j + 1)],
                                 AF.Copy)
                             for hi, h in enumerate(hs) for j in range(2)],
                         waits=[e_t])
            ev_1t.append(e_e)
    A1T.close()
    A1S.close()
    cut("attn1")

    UL = ExitStack()
    ev_ul = p.emit(SY, [dmaf(xring[0][:, 0:512].rearrange("p (h c) -> p h c", h=8),
                             ag_u_out[:, 0:128, :].transpose([1, 0, 2])),
                        dmaf(xring[0][:, 512:1024].rearrange("p (h c) -> p h c", h=8),
                             ag_u_out[:, 128:256, :].transpose([1, 0, 2]))],
                   waits=[ev_ag_u], dma=True)
    ev_ub = p.emit(V, [cp(U_bf[0][:], xring[0][:, 0:512]), cp(U_bf[1][:], xring[0][:, 512:1024])],
                   waits=[ev_ul])
    UL.close()

    CH = ExitStack()
    ps_ch = CH.enter_context(nc.psum_tensor("ps_ch", [128, 1024], F32))
    fns = []
    for h in range(8):
        for kc in range(2):
            fns.append(mm(ps_ch[64 * (h % 2):64 * (h % 2) + 64, 256 * (h // 2):256 * (h // 2 + 1)],
                          U_bf[kc][:, 64 * h:64 * (h + 1)],
                          attn1T[kc][:, 256 * h:256 * (h + 1)],
                          (h in (0, 1) and kc == 0) if False else (h % 2 == 0 and kc == 0) if False else (h < 2 and kc == 0),
                          h >= 6 and kc == 1))
    fns = []
    for pair in range(4):
        for h in (2 * pair, 2 * pair + 1):
            for kc in range(2):
                first = (h == 2 * pair) and kc == 0
                fns.append(mm(ps_ch[64 * (h % 2):64 * (h % 2) + 64, 256 * (h // 2):256 * (h // 2 + 1)],
                              U_bf[kc][:, 64 * h:64 * (h + 1)],
                              attn1T[kc][:, 256 * h:256 * (h + 1)],
                              first and (pair % 2 == 0 or True) and (h % 2 == 0),
                              (h % 2 == 1) and kc == 1))
    def ch_fns(hs):
        return [mm(ps_ch[64 * (h % 2):64 * (h % 2) + 64,
                         256 * (h // 2):256 * (h // 2 + 1)],
                   U_bf[kc][:, 64 * h:64 * (h + 1)],
                   attn1T[kc][:, 256 * h:256 * (h + 1)], kc == 0, kc == 1)
                for h in hs for kc in range(2)]
    ev_ch0 = p.emit(TE, ch_fns((0, 2, 4, 6)), waits=[ev_ub, ev_1t[-1]])
    ev_ch = p.emit(TE, ch_fns((1, 3, 5, 7)), waits=[ev_ch0])
    ev_xgp = p.emit(V, [tt(xgp[j][:], ps_ch[:, 256 * j:256 * (j + 1)], res_fm[j][:], ALU.add)
                        for j in range(4)], waits=[ev_ch, ev_conv[-1]])
    CH.close()
    cut("chain")
    dump("xgp", xgp[0][:], [ev_xgp])

    OP = ExitStack()
    ps_op = OP.enter_context(nc.psum_tensor("ps_op", [128, 512], F32))
    ev_xg0 = []
    ev_oww = [ev_outw, ev_outw]
    ev_stg = None
    for m in range(DC):
        r = m % 2
        evl = p.emit(SY, dmaf(stage[:, 0:512].rearrange("p (k c) -> p k c", k=4),
                              out_w[:, 128 * m:128 * (m + 1)]
                              .rearrange("(k p) c -> k p c", p=128).transpose([1, 0, 2])),
                     waits=[ev_oww[r], ev_stg], dma=True)
        evc = p.emit(V, cp(oww[r][:], stage[:, 0:512]), waits=[evl])
        ev_stg = evc
        e_m = p.emit(TE, [mm(ps_op[:, 0:S], oww[r][:, 128 * kc:128 * (kc + 1)], xgp[kc][:],
                             kc == 0, kc == 3) for kc in range(4)],
                     waits=[evc, ev_xgp] + ([ev_xg0[-1]] if ev_xg0 else []))
        e_e = p.emit(V, ts(xg0[m][:], ps_op[:, 0:S], biasc[:, 8 + m:9 + m], None, ALU.add),
                     waits=[e_m, ev_bias])
        ev_xg0.append(e_e)
        ev_oww[r] = e_m
    OP.close()
    cut("outproj")
    dump("xg0", xg0[0][:], [ev_xg0[0]])

    ev_enc = []
    ev_xr = [ev_x2_ok[0], ev_x2_ok[1]]
    for k in range(DC):
        r = k % 2
        evl = p.emit(SY, dmaf(xring[r][:], x_fm[128 * k:128 * (k + 1), :]),
                     waits=[ev_xr[r]] if not ev_enc else [ev_xr[r]], dma=True)
        e1 = p.emit(V, tt(misc[:, 0:S], b0ap(xring[r]), b0ap(m_b), ALU.subtract),
                    waits=[evl, ev_mb, ev_hxn[-1], c_vh] + ([ev_enc[-1]] if ev_enc else []))
        e2 = p.emit(V, tt(misc[:, 0:S], misc[:, 0:S], b0ap(istd_b), ALU.mult), waits=[e1])
        e3 = p.emit(V, tt(misc[:, 256:256 + S], misc[:, 0:S], xg0[k][:], ALU.add),
                    waits=[e2, ev_xg0[k]])
        e4 = p.emit(V, cp(enc0b[k][:], misc[:, 256:256 + S]), waits=[e3])
        ev_enc.append(e4)
        ev_xr[r] = e1
    dump("enc0", enc0b[0][:], [ev_enc[0]])

    GK = ExitStack()
    ps_g1 = GK.enter_context(nc.psum_tensor("ps_g1", [128, 512], F32))
    ev_qg = []
    for m2 in range(2):
        e_m = p.emit(TE, [mm(ps_g1[:, 0:S], wqg_b[k][:, 128 * m2:128 * (m2 + 1)], enc0b[k][:],
                             k == 0, k == DC - 1) for k in range(DC)],
                     waits=[ev_enc[-1], ev_wg] + ([ev_qg[-1]] if ev_qg else []))
        e_e = p.emit(V, cp(q_g[m2][:], ps_g1[:, 0:S]), waits=[e_m])
        ev_qg.append(e_e)
    ev_kg_st = []
    for mt in range(2):
        e_m = p.emit(TE, [mm(ps_g1[:, 0:S], enc0b[k][:, 128 * mt:128 * (mt + 1)], wkg_b[k][:],
                             k == 0, k == DC - 1) for k in range(DC)],
                     waits=[ev_qg[-1]] + ([ev_kg_st[-1]] if ev_kg_st else []))
        e_e = p.emit(V, cp(convtmp[:, 0:S], ps_g1[:, 0:S]), waits=[e_m])
        e_s = p.emit(SY, dmaf(ag_kg_in[128 * mt:128 * (mt + 1), :], convtmp[:, 0:S]),
                     waits=[e_e], dma=True)
        ev_kg_st.append(e_s)
    ev_ag_kg = coll("AllGather", ALU.bypass, ag_kg_in[:, :], ag_kg_out[:, :, :], list(ev_kg_st))
    GK.close()

    GM = ExitStack()
    ps_M = GM.enter_context(nc.psum_tensor("ps_M", [128, 512], F32))
    ps_A = GM.enter_context(nc.psum_tensor("ps_A", [1, 256], F32))
    ev_gmm = []
    ci = 0
    for jc in range(16):
        r = jc % 2
        evk = p.emit(SY, dmaf(kgring[r][:], ag_kg_out[jc // 2, 128 * (jc % 2):128 * (jc % 2) + 128, :]),
                     waits=[ev_ag_kg] + ([ev_gmm[-1]] if ev_gmm else []), dma=True)
        for d in range(Dr):
            rc2 = ci % 2
            ci += 1
            evc = p.emit(SY, dmaf(canring[rc2][:], wt_can[d][128 * jc:128 * (jc + 1), :]),
                         waits=([ev_gmm[-1]] if ev_gmm else []), dma=True)
            e_m = p.emit(TE, [mm(ps_M[:, 256 * m2:256 * (m2 + 1)], kgring[r][:, 128 * m2:128 * (m2 + 1)],
                                 canring[rc2][:],
                                 jc == 0 and d == 0 and m2 == 0,
                                 jc == 15 and d == Dr - 1 and m2 == 1)
                              for m2 in range(2)],
                         waits=[evk, evc])
            ev_gmm.append(e_m)
    e_pr = p.emit(V, [tt(misc[:, 256 * m2:256 * (m2 + 1)], ps_M[:, 256 * m2:256 * (m2 + 1)], q_g[m2][:], ALU.mult)
                      for m2 in range(2)], waits=[ev_gmm[-1], ev_qg[-1], ev_enc[-1]])
    e_am = p.emit(TE, [mm(ps_A[:, :], ones128[:, 0:1], misc[:, 256 * m2:256 * (m2 + 1)], m2 == 0, m2 == 1)
                       for m2 in range(2)], waits=[e_pr])
    e_ar = p.emit(V, ts(arow[0:1, 256:512], ps_A[:, :], 1.0 / 16.0), waits=[e_am])
    GM.close()
    cut("graph")
    e_amx = p.emit(V, red(srow[0:1, 104:105], arow[0:1, 256:512], ALU.max), waits=[e_ar])
    e_sta = p.emit(G, dmaf(ar_a_in[:, :], srow[0:1, 104:105]), waits=[e_amx], dma=True)
    ev_ar_a = coll("AllReduce", ALU.max, ar_a_in[:, :], ar_a_out[:, :], [e_sta])
    e_lda = p.emit(G, dmaf(srow[0:1, 105:106], ar_a_out[:, :]), waits=[ev_ar_a], dma=True)
    e_ng = p.emit(V, ts(srow[0:1, 106:107], srow[0:1, 105:106], -1.0), waits=[e_lda])
    e_ea = p.emit(A, act(arow[0:1, 256:512], arow[0:1, 256:512], AF.Exp, bias=srow[0:1, 106:107]),
                  waits=[e_ng, e_amx])
    e_asm = p.emit(V, red(srow[0:1, 107:108], arow[0:1, 256:512], ALU.add), waits=[e_ea])
    e_sts = p.emit(G, dmaf(ar_s_in[:, :], srow[0:1, 107:108]), waits=[e_asm], dma=True)
    ev_ar_s = coll("AllReduce", ALU.add, ar_s_in[:, :], ar_s_out[:, :], [e_sts])
    e_lds = p.emit(G, dmaf(srow[0:1, 108:109], ar_s_out[:, :]), waits=[ev_ar_s], dma=True)
    e_rss = p.emit(V, recip(srow[0:1, 109:110], srow[0:1, 108:109]), waits=[e_lds])
    e_alr = p.emit(V, ts(arow[0:1, 256:512], arow[0:1, 256:512], srow[0:1, 109:110]), waits=[e_rss])
    ev_alpha = bcast_row(alpha_b, arow[0:1, 256:512], S, [e_alr])
    dump("alpha", alpha_b[:], [ev_alpha])
    NY.close()
    cut("ny")

    RT = ExitStack()
    cnT = [RT.enter_context(nc.sbuf_tensor(f"cnT{k}", [128, 8], BF16)) for k in range(DC)]
    rtT = [RT.enter_context(nc.sbuf_tensor(f"rtT{k}", [128, 8], BF16)) for k in range(DC)]
    logit = RT.enter_context(nc.sbuf_tensor("logit", [8, T], F32))
    gat_tm = RT.enter_context(nc.sbuf_tensor("gat_tm", [128, 128], F32))
    gat_sc = RT.enter_context(nc.sbuf_tensor("gat_sc", [128, 128], F32))
    g16 = RT.enter_context(nc.sbuf_tensor("g16", [128, 16], F32))
    ninv = RT.enter_context(nc.sbuf_tensor("ninv", [8, T], F32))
    gates_fm = RT.enter_context(nc.sbuf_tensor("gates_fm", [8, T], BF16))
    RTP = ExitStack()
    ps_r = RTP.enter_context(nc.psum_tensor("ps_r", [128, 512], F32))
    ps_r2 = RTP.enter_context(nc.psum_tensor("ps_r2", [8, T], F32))

    e_cl = p.emit(SY, dmaf(sq_s[0:8, 1024:2048], cent[:, :]), waits=[ev_alpha], dma=True)
    e_cs = p.emit(A, act(sq_s[0:8, 0:1024], sq_s[0:8, 1024:2048],
                         AF.Square), waits=[e_cl])
    e_cr = p.emit(V, red(scol[0:8, 13:14], sq_s[0:8, 0:1024], ALU.add), waits=[e_cs])
    e_cq = p.emit(A, act(scol[0:8, 14:15], scol[0:8, 13:14], AF.Sqrt), waits=[e_cr])
    e_ci = p.emit(V, recip(scol[0:8, 14:15], scol[0:8, 14:15]), waits=[e_cq])
    e_ct = p.emit(TE, trf(ps_r[0:8, 0:8] if False else ps_r[0:128, 0:8],
                          scol[:, 14:15].broadcast_to([128, 8]) if False else scol[:, 14:15]),
                  waits=[e_ci]) if False else None
    e_ct = p.emit(TE, trf(ps_r[0:1, 0:128], scol[:, 14:15]), waits=[e_ci, ev_cld])
    e_cre = p.emit(V, cp(srow[0:1, 110:118], ps_r[0:1, 0:8]), waits=[e_ct])
    e_crb = p.emit(TE, mm(ps_r[:, 0:8], ones1[:, 0:128], srow[0:1, 110:118], True, True),
                   waits=[e_cre])
    e_crc = p.emit(V, cp(misc[:, 256:264], ps_r[:, 0:8]), waits=[e_crb])
    ev_ctl = []
    for k in range(DC):
        evl = p.emit(SY, [dmaf(misc[:, 264:272], cent_T[128 * k:128 * (k + 1), :]),
                          dmaf(misc[:, 272:280], route_T[128 * k:128 * (k + 1), :])],
                     waits=[ev_ctl[-1]] if ev_ctl else [e_crc], dma=True)
        e_c1 = p.emit(V, tt(cnT[k][:], misc[:, 264:272], misc[:, 256:264], ALU.mult), waits=[evl])
        e_c2 = p.emit(V, cp(rtT[k][:], misc[:, 272:280]), waits=[evl, e_c1])
        ev_ctl.append(e_c2)
    ev_ct_ok = ev_ctl[-1]
    e_n1 = p.emit(V, ts(sq_s[0:8, 0:T], var_b[0:8, :], float(D)), waits=[ev_var])
    e_n2 = p.emit(V, recip(sq_s[0:8, 0:T], sq_s[0:8, 0:T]), waits=[e_n1])
    e_n3 = p.emit(V, ts(ninv[:, :], var_b[0:8, :], 1.0, EPS_LN, ALU.mult, ALU.add), waits=[e_n1])
    e_n4 = p.emit(V, tt(ninv[:, :], ninv[:, :], sq_s[0:8, 0:T], ALU.mult), waits=[e_n3, e_n2])
    e_n5 = p.emit(A, act(ninv[:, :], ninv[:, :], AF.Sqrt), waits=[e_n4])
    ev_lg = []
    for nt in range(4):
        e_m1 = p.emit(TE, [mm(ps_r2[:, 512 * nt:512 * (nt + 1)], cnT[k][:, :],
                              xn_bf[k][:, 512 * nt:512 * (nt + 1)], k == 0, k == DC - 1)
                           for k in range(DC)],
                      waits=[ev_ct_ok, ev_xn[-1]] + ([ev_lg[-1]] if ev_lg else []))
        e_l1 = p.emit(V, tt(logit[:, 512 * nt:512 * (nt + 1)], ps_r2[:, 512 * nt:512 * (nt + 1)],
                            ninv[:, 512 * nt:512 * (nt + 1)], ALU.mult), waits=[e_m1, e_n5])
        e_m2 = p.emit(TE, [mm(ps_r2[:, 512 * nt:512 * (nt + 1)], rtT[k][:, :],
                              xn_bf[k][:, 512 * nt:512 * (nt + 1)], k == 0, k == DC - 1)
                           for k in range(DC)],
                      waits=[e_l1])
        e_l2 = p.emit(V, tt(logit[:, 512 * nt:512 * (nt + 1)], logit[:, 512 * nt:512 * (nt + 1)],
                            ps_r2[:, 512 * nt:512 * (nt + 1)], ALU.add), waits=[e_m2])
        ev_lg.append(e_l2)
    dump("logits", logit[:, :], [ev_lg[-1]])
    ev_gf = []
    for blk in range(16):
        e_t = p.emit(TE, trf(ps_r[:, 0:8] if blk == 0 else ps_r[:, 0:8],
                             logit[0:8, 128 * blk:128 * (blk + 1)]),
                     waits=[ev_lg[-1]] + ([ev_gf[-1]] if ev_gf else []))
        e_e = p.emit(V, cp(gat_tm[:, 8 * (blk % 16):8 * (blk % 16) + 8], ps_r[0:128, 0:8]),
                     waits=[e_t])
        ev_gf.append(e_e)
    g3 = gat_tm[:].rearrange("p (g e) -> p g e", e=8)
    e_m1g = p.emit(V, red(g16[:, :], g3, ALU.max, neg=True), waits=[ev_gf[-1]])
    gat3 = gat_tm[:].rearrange("p (g e) -> p g e", e=8)
    gsc3 = gat_sc[:].rearrange("p (g e) -> p g e", e=8)
    e_z = p.emit(V, tt(gat3, gat3,
                       g16[:].unsqueeze(2).broadcast_to([128, 16, 8]), ALU.add), waits=[e_m1g])
    e_msk = p.emit(V, ts(gat_sc[:], gat_tm[:], 0.0, None, ALU.is_ge), waits=[e_z])
    e_zt = p.emit(V, stt(gat_sc[:], gat_sc[:], NEG / 2, gat_tm[:], ALU.mult, ALU.add), waits=[e_msk])
    e_m2g = p.emit(V, red(g16[:, :], gat_sc[:].rearrange("p (g e) -> p g e", e=8), ALU.max),
                   waits=[e_zt])
    e_kp = p.emit(V, tt(gsc3, gat3,
                        g16[:].unsqueeze(2).broadcast_to([128, 16, 8]), ALU.is_ge), waits=[e_m2g])
    e_eg = p.emit(A, act(gat_tm[:], gat_tm[:], AF.Exp), waits=[e_kp])
    e_egm = p.emit(V, tt(gat_tm[:], gat_tm[:], gat_sc[:], ALU.mult), waits=[e_eg])
    e_den = p.emit(V, red(g16[:, :], gat_tm[:].rearrange("p (g e) -> p g e", e=8), ALU.add),
                   waits=[e_egm])
    e_rcg = p.emit(V, recip(g16[:, :], g16[:, :]), waits=[e_den])
    e_gn = p.emit(V, tt(gat3, gat3,
                        g16[:].unsqueeze(2).broadcast_to([128, 16, 8]), ALU.mult), waits=[e_rcg])
    ev_gfm = []
    for blk in range(16):
        e_t = p.emit(TE, trf(ps_r2[0:8, 128 * blk:128 * (blk + 1)],
                             gat_tm[:, 8 * blk:8 * blk + 8]),
                     waits=[e_gn] + ([ev_gfm[-1]] if ev_gfm else []))
        ev_gfm.append(e_t)
    e_gfe = p.emit(V, cp(gates_fm[:, :], ps_r2[0:8, :]), waits=[ev_gfm[-1]])
    e_grw = p.emit(SY, dmaf(grow_dram[0:1, :], gates_fm[:, :]), waits=[e_gfe], dma=True)
    dump("gates", gates_fm[:, :], [e_gfe])
    RTP.close()
    cut("route")

    MO = ExitStack()
    w1r = [MO.enter_context(nc.sbuf_tensor(f"w1r{j}", [128, 8 * 512], BF16)) for j in range(2)]
    hp = [MO.enter_context(nc.sbuf_tensor(f"hp{e}", [128, 4 * 512], BF16)) for e in range(8)]
    w2r = [MO.enter_context(nc.sbuf_tensor(f"w2r{j}", [128, 4096], BF16)) for j in range(2)]
    gtile = MO.enter_context(nc.sbuf_tensor("gtile", [128, 512], BF16))
    grow_sb = MO.enter_context(nc.sbuf_tensor("grow_sb", [1, 512], BF16))
    ev_gb_prev = None
    hstg = MO.enter_context(nc.sbuf_tensor("hstg", [128, 512], BF16))
    gstg = MO.enter_context(nc.sbuf_tensor("gstg", [128, 512], F32))
    b1c = MO.enter_context(nc.sbuf_tensor("b1c", [128, 32], F32))
    b2b = MO.enter_context(nc.sbuf_tensor("b2b", [8, D], BF16))
    ps_mh = MO.enter_context(nc.psum_tensor("ps_mh", [128, 512], F32))
    ps_mo = MO.enter_context(nc.psum_tensor("ps_mo", [128, 512], F32))
    ps_gb = MO.enter_context(nc.psum_tensor("ps_gb", [128, 512], F32))

    e_b1 = p.emit(SY, dmaf(b1c[:, :], moe_b1[:, :]), dma=True)
    e_b2l = p.emit(SY, dmaf(misc[0:8, 0:128], moe_b2[:, 0:128]), waits=[e_am], dma=True)
    ev_b2c = []
    for kk in range(8):
        evl = p.emit(SY, dmaf(misc[0:8, 128:256], moe_b2[:, 128 * kk:128 * (kk + 1)]),
                     waits=[ev_b2c[-1]] if ev_b2c else [], dma=True)
        ev_b2c.append(p.emit(V, cp(b2b[:, 128 * kk:128 * (kk + 1)], misc[0:8, 128:256]),
                             waits=[evl]))
    ev_b2 = ev_b2c[-1]

    w1view = [w1r[j][:].rearrange("p (k c) -> p k c", k=8) for j in range(2)]
    w2view = [w2r[j][:].rearrange("p (e k c) -> p e k c", e=8, k=4) for j in range(2)]
    ev_w1use = [None, None]
    ev_w2use = [None, None]
    ev_ge_prev = [None]
    ev_gm_prev = [None]
    ev_sil_prev = [None]
    ev_hp_use = [[None] * 8, [None] * 8]
    hp_ready = [None] * 8
    ev_g_st = []
    ev_gstg_ok = None
    for tti in range(4):
        tt0 = 512 * tti
        for e_i in range(8):
            r = e_i % 2
            evw = p.emit(SY, dmaf(w1r[r][:].rearrange("p (k c) -> p k c", k=8),
                                  w1bf_dram[e_i, :, :, :].transpose([1, 0, 2])),
                         waits=[ev_w1use[r], ev_moe_w[1]], dma=True)
            e_gl = p.emit(G, dmaf(grow_sb[0:1, :], grow_dram[0:1, T * e_i + tt0:T * e_i + tt0 + 512]),
                          waits=[e_grw, ev_gb_prev], dma=True)
            e_gb = p.emit(TE, mm(ps_gb[:, :], ones1b[:, 0:128], grow_sb[0:1, :], True, True),
                          waits=[e_gl])
            e_gt = p.emit(V, cp(gtile[:], ps_gb[:, :]), waits=[e_gb, ev_gm_prev[0]])
            ev_gb_prev = e_gt
            last_gm = None
            for ci in range(4):
                e_h = p.emit(TE, [mm(ps_mh[:, :], w1view[r][:, k, 128 * ci:128 * (ci + 1)],
                                     xn_bf[k][:, tt0:tt0 + 512], k == 0, k == DC - 1)
                                  for k in range(DC)],
                             waits=[evw, ev_xn[-1], hp_ready[e_i], ev_ge_prev[0]])
                e_ge = p.emit(A, act(hstg[:], ps_mh[:, :], AF.Gelu, bias=b1c[:, 4 * e_i + ci:4 * e_i + ci + 1]),
                              waits=[e_h, e_b1, ev_gm_prev[0]])
                ev_ge_prev[0] = e_ge
                e_gm = p.emit(V, tt(hp[e_i][:, 512 * ci:512 * (ci + 1)], hstg[:], gtile[:], ALU.mult),
                              waits=[e_ge, e_gt])
                ev_gm_prev[0] = e_gm
                last_gm = e_gm
            ev_w1use[r] = last_gm
            hp_ready[e_i] = last_gm
        for m in range(8):
            r2 = m % 2
            evw2 = p.emit(SY, dmaf(w2r[r2][:].rearrange("p (e k c) -> p e k c", e=8, k=4),
                                   w2bf_dram[m, :, :, :, :].transpose([2, 0, 1, 3])),
                          waits=[ev_w2use[r2], ev_moe_w[1]], dma=True)
            fns = []
            for e_i in range(8):
                for kc in range(4):
                    fns.append(mm(ps_mo[:, :], w2view[r2][:, e_i, kc, :],
                                  hp[e_i][:, 512 * kc:512 * kc + 512],
                                  e_i == 0 and kc == 0, False))
            fns.append(mm(ps_mo[:, :], b2b[:, 128 * m:128 * (m + 1)],
                          gates_fm[:, tt0:tt0 + 512], False, True))
            e_o = p.emit(TE, fns, waits=[evw2, ev_b2, e_gfe, ev_sil_prev[0]] +
                         [hp_ready[e_i] for e_i in range(8)])
            e_sil = p.emit(A, act(gstg[:], ps_mo[:, :], AF.Silu), waits=[e_o, ev_gstg_ok])
            ev_sil_prev[0] = e_sil
            e_gs = p.emit(SY, dmaf(g_dram[128 * m:128 * (m + 1), tt0:tt0 + 512], gstg[:]),
                          waits=[e_sil], dma=True)
            ev_gstg_ok = e_gs
            ev_w2use[r2] = e_o
            ev_g_st.append(e_gs)
        if tti == 3:
            break
        for e_i in range(8):
            hp_ready[e_i] = ev_g_st[-1]
    ev_moe_done = ev_g_st[-1]
    MO.close()
    RT.close()
    cut("moe")
    dump("g0", gstg[:], [ev_moe_done])

    VAL = ExitStack()
    val_bf = [VAL.enter_context(nc.sbuf_tensor(f"val_bf{k}", [128, T], BF16)) for k in range(DC)]
    VL = ExitStack()
    wvb2 = [VL.enter_context(nc.sbuf_tensor(f"wvb2_{k}", [128, D], BF16)) for k in range(DC)]
    ps_vl = VL.enter_context(nc.psum_tensor("ps_vl", [128, 512], F32))
    ev_wv = None
    for k in range(DC):
        evl = p.emit(SY, dmaf(xring[0][:, 0:D], wv_w[128 * k:128 * (k + 1), :]),
                     waits=[ev_wv] if ev_wv else [ev_xr[0]], dma=True)
        ev_wv = p.emit(V, cp(wvb2[k][:], xring[0][:, 0:D]), waits=[evl])
    ev_val = []
    for m in range(DC):
        for nt in range(4):
            e_m = p.emit(TE, [mm(ps_vl[:, :], wvb2[k][:, 128 * m:128 * (m + 1)],
                                 xn_bf[k][:, 512 * nt:512 * (nt + 1)], k == 0, k == DC - 1)
                              for k in range(DC)],
                         waits=[ev_wv, ev_xn[-1]] + ([ev_val[-1]] if ev_val else []))
            e_e = p.emit(V, ts(val_bf[m][:, 512 * nt:512 * (nt + 1)], ps_vl[:, :],
                               biasc[:, m:m + 1], None, ALU.add), waits=[e_m, ev_bias])
            ev_val.append(e_e)
    VL.close()
    cut("val")
    dump("value", val_bf[0][:], [ev_val[3]])

    CB = ExitStack()
    t_xl = CB.enter_context(nc.sbuf_tensor("t_xl", [128, T], F32))
    t_w = CB.enter_context(nc.sbuf_tensor("t_w", [128, T], F32))
    t_sw = CB.enter_context(nc.sbuf_tensor("t_sw", [128, T], F32))
    t_enc = CB.enter_context(nc.sbuf_tensor("t_enc", [128, T], F32))
    gring = [CB.enter_context(nc.sbuf_tensor(f"gring{j}", [128, T], F32)) for j in range(2)]
    alpha3 = alpha_b[:].unsqueeze(2).broadcast_to([128, S, 8])
    ev_xnew_st = []
    ev_xr2 = [ev_wv, ev_wv]
    ev_gr = [ev_moe_done, ev_moe_done]
    ev_cb_prev = None
    for k in range(DC):
        r = k % 2
        evl = p.emit(SY, dmaf(xring[r][:], x_fm[128 * k:128 * (k + 1), :]),
                     waits=[ev_xr2[r]], dma=True)
        evg = p.emit(SY, dmaf(gring[r][:], g_dram[128 * k:128 * (k + 1), :]),
                     waits=[ev_gr[r]], dma=True)
        w0 = [ev_alpha, ev_val[-1], ev_cb_prev]
        e1 = p.emit(V, tt(t_xl[:].rearrange("p (s b) -> p s b", b=8),
                          val_bf[k][:].rearrange("p (s b) -> p s b", b=8), alpha3, ALU.mult),
                    waits=w0)
        e2 = p.emit(A, act(t_w[:], t_xl[:], AF.Sigmoid, scale=cconst[:, 4:5]), waits=[e1])
        e3 = p.emit(A, act(t_sw[:], t_w[:], AF.Square), waits=[e2])
        e4 = p.emit(V, tt(t_w[:], t_xl[:], t_sw[:], ALU.mult), waits=[e3])
        e5 = p.emit(V, ts(t_xl[:], t_sw[:], -1.0, 1.0, ALU.mult, ALU.add), waits=[e4])
        e6 = p.emit(V, tt(t_enc[:], xring[r][:], m_b[:], ALU.subtract), waits=[evl, e5])
        e7 = p.emit(V, tt(t_enc[:], t_enc[:], istd_b[:], ALU.mult), waits=[e6])
        e8 = p.emit(V, tt(t_enc[:].rearrange("p (s b) -> p s b", b=8),
                          t_enc[:].rearrange("p (s b) -> p s b", b=8),
                          xg0[k][:].unsqueeze(2).broadcast_to([128, S, 8]), ALU.add),
                    waits=[e7, ev_xg0[k]])
        e9 = p.emit(V, tt(t_xl[:], t_xl[:], t_enc[:], ALU.mult), waits=[e8])
        e10 = p.emit(V, tt(t_w[:], t_w[:], t_xl[:], ALU.add), waits=[e9])
        e11 = p.emit(V, tt(t_w[:], t_w[:], gring[r][:], ALU.mult), waits=[e10, evg])
        e12 = p.emit(V, stt(xring[r][:], t_w[:], 2.0, xring[r][:], ALU.mult, ALU.add),
                     waits=[e11])
        evst = p.emit(SY, dmaf(xnew_dram[128 * k:128 * (k + 1), :], xring[r][:]),
                      waits=[e12], dma=True)
        ev_xnew_st.append(evst)
        ev_xr2[r] = evst
        ev_gr[r] = e11
        ev_cb_prev = e12
    dump("xnew", xring[1][:], [ev_xnew_st[-1]])
    CB.close()
    VAL.close()
    L0.close()
    cut("combine")

    L1 = ExitStack()

    def sb1(name, shape, dt=F32):
        return L1.enter_context(nc.sbuf_tensor(name, list(shape), dt))

    xq_bf = [sb1(f"xq{k}", [128, T], BF16) for k in range(DC)]
    clip2 = sb1("clip2", [1, 2])
    l1r2 = sb1("l1r2", [1, I_FF])
    bc1_ps = None

    bc1_last = [None]

    def bcast1(dst_tile, row_ap, n, waits, scale=None):
        evs = []
        nt = (n + 511) // 512
        for j in range(nt):
            w = min(512, n - 512 * j)
            ev_m = p.emit(TE, mm(bc1_ps[:, 0:w], ones1[:, 0:128], row_ap[0:1, 512 * j:512 * j + w],
                                 True, True),
                          waits=list(waits) + [evs[-1] if evs else bc1_last[0]])
            if scale is None:
                ev_c = p.emit(V, cp(dst_tile[:, 512 * j:512 * j + w], bc1_ps[:, 0:w]), waits=[ev_m])
            else:
                ev_c = p.emit(V, ts(dst_tile[:, 512 * j:512 * j + w], bc1_ps[:, 0:w], scale), waits=[ev_m])
            evs.append(ev_c)
        bc1_last[0] = evs[-1]
        return evs[-1]

    LA = ExitStack()

    def sba(name, shape, dt=F32):
        return LA.enter_context(nc.sbuf_tensor(name, list(shape), dt))

    xn1_bf = [sba(f"xn1_{k}", [128, T], BF16) for k in range(DC)]
    absa = sba("absa", [128, 2])
    LSB = ExitStack()
    m1_b = LSB.enter_context(nc.sbuf_tensor("m1_b", [128, T], F32))
    i1_b = LSB.enter_context(nc.sbuf_tensor("i1_b", [128, T], F32))

    RW1 = ExitStack()
    l1row = RW1.enter_context(nc.sbuf_tensor("l1row", [1, 2 * T], F32))
    LN1 = ExitStack()
    ps_s1 = LN1.enter_context(nc.psum_tensor("ps_s1", [1, T], F32))
    ps_s2 = LN1.enter_context(nc.psum_tensor("ps_s2", [1, T], F32))
    mm_s1, mm_s2 = [], []
    ev_xn_ok = [[ev_xnew_st[-2]], [ev_xnew_st[-1]]]
    for k in range(DC):
        r = k % 2
        evl = p.emit(SY, dmaf(xring[r][:], xnew_dram[128 * k:128 * (k + 1), :]),
                     waits=ev_xn_ok[r] + [ev_xnew_st[k]], dma=True)
        evsq = p.emit(A, act(sq_s[:], xring[r][:], AF.Square),
                      waits=[evl] + ([mm_s2[-1]] if mm_s2 else []))

        def mk1(dst, src, k=k):
            return [mm(dst[:, 512 * j:512 * (j + 1)], ones128[:, 0:1],
                       src[:, 512 * j:512 * (j + 1)], k == 0, k == DC - 1) for j in range(4)]

        m1_ = p.emit(TE, mk1(ps_s1, xring[r]), waits=[evl])
        mm_s1.append(m1_)
        m2_ = p.emit(TE, mk1(ps_s2, sq_s), waits=[evsq])
        mm_s2.append(m2_)
        ev_xn_ok[r] = [m1_, evsq]
    ev_m1r = p.emit(V, ts(l1row[0:1, 0:T], ps_s1[:, :], 1.0 / D), waits=[mm_s1[-1]])
    ev_q1r = p.emit(V, ts(l1row[0:1, T:2 * T], ps_s2[:, :], 1.0 / D), waits=[mm_s2[-1]])
    LN1.close()
    bc1_ps = L1.enter_context(nc.psum_tensor("bc1_ps", [128, 512], F32))
    ev_m1b = bcast1(m1_b, l1row[0:1, 0:T], T, [ev_m1r])
    ev_v1b = bcast1(i1_b, l1row[0:1, T:2 * T], T, [ev_q1r])
    RW1.close()
    ev_m12 = p.emit(V, tt(sq_s[:], m1_b[:], m1_b[:], ALU.mult), waits=[ev_m1b, ev_q1r])
    ev_var1 = p.emit(V, tt(i1_b[:], i1_b[:], sq_s[:], ALU.subtract), waits=[ev_m12, ev_v1b])
    ev_sd1 = p.emit(A, act(sq_s[:], i1_b[:], AF.Sqrt, bias=cconst[:, 0:1]), waits=[ev_var1])
    ev_i1 = p.emit(V, recip(i1_b[:], sq_s[:]), waits=[ev_sd1])

    ev_xn1 = []
    ev_ab = None
    ev_sq_last = None
    ev_xr3 = [ev_i1, ev_i1]
    for k in range(DC):
        r = k % 2
        evl = p.emit(SY, dmaf(xring[r][:], xnew_dram[128 * k:128 * (k + 1), :]),
                     waits=[ev_xr3[r]], dma=True)
        e1 = p.emit(V, tt(sq_s[:], xring[r][:], m1_b[:], ALU.subtract),
                    waits=[evl, ev_i1, ev_ab, ev_sq_last])
        e2 = p.emit(V, tt(sq_s[:], sq_s[:], i1_b[:], ALU.mult), waits=[e1])
        e3 = p.emit(V, cp(xn1_bf[k][:], sq_s[:]), waits=[e2])
        if ev_ab is None:
            ev_ab = p.emit(V, red(absa[:, 0:1], sq_s[:], ALU.max, absv=True), waits=[e2])
        else:
            e4 = p.emit(V, red(absa[:, 1:2], sq_s[:], ALU.max, absv=True), waits=[e2])
            ev_ab = p.emit(V, tt(absa[:, 0:1], absa[:, 0:1], absa[:, 1:2], ALU.max), waits=[e4, ev_ab])
        ev_xn1.append(e3)
        ev_sq_last = e3
        ev_xr3[r] = e2
    dump("xn1", xn1_bf[0][:], [ev_xn1[0]])
    LSB.close()
    PMX = ExitStack()
    ps_px = PMX.enter_context(nc.psum_tensor("ps_px", [128, 128], F32))
    e_tp = p.emit(TE, trf(ps_px[0:1, :], absa[:, 0:1]), waits=[ev_ab])
    e_te = p.emit(V, cp(misc[0:1, 0:128], ps_px[0:1, :]), waits=[e_tp])
    e_tm = p.emit(V, red(srow[0:1, 120:121], misc[0:1, 0:128], ALU.max), waits=[e_te])
    e_sx = p.emit(G, dmaf(ar_x_in[:, :], srow[0:1, 120:121]), waits=[e_tm], dma=True)
    ev_ar_x = coll("AllReduce", ALU.max, ar_x_in[:, :], ar_x_out[:, :], [e_sx])
    e_lx = p.emit(G, dmaf(srow[0:1, 121:122], ar_x_out[:, :]), waits=[ev_ar_x], dma=True)
    PMX.close()
    cut("ln1")

    AT = ExitStack()

    def sbat(name, shape, dt=F32):
        return AT.enter_context(nc.sbuf_tensor(name, list(shape), dt))

    q_il = [sbat(f"qil{k}", [128, T], BF16) for k in range(DC)]
    k_il = [sbat(f"kil{k}", [128, T], BF16) for k in range(DC)]
    wm_bf = [sbat(f"wmbf{j}", [128, D], BF16) for j in range(2)]
    wstage = sbat("wstage", [128, D])
    wvT_b = [sbat(f"wvT{k}", [128, D], BF16) for k in range(DC)]
    bqk = sbat("bqk", [128, 16])
    bv_bc = sbat("bv_bc", [128, D])
    nk2_b = sbat("nk2_b", [128, T])
    vt_r = [sbat(f"vt{j}", [128, D], BF16) for j in range(2)]
    e_til = sbat("e_til", [128, 128], BF16)
    s_til = sbat("s_til", [128, 128], BF16)
    st_til = sbat("st_til", [128, 128], BF16)
    ostg = sbat("ostg", [128, D], BF16)

    ev_bqk = p.emit(SY, dmaf(bqk[:, :], l1_bqk[:, :]), dma=True)
    ev_bvl = p.emit(SY, dmaf(l1r2[0:1, 0:D], l1_bv_row[:, :]), dma=True)
    ev_bvb = bcast1(bv_bc, l1r2[0:1, 0:D], D, [ev_bvl])
    ev_wc = None
    for k in range(DC):
        evl = p.emit(SY, dmaf(wstage[:, 0:D], l1_w_T[2, 128 * k:128 * (k + 1), :]),
                     waits=[ev_wc] if ev_wc else [ev_xr3[0]], dma=True)
        ev_wc = p.emit(V, cp(wvT_b[k][:], wstage[:, 0:D]), waits=[evl])
    QK1 = ExitStack()
    ps_qk = QK1.enter_context(nc.psum_tensor("ps_qk", [128, 512], F32))
    ev_q1 = []
    ev_wm = [ev_wc, ev_wc]
    ev_wstg = ev_wc
    for w_i, dstl, bcol in ((0, q_il, 0), (1, k_il, 8)):
        for m in range(8):
            r = m % 2
            evl = p.emit(SY, dmaf(wstage[:, 0:D].rearrange("p (k c) -> p k c", k=8),
                                  l1_w_T[w_i, :, 128 * m:128 * (m + 1)]
                                  .rearrange("(k p) c -> k p c", p=128).transpose([1, 0, 2])),
                         waits=[ev_wm[r], ev_wstg], dma=True)
            evc = p.emit(V, cp(wm_bf[r][:], wstage[:, 0:D]), waits=[evl])
            ev_wstg = evc
            for nt in range(4):
                e_m = p.emit(TE, [mm(ps_qk[:, :], wm_bf[r][:, 128 * k:128 * (k + 1)],
                                     xn1_bf[k][:, 512 * nt:512 * (nt + 1)], k == 0, k == DC - 1)
                                  for k in range(DC)],
                             waits=[evc, ev_xn1[-1]] + ([ev_q1[-1]] if ev_q1 else []))
                e_e = p.emit(V, ts(dstl[m][:, 512 * nt:512 * (nt + 1)], ps_qk[:, :],
                                   bqk[:, bcol + m:bcol + m + 1], None, ALU.add),
                             waits=[e_m, ev_bqk])
                ev_q1.append(e_e)
            ev_wm[r] = ev_q1[-1]
    QK1.close()
    K2 = ExitStack()
    ps_k2 = K2.enter_context(nc.psum_tensor("ps_k2", [1, T], F32))
    mm_k2 = []
    for k in range(DC):
        e_sq = p.emit(A, act(sq_s[:], k_il[k][:], AF.Square),
                      waits=[ev_q1[-1]] + ([mm_k2[-1]] if mm_k2 else []))
        e_m = p.emit(TE, [mm(ps_k2[:, 512 * j:512 * (j + 1)], ones128[:, 0:1],
                             sq_s[:, 512 * j:512 * (j + 1)], k == 0, k == DC - 1) for j in range(4)],
                     waits=[e_sq])
        mm_k2.append(e_m)
    e_k2r = p.emit(V, ts(l1r2[0:1, 0:T], ps_k2[:, :], -0.5), waits=[mm_k2[-1], ev_bvb])
    K2.close()
    ev_nk2 = bcast1(nk2_b, l1r2[0:1, 0:T], T, [e_k2r])

    GR = ExitStack()
    ps_v2 = GR.enter_context(nc.psum_tensor("ps_v2", [128, 1024], F32))
    ps_gr = GR.enter_context(nc.psum_tensor("ps_gr", [128, 512], F32))
    ps_st = GR.enter_context(nc.psum_tensor("ps_st", [128, 1024], BF16))
    ps_o1 = GR.enter_context(nc.psum_tensor("ps_o1", [128, 1024], F32))
    ev_ost = []
    ev_vt = [None, None]
    ev_grprev = None
    for t in range(16):
        r = t % 2
        e_vm = p.emit(TE, [mm(ps_v2[:, 512 * n2:512 * (n2 + 1)],
                              xn1_bf[k][:, 128 * t:128 * (t + 1)],
                              wvT_b[k][:, 512 * n2:512 * (n2 + 1)], k == 0, k == DC - 1)
                           for n2 in range(2) for k in range(DC)],
                      waits=[ev_wc, ev_xn1[-1], ev_vt[r]])
        e_ve = p.emit(V, [tt(vt_r[r][:, 512 * n2:512 * (n2 + 1)], ps_v2[:, 512 * n2:512 * (n2 + 1)],
                             bv_bc[:, 512 * n2:512 * (n2 + 1)], ALU.add) for n2 in range(2)],
                      waits=[e_vm, ev_bvb])
        e_gm = p.emit(TE, [mm(ps_gr[:, 0:128], q_il[k][:, 128 * t:128 * (t + 1)],
                              k_il[k][:, 128 * t:128 * (t + 1)], k == 0, k == DC - 1)
                           for k in range(DC)],
                      waits=[ev_q1[-1]] + ([ev_grprev] if ev_grprev else []))
        e_z1 = p.emit(V, tt(misc[:, 0:128], ps_gr[:, 0:128], nk2_b[:, 128 * t:128 * (t + 1)],
                            ALU.add), waits=[e_gm, ev_nk2])
        e_z2 = p.emit(V, tt(misc[:, 128:256], misc[:, 0:128], maskb[:], ALU.add), waits=[e_z1])
        ev_grprev = e_z1
        e_nm = p.emit(V, red(scol[:, 2:3], misc[:, 128:256], ALU.max, neg=True), waits=[e_z2])
        e_ex = p.emit(A, act(e_til[:], misc[:, 128:256], AF.Exp, bias=scol[:, 2:3]), waits=[e_nm])
        e_dn = p.emit(V, red(scol[:, 3:4], e_til[:], ALU.add), waits=[e_ex])
        e_rc = p.emit(V, recip(scol[:, 4:5], scol[:, 3:4]), waits=[e_dn])
        e_s = p.emit(V, ts(s_til[:], e_til[:], scol[:, 4:5]), waits=[e_rc])
        e_tr = p.emit(TE, trb(ps_st[:, 0:128], s_til[:]), waits=[e_s])
        e_ste = p.emit(A, act(st_til[:], ps_st[:, 0:128], AF.Copy), waits=[e_tr])
        e_om = p.emit(TE, [mm(ps_o1[:, 512 * n2:512 * (n2 + 1)], st_til[:],
                              vt_r[r][:, 512 * n2:512 * (n2 + 1)], True, True) for n2 in range(2)],
                      waits=[e_ste, e_ve])
        e_oe = p.emit(A, [act(ostg[:, 512 * n2:512 * (n2 + 1)], ps_o1[:, 512 * n2:512 * (n2 + 1)],
                              AF.Copy) for n2 in range(2)],
                      waits=[e_om] + ([ev_ost[-1]] if ev_ost else []))
        e_os = p.emit(SY, dmaf(o_dram[t, :, :], ostg[:, :]), waits=[e_oe], dma=True)
        ev_ost.append(e_os)
        ev_vt[r] = e_om
    GR.close()
    dump("o_l1", ostg[:, :], [ev_ost[-1]])
    AT.close()
    XQ = ExitStack()
    e_cll = p.emit(SY, dmaf(clip2[:, :], l1_clip[:, :]), dma=True)
    e_sig = p.emit(A, act(clip2[:, :], clip2[:, :], AF.Sigmoid), waits=[e_cll])
    e_t1 = p.emit(V, tt(srow[0:1, 122:123], clip2[0:1, 0:1], srow[0:1, 121:122], ALU.mult),
                  waits=[e_sig, e_lx])
    e_r1 = p.emit(V, recip(srow[0:1, 123:124], srow[0:1, 122:123]), waits=[e_t1])
    e_i1 = p.emit(V, ts(srow[0:1, 124:125], srow[0:1, 123:124], 127.0), waits=[e_r1])
    e_s1 = p.emit(V, ts(srow[0:1, 125:126], srow[0:1, 122:123], 1.0 / 127.0), waits=[e_t1])
    ps_xx = XQ.enter_context(nc.psum_tensor("ps_xx", [128, 128], F32))
    e_bx = p.emit(TE, mm(ps_xx[:, 0:2], ones1[:, 0:128], srow[0:1, 124:126], True, True),
                  waits=[e_i1, e_s1])
    e_bxe = p.emit(V, cp(scol[:, 5:7], ps_xx[:, 0:2]), waits=[e_bx])
    ev_xq = []
    for k in range(DC):
        e1 = p.emit(V, ts(sq_s[:], xn1_bf[k][:], scol[:, 5:6], MAGIC, ALU.mult, ALU.add),
                    waits=[e_bxe, ev_xn1[k], ev_sq_last] + ([ev_xq[-1]] if ev_xq else []))
        e2 = p.emit(V, ts(sq_s[:], sq_s[:], MAGIC, None, ALU.subtract), waits=[e1])
        e3 = p.emit(V, ts(xq_bf[k][:], sq_s[:], -128.0, 127.0, ALU.max, ALU.min), waits=[e2])
        ev_xq.append(e3)
        ev_sq_last = e3
    dump("xq", xq_bf[0][:], [ev_xq[0]])
    XQ.close()
    LA.close()
    cut("l1attn")

    FF = ExitStack()

    def sbf(name, shape, dt=F32):
        return FF.enter_context(nc.sbuf_tensor(name, list(shape), dt))

    rinv_g = sbf("rinv_g", [128, IC])
    rinv_u = sbf("rinv_u", [128, IC])
    comb_g = sbf("comb_g", [128, IC])
    comb_u = sbf("comb_u", [128, IC])
    hp_t = [sbf(f"hpre{m}", [128, T], BF16) for m in range(IC)]
    habs = sbf("habs", [128, 2])
    GQ = ExitStack()
    rb_t = GQ.enter_context(nc.sbuf_tensor("rb_t", [128, I_FF], F32))
    wscr = GQ.enter_context(nc.sbuf_tensor("wscr", [128, 1408], F32))
    wbst = GQ.enter_context(nc.sbuf_tensor("wbst", [128, 1408], BF16))
    PSW = ExitStack()
    ps_w = PSW.enter_context(nc.psum_tensor("ps_w", [128, 128], F32))

    def wquant(nat, wT, qdram, rinv, combv, ev_after):
        evs = []
        for k in range(IC):
            r = k % 2
            evl = p.emit(SY, dmaf(xring[r][:, 0:D], nat[128 * k:128 * (k + 1), :]),
                         waits=[evs[-2][1] if len(evs) >= 2 else ev_after], dma=True)
            e_rm = p.emit(V, red(scol[:, 8:9], xring[r][:, 0:D], ALU.max, absv=True),
                          waits=[evl] + ([evs[-1][0], evs[-1][1]] if evs else []))
            e_rc = p.emit(V, recip(scol[:, 9:10], scol[:, 8:9]), waits=[e_rm])
            e_ri = p.emit(V, ts(rinv[:, k:k + 1], scol[:, 9:10], 127.0), waits=[e_rc])
            e_sw = p.emit(V, ts(combv[:, k:k + 1], scol[:, 8:9], 1.0 / 127.0), waits=[e_rm])
            evs.append((e_ri, e_sw))
        e_cmb = p.emit(V, tt(combv[:, :], combv[:, :],
                             scol[:, 6:7].broadcast_to([128, IC]), ALU.mult),
                       waits=[evs[-1][1], evs[-2][1], e_bxe])
        e_tp = p.emit(TE, trf(ps_w[0:IC, :], rinv[:, :]), waits=[evs[-1][0], evs[-2][0]])
        e_te = p.emit(V, cp(misc[0:IC, 0:128], ps_w[0:IC, :]), waits=[e_tp])
        e_rw = p.emit(SY, dmaf(l1r2[0:1, 0:I_FF], misc[0:IC, 0:128]),
                      waits=[e_te, ev_nk2], dma=True)
        e_rb = bcast1(rb_t, l1r2[0:1, 0:I_FF], I_FF, [e_rw])
        ev_q = []
        for k in range(DC):
            for hf in range(2):
                r = (2 * k + hf) % 2
                evl = p.emit(SY, dmaf(xring[r][:, 0:1408], wT[128 * k:128 * (k + 1),
                                                              1408 * hf:1408 * (hf + 1)]),
                             waits=[ev_q[-2] if len(ev_q) >= 2 else e_rb], dma=True)
                e_v = p.emit(V, tt(wscr[:], xring[r][:, 0:1408],
                                   rb_t[:, 1408 * hf:1408 * (hf + 1)], ALU.mult),
                             waits=[evl, e_rb] + ([ev_q[-1]] if ev_q else []))
                e_r = p.emit(V, ts(wbst[:], wscr[:], MAGIC, MAGIC, ALU.add, ALU.subtract),
                             waits=[e_v])
                e_st = p.emit(SY, dmaf(qdram[k, :, 1408 * hf:1408 * (hf + 1)], wbst[:]),
                              waits=[e_r], dma=True)
                ev_q.append(e_st)
        return e_cmb, ev_q[-1], ev_q[-2]

    ev_cg, ev_qg1, ev_qg2 = wquant(wg_nat, wg_T, wgq_dram, rinv_g, comb_g, ev_xq[-1])
    ev_cu, ev_qu1, ev_qu2 = wquant(wu_nat, wu_T, wuq_dram, rinv_u, comb_u, ev_qg1)
    PSW.close()
    GQ.close()
    cut("gq")

    GU = ExitStack()
    wmr = [GU.enter_context(nc.sbuf_tensor(f"wmr{j}", [128, D], BF16)) for j in range(2)]
    gsil = GU.enter_context(nc.sbuf_tensor("gsil", [128, T], BF16))
    PSG = ExitStack()
    ps_g = PSG.enter_context(nc.psum_tensor("ps_g", [128, 512], F32))
    ev_hp = []
    ev_wmr = [ev_qg1, ev_qg2]
    ev_hab = None
    for m in range(IC):
        r = m % 2
        evlg = p.emit(SY, dmaf(wmr[r][:].rearrange("p (k c) -> p k c", k=8),
                               wgq_dram[:, :, 128 * m:128 * (m + 1)].transpose([1, 0, 2])),
                      waits=[ev_wmr[r], ev_qg1, ev_qu1], dma=True)
        ev_gm = []
        for nt in range(4):
            e_m = p.emit(TE, [mm(ps_g[:, :], wmr[r][:, 128 * k:128 * (k + 1)],
                                 xq_bf[k][:, 512 * nt:512 * (nt + 1)], k == 0, k == DC - 1)
                              for k in range(DC)],
                         waits=[evlg, ev_xq[-1]] + ([ev_gm[-1]] if ev_gm else
                                                    ([ev_hp[-1]] if ev_hp else [])))
            e_e = p.emit(A, act(gsil[:, 512 * nt:512 * (nt + 1)], ps_g[:, :], AF.Silu,
                                scale=comb_g[:, m:m + 1]), waits=[e_m, ev_cg])
            ev_gm.append(e_e)
        evlu = p.emit(SY, dmaf(wmr[r][:].rearrange("p (k c) -> p k c", k=8),
                               wuq_dram[:, :, 128 * m:128 * (m + 1)].transpose([1, 0, 2])),
                      waits=[ev_gm[-1]], dma=True)
        ev_um = []
        for nt in range(4):
            e_m = p.emit(TE, [mm(ps_g[:, :], wmr[r][:, 128 * k:128 * (k + 1)],
                                 xq_bf[k][:, 512 * nt:512 * (nt + 1)], k == 0, k == DC - 1)
                              for k in range(DC)],
                         waits=[evlu] + ([ev_um[-1]] if ev_um else []))
            e_e = p.emit(V, ts(misc[:, 0:512], ps_g[:, :], comb_u[:, m:m + 1]),
                         waits=[e_m, ev_cu])
            e_h = p.emit(V, tt(hp_t[m][:, 512 * nt:512 * (nt + 1)],
                               gsil[:, 512 * nt:512 * (nt + 1)], misc[:, 0:512], ALU.mult),
                         waits=[e_e])
            ev_um.append(e_h)
        ev_wmr[r] = ev_um[-1]
        ev_hp.append(ev_um[-1])
        if ev_hab is None:
            ev_hab = p.emit(V, red(habs[:, 0:1], hp_t[m][:], ALU.max, absv=True), waits=[ev_um[-1]])
        else:
            e4 = p.emit(V, red(habs[:, 1:2], hp_t[m][:], ALU.max, absv=True), waits=[ev_um[-1]])
            ev_hab = p.emit(V, tt(habs[:, 0:1], habs[:, 0:1], habs[:, 1:2], ALU.max),
                            waits=[e4, ev_hab])
    PSG.close()
    GU.close()
    dump("hpre", hp_t[0][:], [ev_hp[0]])
    PMH = ExitStack()
    ps_ph = PMH.enter_context(nc.psum_tensor("ps_ph", [128, 128], F32))
    e_tp = p.emit(TE, trf(ps_ph[0:1, :], habs[:, 0:1]), waits=[ev_hab])
    e_te = p.emit(V, cp(misc[0:1, 0:128], ps_ph[0:1, :]), waits=[e_tp])
    e_tm = p.emit(V, red(srow[0:1, 126:127], misc[0:1, 0:128], ALU.max), waits=[e_te])
    e_sh = p.emit(G, dmaf(ar_h_in[:, :], srow[0:1, 126:127]), waits=[e_tm], dma=True)
    ev_ar_h = coll("AllReduce", ALU.max, ar_h_in[:, :], ar_h_out[:, :], [e_sh])
    e_lh = p.emit(G, dmaf(srow[0:1, 127:128], ar_h_out[:, :]), waits=[ev_ar_h], dma=True)
    e_t2 = p.emit(V, tt(srow[0:1, 128:129], clip2[0:1, 1:2], srow[0:1, 127:128], ALU.mult),
                  waits=[e_lh, e_sig])
    e_r2 = p.emit(V, recip(srow[0:1, 129:130], srow[0:1, 128:129]), waits=[e_t2])
    e_i2 = p.emit(V, ts(srow[0:1, 130:131], srow[0:1, 129:130], 127.0), waits=[e_r2])
    e_s2 = p.emit(V, ts(srow[0:1, 131:132], srow[0:1, 128:129], 1.0 / 127.0), waits=[e_t2])
    e_bh = p.emit(TE, mm(ps_ph[:, 0:2], ones1[:, 0:128], srow[0:1, 130:132], True, True),
                  waits=[e_i2, e_s2])
    e_bhe = p.emit(V, cp(scol[:, 8:10], ps_ph[:, 0:2]), waits=[e_bh])
    PMH.close()
    cut("gu")
    ev_hq = []
    for m in range(IC):
        e1 = p.emit(V, ts(sq_s[:], hp_t[m][:], scol[:, 8:9], MAGIC, ALU.mult, ALU.add),
                    waits=[e_bhe, ev_hp[m]] + ([ev_hq[-1]] if ev_hq else []))
        e2 = p.emit(V, ts(sq_s[:], sq_s[:], MAGIC, None, ALU.subtract), waits=[e1])
        e3 = p.emit(V, ts(hp_t[m][:], sq_s[:], -128.0, 127.0, ALU.max, ALU.min), waits=[e2])
        ev_hq.append(e3)

    sd_c = sbf("sd_c", [128, 8])
    rinv_d = sbf("rinv_d", [128, 8])
    rbH = sbf("rbH", [128, D])
    sdh_b = sbf("sdh_b", [128, D])
    PSD = ExitStack()
    ps_d0 = PSD.enter_context(nc.psum_tensor("ps_d0", [128, 128], F32))
    ev_ds = []
    for k in range(DC):
        r = k % 2
        evl = p.emit(SY, [dmaf(xring[r][:, 0:1408], wd_nat[128 * k:128 * (k + 1), 0:1408]),
                          dmaf(xring[r][:, 1408:2016],
                               wd_nat[128 * k:128 * (k + 1), 1408:2016])],
                     waits=[ev_ds[-2] if len(ev_ds) >= 2 else ev_hq[-1]], dma=True)
        evl2 = p.emit(SY, dmaf(sq_s[:, 0:800], wd_nat[128 * k:128 * (k + 1), 2016:2816]),
                      waits=[evl], dma=True)
        e_r1 = p.emit(V, red(scol[:, 10:11], xring[r][:, 0:2016], ALU.max, absv=True),
                      waits=[evl])
        e_r2 = p.emit(V, red(scol[:, 11:12], sq_s[:, 0:800], ALU.max, absv=True), waits=[evl2])
        e_mx = p.emit(V, tt(scol[:, 10:11], scol[:, 10:11], scol[:, 11:12], ALU.max),
                      waits=[e_r1, e_r2])
        e_sd = p.emit(V, ts(sd_c[:, k:k + 1], scol[:, 10:11], 1.0 / 127.0), waits=[e_mx])
        e_rc = p.emit(V, recip(scol[:, 12:13], scol[:, 10:11]), waits=[e_mx])
        e_ri = p.emit(V, ts(rinv_d[:, k:k + 1], scol[:, 12:13], 127.0), waits=[e_rc])
        ev_ds.append(e_ri if k % 2 else e_sd)
        ev_ds.append(e_sd if k % 2 else e_ri)
    e_tp1 = p.emit(TE, trf(ps_d0[0:8, :], rinv_d[:, :]), waits=[ev_ds[-1], ev_ds[-2]])
    e_te1 = p.emit(V, cp(misc[0:8, 0:128], ps_d0[0:8, :]), waits=[e_tp1])
    e_rw1 = p.emit(SY, dmaf(l1r2[0:1, 0:D], misc[0:8, 0:128]), waits=[e_te1], dma=True)
    e_rb1 = bcast1(rbH, l1r2[0:1, 0:D], D, [e_rw1])
    e_tp2 = p.emit(TE, trf(ps_d0[0:8, :], sd_c[:, :]), waits=[e_rb1, ev_ds[-1]])
    e_te2 = p.emit(V, cp(misc[0:8, 0:128], ps_d0[0:8, :]), waits=[e_tp2])
    e_rw2 = p.emit(SY, dmaf(l1r2[0:1, 0:D], misc[0:8, 0:128]), waits=[e_te2], dma=True)
    e_sdr = p.emit(V, ts(l1r2[0:1, D:2 * D], l1r2[0:1, 0:D], srow[0:1, 131:132]),
                   waits=[e_rw2, e_s2])
    e_sdb = bcast1(sdh_b, l1r2[0:1, D:2 * D], D, [e_sdr])
    PSD.close()
    cut("sdown")

    wdq = [sbf(f"wdq{k}", [128, 512], BF16) for k in range(IC)]
    or_t = sbf("or_t", [128, D], BF16)
    gh_t = sbf("gh_t", [128, 512], BF16)
    sgf = sbf("sgf", [128, 512])
    xnb = sbf("xnb", [128, 128])
    xo_st = sbf("xo_st", [128, 128])
    PDD = ExitStack()
    ps_dd = PDD.enter_context(nc.psum_tensor("ps_dd", [128, 512], F32))
    ps_tt = PDD.enter_context(nc.psum_tensor("ps_tt", [128, 512], F32))
    ev_out_st = []
    ev_last_sgtr = None
    for hf in range(2):
        ev_wq = []
        for k in range(IC):
            r = k % 2
            evl = p.emit(SY, dmaf(xring[r][:, 0:512],
                                  wd_T[128 * k:128 * (k + 1), 512 * hf:512 * (hf + 1)]),
                         waits=[ev_wq[-2] if len(ev_wq) >= 2 else (e_sdb if hf == 0 else ev_out_st[-1])],
                         dma=True)
            e_v = p.emit(V, tt(misc[:, 0:512], xring[r][:, 0:512],
                               rbH[:, 512 * hf:512 * (hf + 1)], ALU.mult),
                         waits=[evl, e_rb1] + ([ev_wq[-1]] if ev_wq else []))
            e_r = p.emit(V, ts(wdq[k][:], misc[:, 0:512], MAGIC, MAGIC, ALU.add, ALU.subtract),
                         waits=[e_v])
            ev_wq.append(e_r)
        for m in range(16):
            e_ol = p.emit(SY, dmaf(or_t[:, :], o_dram[m, :, :]),
                          waits=[ev_ost[m]] + ([ev_out_st[-1]] if ev_out_st else []), dma=True)
            e_dm = p.emit(TE, [mm(ps_dd[:, :], hp_t[k][:, 128 * m:128 * (m + 1)], wdq[k][:],
                                  k == 0, k == IC - 1) for k in range(IC)],
                          waits=[ev_wq[-1], ev_hq[-1]] + ([ev_last_sgtr] if ev_last_sgtr else []))
            e_sc = p.emit(V, tt(misc[:, 0:512], ps_dd[:, :], sdh_b[:, 512 * hf:512 * (hf + 1)],
                                ALU.mult), waits=[e_dm, e_sdb])
            e_g = p.emit(A, act(gh_t[:], misc[:, 0:512], AF.Silu), waits=[e_sc])
            e_sg = p.emit(V, tt(sgf[:], gh_t[:], or_t[:, 512 * hf:512 * (hf + 1)], ALU.mult),
                          waits=[e_g, e_ol])
            for j in range(4):
                e_tr = p.emit(TE, trf(ps_tt[:, 128 * j:128 * (j + 1)], sgf[:, 128 * j:128 * (j + 1)]),
                              waits=[e_sg] + ([ev_last_sgtr] if ev_last_sgtr else []))
                kk = 4 * hf + j
                e_xl = p.emit(SY, dmaf(xnb[:, :], xnew_dram[128 * kk:128 * (kk + 1),
                                                            128 * m:128 * (m + 1)]),
                              waits=[ev_xnew_st[kk]] + ([ev_out_st[-1]] if ev_out_st else []),
                              dma=True)
                e_ad = p.emit(V, tt(xo_st[:, :], ps_tt[:, 128 * j:128 * (j + 1)], xnb[:, :],
                                    ALU.add), waits=[e_tr, e_xl] +
                              ([ev_out_st[-1]] if ev_out_st else []))
                e_st = p.emit(SY, dmaf(out_t[128 * kk:128 * (kk + 1), 128 * m:128 * (m + 1)],
                                       xo_st[:, :]), waits=[e_ad], dma=True)
                ev_out_st.append(e_st)
                ev_last_sgtr = e_ad
    PDD.close()
    FF.close()
    L1.close()

    _finish(nc, p, out_t, sq_s)
    return nc


def _finish(nc, p, out_t, sq_s):
    plan_sems = p.sems

    def final_wait(e):
        for eng in ("sync", "gpsimd", "scalar"):
            for j in range(Plan.RING[eng]):
                sname = f"d{eng}{j}"
                if p.cnt[sname] > 0:
                    e.wait_ge(plan_sems[sname], p.cnt[sname])

    p.streams["sync"].append(final_wait)
    p.play()
    nv = p.psum_report()
    if nv and not os.environ.get("KPSUM_OK"):
        raise RuntimeError(f"{nv} unordered cross-engine PSUM bank access pairs")


def _to_il(a):
    return np.ascontiguousarray(a.transpose(2, 1, 0).reshape(a.shape[2], -1))


def build_safe(Dr, dbg=(), stop_after=None):
    if stop_after is None:
        return build_program(Dr, dbg)
    try:
        build_program(Dr, dbg, stop_after)
    except StopPlan as e:
        return e.nc
    raise RuntimeError(f"stop_after tag {stop_after!r} not reached")


def kernel(**inp):
    X = np.asarray(inp["L0_attn_dense"], np.float32)
    rows_i = np.asarray(inp["L0_attn_adj_rows"]).astype(np.int64)
    cols_i = np.asarray(inp["L0_attn_adj_cols"]).astype(np.int64)
    vals = np.asarray(inp["L0_attn_adj_vals"], np.float32)

    keep = rows_i < 2048
    rows_k = rows_i[keep]
    cols_k = np.minimum(cols_i[keep], 2047)
    vals_k = vals[keep]
    key = rows_k * 2048 + cols_k
    order = np.argsort(key, kind="stable")
    sk = key[order]
    first = np.r_[True, sk[1:] != sk[:-1]]
    idx = np.arange(len(sk))
    grp_start = np.maximum.accumulate(np.where(first, idx, 0))
    occ = idx - grp_start
    Dr = int(occ.max()) + 1 if len(occ) else 1
    canv = np.zeros((Dr, 2048, 2048), np.float32)
    canv[occ, cols_k[order], rows_k[order]] = vals_k[order]

    qkv_w = np.asarray(inp["L0_attn_qkv_w"], np.float32)
    wv_w = np.asarray(inp["L0_attn_wv_w"], np.float32)
    wv_b = np.asarray(inp["L0_attn_wv_b"], np.float32)
    out_w = np.asarray(inp["L0_attn_out_w"], np.float32)
    out_b = np.asarray(inp["L0_attn_out_b"], np.float32)
    wq_g = np.asarray(inp["L0_attn_wq"], np.float32)
    wk_g = np.asarray(inp["L0_attn_wk"], np.float32)
    conv_w = np.asarray(inp["L0_attn_conv_w"], np.float32)
    w1 = np.asarray(inp["L0_ffn_w1"], np.float32)
    b1 = np.asarray(inp["L0_ffn_b1"], np.float32)
    w2 = np.asarray(inp["L0_ffn_w2"], np.float32)
    b2 = np.asarray(inp["L0_ffn_b2"], np.float32)
    cent = np.asarray(inp["L0_ffn_centroids"], np.float32)
    w_route = np.asarray(inp["L0_ffn_w_route"], np.float32)
    Wq1 = np.asarray(inp["L1_attn_Wq"], np.float32)
    bq1 = np.asarray(inp["L1_attn_bq"], np.float32)
    Wk1 = np.asarray(inp["L1_attn_Wk"], np.float32)
    bk1 = np.asarray(inp["L1_attn_bk"], np.float32)
    Wv1 = np.asarray(inp["L1_attn_Wv"], np.float32)
    bv1 = np.asarray(inp["L1_attn_bv"], np.float32)
    clip_x = np.asarray(inp["L1_ffn_clip_x"], np.float32)
    clip_d = np.asarray(inp["L1_ffn_clip_down"], np.float32)
    w_gate = np.asarray(inp["L1_ffn_w_gate"], np.float32)
    w_up = np.asarray(inp["L1_ffn_w_up"], np.float32)
    w_down = np.asarray(inp["L1_ffn_w_down"], np.float32)

    taps = np.ascontiguousarray(np.repeat(conv_w[:, 0, :, 0], 64, axis=0))
    l1wT = np.ascontiguousarray(np.stack([Wq1.T, Wk1.T, Wv1.T], 0))
    l1bqk = np.ascontiguousarray(np.concatenate([bq1.reshape(8, 128).T,
                                                 bk1.reshape(8, 128).T], axis=1))
    b1cols = np.ascontiguousarray(b1.reshape(8, 4, 128).transpose(2, 0, 1).reshape(128, 32))

    shared = {
        "qkv_w": qkv_w, "wv_w": wv_w,
        "wv_b": np.ascontiguousarray(wv_b.reshape(8, 128).T),
        "out_w": out_w,
        "out_b": np.ascontiguousarray(out_b.reshape(8, 128).T),
        "wq_g": wq_g, "wk_g": wk_g, "conv_taps": taps,
        "moe_w1": w1, "moe_b1": b1cols, "moe_w2": w2, "moe_b2": b2,
        "cent": cent, "cent_T": np.ascontiguousarray(cent.T),
        "route_T": np.ascontiguousarray(w_route.T),
        "l1_w_T": l1wT, "l1_bqk": l1bqk,
        "l1_bv_row": np.ascontiguousarray(bv1.reshape(1, 1024)),
        "l1_clip": np.ascontiguousarray(np.stack([clip_x[0], clip_d[0]]).reshape(1, 2)),
        "wg_nat": w_gate, "wu_nat": w_up, "wd_nat": w_down,
        "wg_T": np.ascontiguousarray(w_gate.T),
        "wu_T": np.ascontiguousarray(w_up.T),
        "wd_T": np.ascontiguousarray(w_down.T),
    }

    in_maps = []
    for c in range(NCORES):
        im = dict(shared)
        lo = S * c
        im["x_fm"] = _to_il(X[:, lo:lo + S, :])
        left = X[0, lo - 16:lo, :].T if c > 0 else np.zeros((D, 16), np.float32)
        right = X[0, lo + S:lo + S + 16, :].T if c < NCORES - 1 else np.zeros((D, 16), np.float32)
        im["x_halo"] = np.ascontiguousarray(np.concatenate([left, right], axis=1))
        im["qkvw_head"] = np.ascontiguousarray(
            np.concatenate([qkv_w[:, 64 * c:64 * (c + 1)],
                            qkv_w[:, 512 + 64 * c:512 + 64 * (c + 1)]], axis=1))
        sel = np.zeros((512, 64), np.float32)
        sel[64 * c:64 * (c + 1)] = np.eye(64, dtype=np.float32)
        im["sel_head"] = sel.astype(bfd).astype(np.float32) if False else sel
        for d in range(Dr):
            im[f"wt_can{d}"] = np.ascontiguousarray(canv[d][:, lo:lo + S])
        in_maps.append(im)

    nc = build_safe(Dr, stop_after=os.environ.get("KSTOP") or None)
    res = run_bass_kernel_spmd(nc, in_maps, core_ids=list(range(NCORES)))
    out = np.empty((8, 2048, 1024), np.float32)
    for c in range(NCORES):
        o = res.results[c]["out_il"]
        out[:, S * c:S * (c + 1), :] = o.reshape(D, S, 8).transpose(2, 1, 0)
    return out


# version 16
# speedup vs baseline: 1.0082x; 1.0082x over previous
import sys
import numpy as np
import ml_dtypes
import os
from contextlib import ExitStack
import concourse.bass as bass
import concourse.mybir as mybir
from concourse.bass_utils import run_bass_kernel_spmd

F32 = mybir.dt.float32
BF16 = mybir.dt.bfloat16
AF = mybir.ActivationFunctionType
ALU = mybir.AluOpType
AX = mybir.AxisListType

NCORES = 8
S = 256
B = 8
T = 2048
D = 1024
DC = 8
MAGIC = 12582912.0
NEG = -3.0e38
EPS_LN = 1e-5
I_FF = 2816
IC = 22
RG = [list(range(NCORES))]
bfd = ml_dtypes.bfloat16


class _EngProxy:
    def __init__(self, e, plan):
        self._e = e
        self._plan = plan

    def __getattr__(self, name):
        real = getattr(self._e, name)
        if not callable(real):
            return real
        plan = self._plan

        def wrap(*args, **kwargs):
            for o in list(args) + list(kwargs.values()):
                for item in (o if isinstance(o, (list, tuple)) else (o,)):
                    sp = getattr(item, "space", None)
                    if sp is not None and getattr(sp, "value", None) == "psum":
                        plan.note_psum(item)
            return real(*args, **kwargs)

        return wrap


class Plan:
    ENGS = ("tensor", "vector", "scalar", "gpsimd", "sync")
    OWN = {"tensor": "T", "vector": "V", "scalar": "A", "gpsimd": "G", "sync": "SY"}
    RING = {"sync": 6, "scalar": 2, "gpsimd": 4, "vector": 2}

    def __init__(self, nc):
        self.nc = nc
        self.streams = {e: [] for e in self.ENGS}
        self.sems = {}
        self.cnt = {}
        self.hwm = {e: {} for e in self.ENGS}
        for s in ("T", "V", "A", "G", "SY"):
            self.sems[s] = nc.alloc_semaphore("sem_" + s)
            self.cnt[s] = 0
        self.dma_use = {}
        for e, r in self.RING.items():
            self.dma_use[e] = 0
            for j in range(r):
                s = f"d{e}{j}"
                self.sems[s] = nc.alloc_semaphore("sem_" + s)
                self.cnt[s] = 0
        self.groups = []
        self.sem_hist = {}
        self.cur_gid = None
        self.bank_acc = {}
        self.reach = []
        self.eng_last = {}
        self.violations = []

    def _filter(self, eng, waits):
        out, h, own = [], self.hwm[eng], self.OWN[eng]
        for ev in waits:
            if ev is None:
                continue
            s, v = ev
            if v <= 0 or h.get(s, 0) >= v:
                continue
            h[s] = v
            out.append((s, v))
        return out

    def emit(self, eng, fns, waits=(), dma=False):
        if callable(fns):
            fns = [fns]
        fns = [f for f in fns if f is not None]
        assert fns, "empty emit"
        if dma:
            idx = self.dma_use[eng]
            self.dma_use[eng] += 1
            sem = f"d{eng}{idx % self.RING[eng]}"
            waits = list(waits) + [(sem, self.cnt[sem])]
            self.cnt[sem] += 16 * len(fns)
        else:
            sem = self.OWN[eng]
            self.cnt[sem] += 1
        val = self.cnt[sem]
        w = tuple(self._filter(eng, waits))
        semh = self.sems[sem]
        plan = self

        src_line = sys._getframe(1).f_lineno
        gid = len(self.groups)
        self.groups.append((eng, sem, val, w, src_line))
        self.sem_hist.setdefault(sem, []).append((val, gid))
        r = 1 << gid
        if eng in self.eng_last:
            r |= self.reach[self.eng_last[eng]]
        for (s, v) in w:
            tg = self._sem_gid(s, v)
            if tg is not None:
                r |= self.reach[tg]
        self.reach.append(r)
        self.eng_last[eng] = gid

        def closure(e, fns=tuple(fns), w=w, semh=semh, dma=dma, src_line=src_line,
                    gid=gid):
            plan.cur_gid = gid
            ep = _EngProxy(e, plan)
            try:
                for (s, v) in w:
                    e.wait_ge(plan.sems[s], v)
                if dma:
                    for f in fns:
                        f(ep).then_inc(semh, 16)
                else:
                    ins = None
                    for f in fns:
                        ins = f(ep)
                    ins.then_inc(semh, 1)
            except Exception:
                print(f"[plan] failure in group emitted at kernel.py:{src_line}")
                raise

        self.streams[eng].append(closure)
        return (sem, val)

    def _sem_gid(self, s, v):
        hist = self.sem_hist.get(s)
        if not hist:
            return None
        lo, hi = 0, len(hist)
        while lo < hi:
            mid = (lo + hi) // 2
            if hist[mid][0] >= v:
                hi = mid
            else:
                lo = mid + 1
        return hist[lo][1] if lo < len(hist) else None

    def note_psum(self, ap):
        g = self.cur_gid
        eng = self.groups[g][0]
        t = ap.tensor
        esz = 2 if t.dtype == BF16 else 4
        pairs = list(ap.ap)
        stride0 = pairs[0][0] if pairs else 1
        off = ap.offset % stride0 if stride0 > 0 else ap.offset
        span = 1
        for (st, n) in pairs[1:]:
            span += (n - 1) * abs(st)
        lo_b = (off * esz) // 2048
        hi_b = ((off + span) * esz - 1) // 2048
        for b in range(lo_b, hi_b + 1):
            key = (t.name, b)
            acc = self.bank_acc.setdefault(key, [])
            myr = self.reach[g]
            for (pg, peng) in acc:
                if peng != eng and not (myr >> pg) & 1 and not (self.reach[pg] >> g) & 1:
                    self.violations.append(
                        (t.name, b, peng, self.groups[pg][4], eng, self.groups[g][4],
                         pg, g))
            if not acc or acc[-1][0] != g:
                acc.append((g, eng))

    def psum_report(self):
        seen = set()
        import os as _os
        verbose = _os.environ.get("KPSUM_VERBOSE")
        for v in self.violations:
            if v[:6] in seen:
                continue
            seen.add(v[:6])
            print(f"[psumchk] UNORDERED: {v[0]} bank{v[1]}: "
                  f"{v[2]}@L{v[3]} vs {v[4]}@L{v[5]}")
            if verbose:
                for gg in (v[6], v[7]):
                    eng, sem, val, w, ln = self.groups[gg]
                    print(f"    gid{gg}: {eng} {sem}={val} L{ln} waits={list(w)}")
        return len(seen)

    def play(self):
        with self.nc.Block() as block:
            @block.tensor
            def _(e):
                for f in self.streams["tensor"]:
                    f(e)

            @block.vector
            def _(e):
                for f in self.streams["vector"]:
                    f(e)

            @block.scalar
            def _(e):
                for f in self.streams["scalar"]:
                    f(e)

            @block.gpsimd
            def _(e):
                for f in self.streams["gpsimd"]:
                    f(e)

            @block.sync
            def _(e):
                for f in self.streams["sync"]:
                    f(e)


class StopPlan(Exception):
    pass


def build_program(Dr, dbg=(), stop_after=None):
    nc = bass.Bass("TRN2", target_bir_lowering=False, debug=False,
                   dynamic_dma_scratch_size=8192)
    p = Plan(nc)

    def cut(tag):
        if stop_after is not None and tag == stop_after:
            _finish(nc, p, None, None)
            err = StopPlan()
            err.nc = nc
            raise err
    TE, V, A, G, SY = "tensor", "vector", "scalar", "gpsimd", "sync"

    def di(name, shape, dt=F32):
        return nc.dram_tensor(name, list(shape), dt, kind="ExternalInput").ap()

    def dint(name, shape, dt=F32):
        return nc.dram_tensor(name, list(shape), dt, kind="Internal").ap()

    dbg_t = {}
    dbg_ev = []

    def dbg_out(name, shape, dt=F32):
        if name in dbg:
            dbg_t[name] = nc.dram_tensor("dbg_" + name, list(shape), dt, kind="ExternalOutput").ap()
            return dbg_t[name]
        return None

    x_fm = di("x_fm", [D, T])
    x_halo = di("x_halo", [D, 32])
    qkv_w = di("qkv_w", [D, 1536])
    qkvw_head = di("qkvw_head", [D, 128])
    wv_w = di("wv_w", [D, D])
    wv_b = di("wv_b", [128, 8])
    out_w = di("out_w", [512, D])
    out_b = di("out_b", [128, 8])
    wq_g = di("wq_g", [D, 256])
    wk_g = di("wk_g", [D, 256])
    conv_taps = di("conv_taps", [512, 33])
    sel_head = di("sel_head", [512, 64])
    moe_w1 = di("moe_w1", [8, D, 512])
    moe_b1 = di("moe_b1", [128, 32])
    moe_w2 = di("moe_w2", [8, 512, D])
    moe_b2 = di("moe_b2", [8, D])
    cent = di("cent", [8, D])
    cent_T = di("cent_T", [D, 8])
    route_T = di("route_T", [D, 8])
    wt_can = [di(f"wt_can{d}", [T, S]) for d in range(Dr)]
    l1_w_T = di("l1_w_T", [3, D, D])
    l1_bqk = di("l1_bqk", [128, 16])
    l1_bv_row = di("l1_bv_row", [1, D])
    l1_clip = di("l1_clip", [1, 2])
    wg_nat = di("wg_nat", [I_FF, D])
    wu_nat = di("wu_nat", [I_FF, D])
    wd_nat = di("wd_nat", [D, I_FF])
    wg_T = di("wg_T", [D, I_FF])
    wu_T = di("wu_T", [D, I_FF])
    wd_T = di("wd_T", [I_FF, D])
    out_t = nc.dram_tensor("out_il", [D, T], F32, kind="ExternalOutput").ap()

    ag_lm_in = dint("ag_lm_in", [D, S])
    ag_lm_out = dint("ag_lm_out", [NCORES, D, S])
    ag_kv_in = dint("ag_kv_in", [1024, S], BF16)
    ag_kv_out = dint("ag_kv_out", [NCORES, 1024, S], BF16)
    ar2_in = dint("ar2_in", [1, 2])
    ar2_out = dint("ar2_out", [1, 2])
    ag_u_in = dint("ag_u_in", [S, 64])
    ag_u_out = dint("ag_u_out", [NCORES, S, 64])
    ag_kg_in = dint("ag_kg_in", [S, 256])
    ag_kg_out = dint("ag_kg_out", [NCORES, S, 256])
    ar_a_in = dint("ar_a_in", [1, 1])
    ar_a_out = dint("ar_a_out", [1, 1])
    ar_s_in = dint("ar_s_in", [1, 1])
    ar_s_out = dint("ar_s_out", [1, 1])
    ar_x_in = dint("ar_x_in", [1, 1])
    ar_x_out = dint("ar_x_out", [1, 1])
    ar_h_in = dint("ar_h_in", [1, 1])
    ar_h_out = dint("ar_h_out", [1, 1])
    g_dram = dint("g_dram", [D, T])
    xnew_dram = dint("xnew_dram", [D, T])
    w1bf_dram = dint("w1bf_dram", [8, DC, 128, 512], BF16)
    w2bf_dram = dint("w2bf_dram", [8, 8, 4, 128, 128], BF16)
    o_dram = dint("o_dram", [16, 128, D], BF16)
    grow_dram = dint("grow_dram", [1, 8 * T], BF16)
    wgq_dram = dint("wgq_dram", [DC, 128, I_FF], BF16)
    wuq_dram = dint("wuq_dram", [DC, 128, I_FF], BF16)

    identb_d = nc.inline_tensor(np.eye(128, dtype=bfd), name="c_identb").ap()
    identf_d = nc.inline_tensor(np.eye(128, dtype=np.float32), name="c_identf").ap()
    mask_np = np.where(np.kron(np.eye(16), np.ones((8, 8))) > 0, 0.0, NEG).astype(np.float32)
    maskb_d = nc.inline_tensor(mask_np, name="c_maskb").ap()
    dg = np.zeros((128, 512), np.float32)
    dg[np.arange(128), np.arange(128)] = 1.0
    dg[np.arange(128), 256 + 128 + np.arange(128)] = 1.0
    diagI_d = nc.inline_tensor(dg, name="c_diagI").ap()

    xring = [nc.alloc_sbuf_tensor(f"xring{j}", [128, T], F32) for j in range(2)]
    sq_s = nc.alloc_sbuf_tensor("sq_s", [128, T], F32)
    ones1 = nc.alloc_sbuf_tensor("ones1", [1, 128], F32)
    ones128 = nc.alloc_sbuf_tensor("ones128", [128, 1], F32)
    ones1b = nc.alloc_sbuf_tensor("ones1b", [1, 128], BF16)
    identb = nc.alloc_sbuf_tensor("identb", [128, 128], BF16)
    identf = nc.alloc_sbuf_tensor("identf", [128, 128], F32)
    maskb = nc.alloc_sbuf_tensor("maskb", [128, 128], F32)
    diagI = nc.alloc_sbuf_tensor("diagI", [128, 512], F32)
    cconst = nc.alloc_sbuf_tensor("cconst", [128, 5], F32)
    srow = nc.alloc_sbuf_tensor("srow", [1, 512], F32)
    scol = nc.alloc_sbuf_tensor("scol", [128, 16], F32)
    misc = nc.alloc_sbuf_tensor("misc", [128, 512], F32)

    ev_const = p.emit(V, [
        lambda e: e.memset(ones1[:], 1.0),
        lambda e: e.memset(ones128[:], 1.0),
        lambda e: e.memset(ones1b[:], 1.0),
        lambda e: e.memset(scol[0:64, 0:1], 0.125 / 8.0),
        lambda e: e.memset(scol[64:128, 0:1], 1.0 / 8.0),
        lambda e: e.memset(cconst[:, 0:1], EPS_LN),
        lambda e: e.memset(cconst[:, 1:2], 0.125),
        lambda e: e.memset(cconst[:, 2:3], 0.125 / 8.0),
        lambda e: e.memset(cconst[:, 3:4], -3.25),
        lambda e: e.memset(cconst[:, 4:5], -1.0),
    ])
    ev_cld = p.emit(SY, [
        lambda e: e.dma_start(out=identb[:], in_=identb_d[:, :]),
        lambda e: e.dma_start(out=identf[:], in_=identf_d[:, :]),
        lambda e: e.dma_start(out=maskb[:], in_=maskb_d[:, :]),
        lambda e: e.dma_start(out=diagI[:], in_=diagI_d[:, :]),
    ], dma=True)

    def mm(out_ap, lhsT_ap, rhs_ap, start, stop):
        return lambda e: e.matmul(out=out_ap, lhsT=lhsT_ap, rhs=rhs_ap, start=start, stop=stop)

    def act(out_ap, in_ap, func=AF.Copy, bias=0.0, scale=1.0):
        return lambda e: e.activation(out=out_ap, in_=in_ap, func=func, bias=bias, scale=scale)

    def tt(out_ap, a_ap, b_ap, op):
        return lambda e: e.tensor_tensor(out=out_ap, in0=a_ap, in1=b_ap, op=op)

    def ts(out_ap, in_ap, s1, s2=None, op0=ALU.mult, op1=ALU.bypass):
        return lambda e: e.tensor_scalar(out=out_ap, in0=in_ap, scalar1=s1, scalar2=s2,
                                         op0=op0, op1=op1)

    def stt(out_ap, in0_ap, scalar, in1_ap, op0, op1):
        return lambda e: e.scalar_tensor_tensor(out=out_ap, in0=in0_ap, scalar=scalar,
                                                in1=in1_ap, op0=op0, op1=op1)

    def red(out_ap, in_ap, op=ALU.add, absv=None, neg=None):
        return lambda e: e.tensor_reduce(out=out_ap, in_=in_ap, axis=AX.X, op=op,
                                         apply_absolute_value=absv, negate=neg)

    def recip(out_ap, in_ap):
        return lambda e: e.reciprocal(out=out_ap, in_=in_ap)

    def cp(out_ap, in_ap):
        return lambda e: e.tensor_copy(out=out_ap, in_=in_ap)

    def dmaf(dst, src):
        return lambda e: e.dma_start(out=dst, in_=src)

    def trb(out_psum_ap, in_ap):
        np_ = in_ap.shape[0]
        return lambda e: e.transpose(out_psum_ap, in_ap, identb[0:np_, 0:np_])

    def trf(out_psum_ap, in_ap):
        np_ = in_ap.shape[0]
        return lambda e: e.transpose(out_psum_ap, in_ap, identf[0:np_, 0:np_])

    def coll(kind, op, in_ap, out_ap, waits):
        return p.emit(G, lambda e: e.collective_compute(
            kind, op, replica_groups=RG, ins=[in_ap], outs=[out_ap]), waits=waits)

    def b0ap(tile):
        return tile[:].rearrange("p (s b) -> p s b", b=8)[:, :, 0]

    def bc(col_ap, n):
        return col_ap.broadcast_to((col_ap.shape[0], n))

    def dump(name, src_ap, waits, shape=None):
        t = dbg_t.get(name)
        if t is None:
            return None
        ev = p.emit(SY, dmaf(t[:, :] if t.ndim == 2 else t[:, :, :], src_ap), waits=waits, dma=True)
        dbg_ev.append(ev)
        return ev

    L0 = ExitStack()

    def sb0(name, shape, dt=F32):
        return L0.enter_context(nc.sbuf_tensor(name, list(shape), dt))

    xn_bf = [sb0(f"xn_bf{k}", [128, T], BF16) for k in range(DC)]
    m_b = sb0("m_b", [128, T])
    istd_b = sb0("istd_b", [128, T])
    var_b = sb0("var_b", [128, T])
    xg0 = [sb0(f"xg0_{k}", [128, S]) for k in range(DC)]
    alpha_b = sb0("alpha_b", [128, S])
    biasc = sb0("biasc", [128, 16])
    QW = ExitStack()
    qkvw_b = [QW.enter_context(nc.sbuf_tensor(f"qkvw{k}", [128, 1536], BF16)) for k in range(DC)]
    bc_ps = None

    bc_last = [None]

    def bcast_row(dst_tile, row_ap, n, waits, scale=None):
        evs = []
        nt = (n + 511) // 512
        for j in range(nt):
            w = min(512, n - 512 * j)
            ev_m = p.emit(TE, mm(bc_ps[:, 0:w], ones1[:, 0:128], row_ap[0:1, 512 * j:512 * j + w],
                                 True, True),
                          waits=list(waits) + [evs[-1] if evs else bc_last[0]])
            if scale is None:
                ev_c = p.emit(V, cp(dst_tile[:, 512 * j:512 * j + w], bc_ps[:, 0:w]), waits=[ev_m])
            else:
                ev_c = p.emit(V, ts(dst_tile[:, 512 * j:512 * j + w], bc_ps[:, 0:w], scale), waits=[ev_m])
            evs.append(ev_c)
        bc_last[0] = evs[-1]
        return evs[-1]

    ev_bias = p.emit(SY, [dmaf(biasc[:, 0:8], wv_b[:, :]), dmaf(biasc[:, 8:16], out_b[:, :])],
                     dma=True)
    ev_qkv_cast = []
    for k in range(DC):
        evl = p.emit(SY, dmaf(xring[0][:, 0:1536], qkv_w[128 * k:128 * (k + 1), :]),
                     waits=[ev_qkv_cast[-1]] if ev_qkv_cast else [], dma=True)
        evc = p.emit(V, cp(qkvw_b[k][:], xring[0][:, 0:1536]), waits=[evl])
        ev_qkv_cast.append(evc)
    ev_qkvw = ev_qkv_cast[-1]
    mw_w1st = {}
    mw_w2st = [None, None]
    cut("mw")

    RWD = ExitStack()
    rows = RWD.enter_context(nc.sbuf_tensor("rows", [1, 2 * T], F32))
    LNs = ExitStack()
    ps_sx = LNs.enter_context(nc.psum_tensor("ps_sx", [1, T], F32))
    ps_sq = LNs.enter_context(nc.psum_tensor("ps_sq", [1, T], F32))
    ev_x_ok = [[ev_qkvw], [None]]
    mm_sx, mm_sq = [], []
    for k in range(DC):
        r = k % 2
        evl = p.emit(SY, dmaf(xring[r][:], x_fm[128 * k:128 * (k + 1), :]),
                     waits=ev_x_ok[r], dma=True)
        evsq = p.emit(A, act(sq_s[:], xring[r][:], AF.Square),
                      waits=[evl] + ([mm_sq[-1]] if mm_sq else [ev_const]))

        def mk(dst, src, k=k):
            return [mm(dst[:, 512 * j:512 * (j + 1)], ones128[:, 0:1],
                       src[:, 512 * j:512 * (j + 1)], k == 0, k == DC - 1) for j in range(4)]

        m1 = p.emit(TE, mk(ps_sx, xring[r]), waits=[evl])
        mm_sx.append(m1)
        m2 = p.emit(TE, mk(ps_sq, sq_s), waits=[evsq])
        mm_sq.append(m2)
        ev_x_ok[r] = [m1, evsq]
    ev_mrow = p.emit(V, ts(rows[0:1, 0:T], ps_sx[:, :], 1.0 / D), waits=[mm_sx[-1]])
    ev_qrow = p.emit(V, ts(rows[0:1, T:2 * T], ps_sq[:, :], 1.0 / D), waits=[mm_sq[-1]])
    LNs.close()
    bc_ps = L0.enter_context(nc.psum_tensor("bc_ps", [128, 512], F32))
    ev_mb = bcast_row(m_b, rows[0:1, 0:T], T, [ev_mrow])
    ev_vb0 = bcast_row(var_b, rows[0:1, T:2 * T], T, [ev_qrow])
    RWD.close()
    cut("ln0")
    ev_m2 = p.emit(V, tt(sq_s[:], m_b[:], m_b[:], ALU.mult), waits=[ev_mb, ev_qrow])
    ev_var = p.emit(V, tt(var_b[:], var_b[:], sq_s[:], ALU.subtract), waits=[ev_m2, ev_vb0])
    ev_sd = p.emit(A, act(sq_s[:], var_b[:], AF.Sqrt, bias=cconst[:, 0:1]), waits=[ev_var])
    ev_istd = p.emit(V, recip(istd_b[:], sq_s[:]), waits=[ev_sd])

    PL = ExitStack()
    pool_t = [PL.enter_context(nc.sbuf_tensor(f"pool{j}", [128, S], F32)) for j in range(2)]
    ev_xn = []
    ev_x2_ok = [ev_istd, ev_istd]
    ev_pool_ok = [None, None]
    ev_pool_st = []
    for k in range(DC):
        r = k % 2
        evl = p.emit(SY, dmaf(xring[r][:], x_fm[128 * k:128 * (k + 1), :]),
                     waits=[ev_x2_ok[r]], dma=True)
        evs = p.emit(V, tt(sq_s[:], xring[r][:], m_b[:], ALU.subtract), waits=[evl, ev_istd])
        evn = p.emit(V, tt(xn_bf[k][:], sq_s[:], istd_b[:], ALU.mult), waits=[evs])
        ev_xn.append(evn)
        ev_x2_ok[r] = evn
        evp = p.emit(V, red(pool_t[r][:].rearrange("p (g b) -> p g b", b=8),
                            xn_bf[k][:].rearrange("p (g j b) -> p g b j", g=32, j=8),
                            ALU.add),
                     waits=[evn, ev_pool_ok[r]])
        evst = p.emit(SY, dmaf(ag_lm_in[128 * k:128 * (k + 1), :], pool_t[r][:]),
                      waits=[evp], dma=True)
        ev_pool_st.append(evst)
        ev_pool_ok[r] = evst
    ev_ag_lm = coll("AllGather", ALU.bypass, ag_lm_in[:, :], ag_lm_out[:, :, :],
                    list(ev_pool_st))
    PL.close()
    cut("aglm")
    dump("xn", xn_bf[0][:], [ev_xn[0]])

    NY = ExitStack()

    def sbn(name, shape, dt=F32):
        return NY.enter_context(nc.sbuf_tensor(name, list(shape), dt))

    q0 = [sbn(f"q0_{j}", [128, S], BF16) for j in range(4)]
    k_l0 = [sbn(f"kl0_{j}", [128, S], BF16) for j in range(4)]
    ql_c = sbn("ql_c", [64, T], BF16)
    kl_c = sbn("kl_c", [64, T], BF16)
    v0ext = [sbn(f"v0e{j}", [128, S + 32], BF16) for j in range(4)]
    res_fm = [sbn(f"res{j}", [128, S]) for j in range(4)]
    convtmp = sbn("convtmp", [128, S])
    taps = [sbn(f"taps{j}", [128, 33]) for j in range(4)]
    qh_w = [sbn(f"qhw{k}", [128, 128], BF16) for k in range(DC)]
    enc0b = [sbn(f"enc0b_{k}", [128, S], BF16) for k in range(DC)]
    halo_x = sbn("halo_x", [128, 256])
    halo_n = sbn("halo_n", [128, 256], BF16)
    kst = sbn("kst", [128, S], BF16)
    lmr0 = sbn("lmr0", [128, T], BF16)
    lmr = [lmr0, lmr0]
    a2e = [sbn(f"a2e_{j}", [128, 256]) for j in range(2)]
    a20b = [sbn(f"a20b_{j}", [128, 256], BF16) for j in range(2)]
    racc = sbn("racc", [128, 2])
    a2cols = sbn("a2cols", [128, 16])
    k0c = sbn("k0c", [64, T], BF16)
    v0c = sbn("v0c", [64, T], BF16)
    v0ctm = sbn("v0ctm", [128, 16 * 64], BF16)
    selb = sbn("selb", [128, 4 * 64], BF16)
    kvring = [lmr0, lmr0]
    at3e0 = sbn("at3e0", [128, T], BF16)
    att3e = [at3e0, at3e0]
    att3rc = sbn("at3rc", [128, 2])
    attn3T = sbn("at3T", [128, 16 * 256], BF16)
    avb = [sbn(f"av{j}", [128, 64], BF16) for j in range(2)]
    a2T = [sbn(f"a2T{j}", [128, 256], BF16) for j in range(2)]
    z_t = [sbn(f"z{j}", [128, 256], BF16) for j in range(2)]
    zT_t = [sbn(f"zT{j}", [128, 256], BF16) for j in range(2)]
    xz_t = [sbn(f"xz{j}", [128, 256], BF16) for j in range(2)]
    xzT_t = [sbn(f"xzT{j}", [128, 256], BF16) for j in range(2)]
    tm_t = [sbn(f"tm{j}", [128, 256], BF16) for j in range(2)]
    tm2_t = tm_t
    U_bf = [sbn(f"U_{j}", [128, 8 * 64], BF16) for j in range(2)]
    Uloc = [sbn(f"Ul{j}", [128, 64]) for j in range(2)]
    at1_0 = sbn("at1_0", [128, 8 * S], BF16)
    attn1 = [at1_0, at1_0]
    attn1T = [sbn(f"at1T_{j}", [128, 8 * S], BF16) for j in range(2)]
    xgp = [sbn(f"xgp{j}", [128, S], BF16) for j in range(4)]
    oww = [sbn(f"oww{j}", [128, 512], BF16) for j in range(2)]
    wqg_b = [sbn(f"wqg{k}", [128, 256], BF16) for k in range(DC)]
    wkg_b = [sbn(f"wkg{k}", [128, 256], BF16) for k in range(DC)]
    q_g = [sbn(f"qg{j}", [128, S]) for j in range(2)]
    kgr0 = sbn("kgr0", [128, S])
    kgring = [kgr0, kgr0]
    canr0 = sbn("canr0", [128, S])
    canring = [canr0, canr0]
    stage = misc
    arow = srow

    evl = p.emit(SY, [dmaf(taps[j][:], conv_taps[128 * j:128 * (j + 1), :]) for j in range(4)],
                 dma=True)
    ev_taps = evl
    ev_qhw = ev_qkvw
    for g2 in range(2):
        evl = p.emit(SY, [dmaf(stage[:, 128 * k:128 * (k + 1)],
                               qkvw_head[128 * (4 * g2 + k):128 * (4 * g2 + k + 1), :])
                          for k in range(4)], waits=[ev_qhw], dma=True)
        ev_qhw = p.emit(V, [cp(qh_w[4 * g2 + k][:], stage[:, 128 * k:128 * (k + 1)])
                            for k in range(4)], waits=[evl])
    evl = p.emit(SY, [dmaf(stage[:, 64 * j:64 * (j + 1)], sel_head[128 * j:128 * (j + 1), :])
                      for j in range(4)], waits=[ev_qhw], dma=True)
    ev_sel = p.emit(V, [cp(selb[:, 64 * j:64 * (j + 1)], stage[:, 64 * j:64 * (j + 1)])
                        for j in range(4)], waits=[evl])
    ev_outw = ev_sel
    ev_wg = ev_outw
    for k in range(DC):
        evl = p.emit(SY, [dmaf(stage[:, 0:256], wq_g[128 * k:128 * (k + 1), :]),
                          dmaf(stage[:, 256:512], wk_g[128 * k:128 * (k + 1), :])],
                     waits=[ev_wg], dma=True)
        ev_wg = p.emit(V, [cp(wqg_b[k][:], stage[:, 0:256]),
                           cp(wkg_b[k][:], stage[:, 256:512])], waits=[evl])

    HL = ExitStack()
    ps_h = HL.enter_context(nc.psum_tensor("ps_h", [1, 64], F32))
    ev_hl = p.emit(SY, [dmaf(halo_x[:, 32 * k:32 * (k + 1)], x_halo[128 * k:128 * (k + 1), :])
                        for k in range(DC)], waits=[ev_sel], dma=True)
    ev_hsq = p.emit(A, act(misc[:, 0:256], halo_x[:, :], AF.Square), waits=[ev_hl, ev_wg])
    ev_hmm = p.emit(TE, [mm(ps_h[:, 0:32], ones128[:, 0:1], halo_x[:, 32 * k:32 * (k + 1)],
                            k == 0, k == DC - 1) for k in range(DC)] +
                    [mm(ps_h[:, 32:64], ones128[:, 0:1], misc[:, 32 * k:32 * (k + 1)],
                        k == 0, k == DC - 1) for k in range(DC)],
                    waits=[ev_hsq, ev_const])
    ev_hm = p.emit(V, ts(srow[0:1, 0:32], ps_h[:, 0:32], 1.0 / D), waits=[ev_hmm])
    ev_hq = p.emit(V, ts(srow[0:1, 32:64], ps_h[:, 32:64], 1.0 / D), waits=[ev_hmm])
    HL.close()
    ev_hm2 = p.emit(V, tt(srow[0:1, 64:96], srow[0:1, 0:32], srow[0:1, 0:32], ALU.mult),
                    waits=[ev_hm])
    ev_hvar = p.emit(V, tt(srow[0:1, 32:64], srow[0:1, 32:64], srow[0:1, 64:96], ALU.subtract),
                     waits=[ev_hm2, ev_hq])
    ev_hsd = p.emit(A, act(srow[0:1, 64:96], srow[0:1, 32:64], AF.Sqrt, bias=cconst[0:1, 0:1]),
                    waits=[ev_hvar])
    ev_histd = p.emit(V, recip(srow[0:1, 32:64], srow[0:1, 64:96]), waits=[ev_hsd])
    ev_hmb = bcast_row(convtmp[:, 0:32], srow[0:1, 0:32], 32, [ev_hm])
    ev_hib = bcast_row(convtmp[:, 32:64], srow[0:1, 32:64], 32, [ev_histd])
    ev_hxn = []
    for k in range(DC):
        e1 = p.emit(V, tt(misc[:, 0:32], halo_x[:, 32 * k:32 * (k + 1)], convtmp[:, 0:32],
                          ALU.subtract), waits=[ev_hmb] + ([ev_hxn[-1]] if ev_hxn else []))
        e2 = p.emit(V, tt(halo_n[:, 32 * k:32 * (k + 1)], misc[:, 0:32], convtmp[:, 32:64],
                          ALU.mult), waits=[e1, ev_hib])
        ev_hxn.append(e2)

    QP = ExitStack()
    ps_q = QP.enter_context(nc.psum_tensor("ps_q", [128, S], F32))
    ps_k = QP.enter_context(nc.psum_tensor("ps_k", [128, S], F32))
    ps_v = QP.enter_context(nc.psum_tensor("ps_v", [128, S], F32))
    ps_vh = QP.enter_context(nc.psum_tensor("ps_vh", [128, 32], F32))
    ev_kv_st = []
    prev = [ev_qkvw, ev_xn[-1], ev_xn[-2]]
    for m2 in range(4):
        e_q = p.emit(TE, [mm(ps_q[:, :], qkvw_b[k][:, 128 * m2:128 * (m2 + 1)], b0ap(xn_bf[k]),
                             k == 0, k == DC - 1) for k in range(DC)], waits=prev)
        c_q = p.emit(A, act(q0[m2][:], ps_q[:, :], AF.Copy, scale=cconst[:, 1:2]), waits=[e_q])
        e_k = p.emit(TE, [mm(ps_k[:, :], qkvw_b[k][:, 512 + 128 * m2:512 + 128 * (m2 + 1)],
                             b0ap(xn_bf[k]), k == 0, k == DC - 1) for k in range(DC)], waits=[c_q])
        c_k = p.emit(A, act(kst[:, :], ps_k[:, :], AF.Copy),
                     waits=[e_k] + ([ev_kv_st[-2]] if len(ev_kv_st) >= 2 else []))
        s_k = p.emit(SY, dmaf(ag_kv_in[128 * m2:128 * (m2 + 1), :], kst[:, :]), waits=[c_k], dma=True)
        e_v = p.emit(TE, [mm(ps_v[:, :], qkvw_b[k][:, 1024 + 128 * m2:1024 + 128 * (m2 + 1)],
                             b0ap(xn_bf[k]), k == 0, k == DC - 1) for k in range(DC)], waits=[c_k])
        c_v = p.emit(A, act(v0ext[m2][:, 16:16 + S], ps_v[:, :], AF.Copy), waits=[e_v])
        e_vh = p.emit(TE, [mm(ps_vh[:, :], qkvw_b[k][:, 1024 + 128 * m2:1024 + 128 * (m2 + 1)],
                              halo_n[:, 32 * k:32 * (k + 1)], k == 0, k == DC - 1) for k in range(DC)],
                      waits=[ev_hxn[-1], c_v])
        c_vh = p.emit(A, [act(v0ext[m2][:, 0:16], ps_vh[:, 0:16], AF.Copy),
                          act(v0ext[m2][:, S + 16:S + 32], ps_vh[:, 16:32], AF.Copy)], waits=[e_vh])
        s_v = p.emit(SY, dmaf(ag_kv_in[512 + 128 * m2:512 + 128 * (m2 + 1), :],
                              v0ext[m2][:, 16:16 + S]), waits=[c_v], dma=True)
        ev_kv_st += [s_k, s_v]
        prev = [c_vh]
    ev_ag_kv = coll("AllGather", ALU.bypass, ag_kv_in[:, :], ag_kv_out[:, :, :],
                    list(ev_kv_st))
    QP.close()
    cut("qkv")

    ev_conv = []
    for m2 in range(4):
        evs = [p.emit(V, ts(res_fm[m2][:], v0ext[m2][:, 0:S], taps[m2][:, 0:1]),
                      waits=[ev_taps, prev[0]])]
        for j in range(1, 33):
            src, dst = (res_fm[m2], convtmp) if j % 2 == 1 else (convtmp, res_fm[m2])
            evs.append(p.emit(V, stt(dst[:], v0ext[m2][:, j:j + S], taps[m2][:, j:j + 1], src[:],
                                     ALU.mult, ALU.add), waits=[evs[-1]]))
        ev_conv.append(evs[-1])

    LM = ExitStack()
    ps_ql = LM.enter_context(nc.psum_tensor("ps_ql", [64, T], F32))
    ev_lm_mm = []
    ev_lml = []
    for k in range(DC):
        r = k % 2
        evl = p.emit(SY, dmaf(xring[r][:].rearrange("p (r s) -> p r s", r=8),
                              ag_lm_out[:, 128 * k:128 * (k + 1), :].transpose([1, 0, 2])),
                     waits=[ev_ag_lm] + ([ev_lml[-2]] if len(ev_lml) >= 2 else [ev_x2_ok[r]]),
                     dma=True)
        evc = p.emit(V, cp(lmr[r][:], xring[r][:]),
                     waits=[evl] + ([ev_lm_mm[-1]] if ev_lm_mm else []))
        ev_lml.append(evc)
        ev_m = p.emit(TE, [mm(ps_ql[:, 512 * j:512 * (j + 1)], qh_w[k][:, 0:64],
                              lmr[r][:, 512 * j:512 * (j + 1)], k == 0, k == DC - 1)
                           for j in range(4)],
                      waits=[evc, ev_qhw])
        ev_lm_mm.append(ev_m)
    ev_qlkl = p.emit(A, [act(ql_c[:, 512 * j:512 * (j + 1)], ps_ql[:, 512 * j:512 * (j + 1)],
                             AF.Copy, scale=cconst[0:64, 2:3]) for j in range(4)],
                     waits=[ev_lm_mm[-1]])
    LM.close()
    LM2 = ExitStack()
    ps_klc = LM2.enter_context(nc.psum_tensor("ps_klc", [64, T], F32))
    ps_kl0 = LM2.enter_context(nc.psum_tensor("ps_kl0", [128, 4 * S], F32))
    ev_kl0mm = []
    for k in range(DC):
        r = k % 2
        evl = p.emit(SY, dmaf(xring[r][:].rearrange("p (r s) -> p r s", r=8),
                              ag_lm_out[:, 128 * k:128 * (k + 1), :].transpose([1, 0, 2])),
                     waits=([ev_kl0mm[-2]] if len(ev_kl0mm) >= 2 else [ev_lm_mm[-1], ev_qlkl]),
                     dma=True)
        evc = p.emit(V, cp(lmr[r][:], xring[r][:]),
                     waits=[evl] + ([ev_kl0mm[-1]] if ev_kl0mm else []))
        ev_m = p.emit(TE, [mm(ps_klc[:, 512 * j:512 * (j + 1)], qh_w[k][:, 64:128],
                              lmr[r][:, 512 * j:512 * (j + 1)], k == 0, k == DC - 1)
                           for j in range(4)] +
                      [mm(ps_kl0[:, S * m2:S * (m2 + 1)],
                          qkvw_b[k][:, 512 + 128 * m2:512 + 128 * (m2 + 1)],
                          lmr[r][:].rearrange("p (r g b) -> p (r g) b", r=8, b=8)[:, :, 0],
                          k == 0 and m2 % 2 == 0, k == DC - 1 and m2 % 2 == 1)
                       for m2 in range(4)],
                      waits=[evc])
        ev_kl0mm.append(ev_m)
    ev_klc = p.emit(A, [act(kl_c[:, 512 * j:512 * (j + 1)], ps_klc[:, 512 * j:512 * (j + 1)],
                            AF.Copy, scale=cconst[0:64, 1:2]) for j in range(4)],
                    waits=[ev_kl0mm[-1]])
    ev_kl0 = p.emit(A, [act(k_l0[m2][:], ps_kl0[:, S * m2:S * (m2 + 1)], AF.Copy, scale=cconst[:, 1:2])
                        for m2 in range(4)], waits=[ev_kl0mm[-1], ev_klc])
    LM2.close()
    cut("conv_lm")
    dump("qlkl", ql_c[:], [ev_qlkl])
    dump("kl0", k_l0[0][:], [ev_kl0])

    A2S = ExitStack()
    ps_a2 = A2S.enter_context(nc.psum_tensor("ps_a2", [128, 512], F32))
    ps_cs = A2S.enter_context(nc.psum_tensor("ps_cs", [128, 16], F32))
    ev_a2 = []
    ev_racc = None
    ev_a20 = None

    def lmslice(which, b, m2=None):
        src = ql_c if which == "q" else kl_c
        v = src[:, :].rearrange("p (r g bb) -> p (r g) bb", r=8, bb=8)[:, :, b]
        if m2 is None:
            return v
        return v[:, 128 * m2:128 * (m2 + 1)]

    for b in range(B):
        e_mm = p.emit(TE, [mm(ps_a2[:, 256 * m2:256 * (m2 + 1)], lmslice("q", b, m2),
                              lmslice("k", b), m2 == 0, m2 == 1) for m2 in range(2)],
                      waits=[ev_qlkl, ev_klc] + ([ev_a2[-1]] if ev_a2 else []))
        e_nm = p.emit(V, [red(scol[:, 2 + m2:3 + m2], ps_a2[:, 256 * m2:256 * (m2 + 1)],
                              ALU.max, neg=True) for m2 in range(2)], waits=[e_mm])
        e_ex = p.emit(A, [act(a2e[m2][:], ps_a2[:, 256 * m2:256 * (m2 + 1)], AF.Exp,
                              bias=scol[:, 2 + m2:3 + m2]) for m2 in range(2)], waits=[e_nm])
        e_rs = p.emit(V, [red(scol[:, 4 + m2:5 + m2], a2e[m2][:], ALU.add) for m2 in range(2)],
                      waits=[e_ex])
        e_rc = p.emit(V, [recip(scol[:, 6 + m2:7 + m2], scol[:, 4 + m2:5 + m2]) for m2 in range(2)],
                      waits=[e_rs])
        e_ra = p.emit(V, [tt(scol[:, 8:9], scol[:, 4:5], scol[:, 6:7], ALU.mult),
                          tt(scol[:, 9:10], scol[:, 5:6], scol[:, 7:8], ALU.mult)], waits=[e_rc])
        if b == 0:
            ev_racc = p.emit(V, tt(racc[:, 0:1], scol[:, 8:9], scol[:, 9:10], ALU.max),
                             waits=[e_ra])
            ev_a20 = p.emit(V, [ts(a20b[m2][:], a2e[m2][:], scol[:, 6 + m2:7 + m2])
                                for m2 in range(2)], waits=[e_rc])
        else:
            e_u1 = p.emit(V, tt(scol[:, 10:11], scol[:, 8:9], scol[:, 9:10], ALU.max), waits=[e_ra])
            ev_racc = p.emit(V, tt(racc[:, 0:1], racc[:, 0:1], scol[:, 10:11], ALU.max),
                             waits=[e_u1, ev_racc])
        e_cs = p.emit(TE, [mm(ps_cs[:, 2 * b + m2:2 * b + m2 + 1],
                              a2e[kc][:, 128 * m2:128 * (m2 + 1)], scol[:, 6 + kc:7 + kc],
                              kc == 0, kc == 1)
                           for m2 in range(2) for kc in range(2)],
                      waits=[e_rc] + ([ev_a20] if b == 0 else []))
        ev_a2.append(e_cs)
    e_cse = p.emit(V, cp(a2cols[:, :], ps_cs[:, :]), waits=[ev_a2[-1]])
    e_cmax = p.emit(V, red(racc[:, 1:2], a2cols[:, :], ALU.max), waits=[e_cse])
    ps_tr = A2S.enter_context(nc.psum_tensor("ps_tr", [128, 128], F32))
    e_tr = p.emit(TE, trf(ps_tr[0:2, :], racc[:, 0:2]), waits=[ev_racc, e_cmax, ev_cld])
    e_tre = p.emit(V, cp(convtmp[0:2, 0:128], ps_tr[0:2, :]), waits=[e_tr, ev_conv[-1]])
    e_mx = p.emit(V, red(scol[0:2, 12:13], convtmp[0:2, 0:128], ALU.max), waits=[e_tre])
    e_st2 = p.emit(G, dmaf(ar2_in[:, :], scol[0:2, 12:13].rearrange("p one -> one p") if False
                           else scol[0:2, 12:13]), waits=[e_mx], dma=True)
    ev_ar2 = coll("AllReduce", ALU.max, ar2_in[:, :], ar2_out[:, :], [e_st2])
    e_ld2 = p.emit(G, dmaf(srow[0:1, 100:102], ar2_out[:, :]), waits=[ev_ar2], dma=True)
    e_sc = p.emit(V, tt(srow[0:1, 102:103], srow[0:1, 100:101], srow[0:1, 101:102], ALU.mult),
                  waits=[e_ld2])
    e_rs2 = p.emit(V, recip(srow[0:1, 103:104], srow[0:1, 102:103]), waits=[e_sc])
    e_mmb = p.emit(TE, mm(ps_tr[:, 0:1], ones1[:, 0:128], srow[0:1, 103:104], True, True),
                   waits=[e_rs2])
    ev_rsc = p.emit(V, cp(scol[:, 11:12], ps_tr[:, 0:1]), waits=[e_mmb])
    A2S.close()
    cut("attn2")
    dump("a20", a20b[0][:], [ev_a20])

    SL = ExitStack()
    ps_sl = SL.enter_context(nc.psum_tensor("ps_sl", [64, T], F32))
    for ti, dst in ((0, k0c), (1, v0c)):
        ev_mm_l = []
        for kc in range(4):
            r = kc % 2
            evl = p.emit(SY, dmaf(kvring[r][:].rearrange("p (r s) -> p r s", r=8),
                                  ag_kv_out[:, 512 * ti + 128 * kc:512 * ti + 128 * (kc + 1), :]
                                  .transpose([1, 0, 2])),
                         waits=[ev_ag_kv, ev_kl0mm[-1]] + ([ev_mm_l[-1]] if ev_mm_l else
                                             ([ev_k0c] if ti == 1 else [])), dma=True)
            e_m = p.emit(TE, [mm(ps_sl[:, 512 * j:512 * (j + 1)], selb[:, 64 * kc:64 * (kc + 1)],
                                 kvring[r][:, 512 * j:512 * (j + 1)], kc == 0, kc == 3)
                              for j in range(4)],
                         waits=[evl, ev_sel, ev_rsc, ev_a20])
            ev_mm_l.append(e_m)
        ev_slc = p.emit(A, [act(dst[:, 512 * j:512 * (j + 1)], ps_sl[:, 512 * j:512 * (j + 1)],
                                AF.Copy) for j in range(4)], waits=[ev_mm_l[-1]])
        if ti == 0:
            ev_k0c = ev_slc
        else:
            ev_v0c = ev_slc
    SL.close()

    VT = ExitStack()
    ps_vt = VT.enter_context(nc.psum_tensor("ps_vt", [128, 1024], BF16))
    ev_v0tm = []
    for tgrp in range(2):
        e_t = p.emit(TE, [trb(ps_vt[:, 128 * j:128 * j + 64], v0c[:, 128 * (8 * tgrp + j):128 * (8 * tgrp + j + 1)])
                          for j in range(8)],
                     waits=[ev_v0c] + ([ev_v0tm[-1]] if ev_v0tm else []))
        e_e = p.emit(A, act(v0ctm[:, 512 * tgrp:512 * (tgrp + 1)],
                            ps_vt[:, :].rearrange("p (j c) -> p j c", j=8)[:, :, 0:64], AF.Copy),
                     waits=[e_t])
        ev_v0tm.append(e_e)
    VT.close()
    cut("slices")

    A3S = ExitStack()
    ps_a3 = A3S.enter_context(nc.psum_tensor("ps_a3", [128, T], F32))
    ev_at3 = []
    A3T = ExitStack()
    ps_3t = A3T.enter_context(nc.psum_tensor("ps_3t", [128, 1024], BF16))
    ev_3t = []
    for m2 in range(2):
        e_m = p.emit(TE, [mm(ps_a3[:, 512 * j:512 * (j + 1)], lmslice("q", 0, m2),
                             k0c[:, 512 * j:512 * (j + 1)], True, True) for j in range(4)],
                     waits=[ev_k0c, ev_qlkl, ev_v0tm[-1]] + ([ev_at3[-1]] if ev_at3 else []))
        e_nm = p.emit(V, red(scol[:, 2 + m2:3 + m2], ps_a3[:, :], ALU.max, neg=True), waits=[e_m])
        e_ex = p.emit(A, act(att3e[m2][:], ps_a3[:, :], AF.Exp, bias=scol[:, 2 + m2:3 + m2]),
                      waits=[e_nm] + ([ev_3t[-1]] if m2 == 1 else []))
        e_rs = p.emit(V, red(scol[:, 4 + m2:5 + m2], att3e[m2][:], ALU.add), waits=[e_ex])
        e_rc = p.emit(V, recip(att3rc[:, m2:m2 + 1], scol[:, 4 + m2:5 + m2]), waits=[e_rs])
        ev_at3.append(e_rc)
        for tq in range(2):
            fns = []
            for tj in range(8):
                t = 8 * tq + tj
                fns.append(trb(ps_3t[:, 128 * tj:128 * (tj + 1)],
                               att3e[m2][:, 128 * t:128 * (t + 1)]))
            e_t = p.emit(TE, fns, waits=[e_rc] + ([ev_3t[-1]] if ev_3t else []))
            e_e = p.emit(A, [act(attn3T[:, 256 * (8 * tq + tj) + 128 * m2:
                                        256 * (8 * tq + tj) + 128 * (m2 + 1)],
                                 ps_3t[:, 128 * tj:128 * (tj + 1)], AF.Copy)
                             for tj in range(8)],
                         waits=[e_t])
            ev_3t.append(e_e)
    A3T.close()
    AVS = ExitStack()
    ps_av = AVS.enter_context(nc.psum_tensor("ps_av", [128, 128], F32))
    ev_av = []
    for m2 in range(2):
        e_m = p.emit(TE, [mm(ps_av[:, 64 * m2:64 * (m2 + 1)],
                             attn3T[:, 256 * t + 128 * m2:256 * t + 128 * (m2 + 1)],
                             v0ctm[:, 64 * t:64 * (t + 1)], t == 0, t == 15) for t in range(16)],
                     waits=[ev_3t[-1], ev_v0tm[-1]] + ([ev_av[-1]] if ev_av else []))
        e_e = p.emit(A, act(avb[m2][:], ps_av[:, 64 * m2:64 * (m2 + 1)], AF.Copy,
                            scale=att3rc[:, m2:m2 + 1]), waits=[e_m, ev_at3[-1]])
        ev_av.append(e_e)
    AVS.close()
    A3S.close()
    cut("attn3")

    PV = ExitStack()
    ps_p1 = PV.enter_context(nc.psum_tensor("ps_p1", [128, 512], F32))
    ps_pt = PV.enter_context(nc.psum_tensor("ps_pt", [128, 1024], BF16))

    def tr2(dst_tiles, src_tiles, waits):
        fns = []
        for i in range(2):
            for j in range(2):
                fns.append(trb(ps_pt[:, 256 * i + 128 * j:256 * i + 128 * (j + 1)],
                               src_tiles[j][:, 128 * i:128 * (i + 1)]))
        e_t = p.emit(TE, fns, waits=waits)
        e_e = p.emit(A, [act(dst_tiles[i][:], ps_pt[:, 256 * i:256 * (i + 1)], AF.Copy)
                         for i in range(2)], waits=[e_t])
        return e_e

    ev_a2T = tr2(a2T, a20b, [ev_a20, ev_cld])
    ev_z = p.emit(V, [ts(z_t[i][:], a2T[i][:], scol[:, 11:12]) for i in range(2)],
                  waits=[ev_a2T, ev_rsc])
    ev_zT = p.emit(V, [ts(zT_t[i][:], a20b[i][:], scol[:, 11:12]) for i in range(2)],
                   waits=[ev_a20, ev_rsc])
    cut("pv_z")

    def mm256(ps, lhsT_tiles, rhs_tiles, waits):
        fns = []
        for m2 in range(2):
            for kc in range(2):
                fns.append(mm(ps[:, 256 * m2:256 * (m2 + 1)], lhsT_tiles[kc][:, 128 * m2:128 * (m2 + 1)],
                              rhs_tiles[kc][:], kc == 0, kc == 1))
        return p.emit(TE, fns, waits=waits)

    for it in range(6):
        e1 = mm256(ps_p1, a2T, z_t, [ev_z, ev_a2T])
        if it == 0:
            cut("pv_m")
        e_xz = p.emit(A, [act(xz_t[i][:], ps_p1[:, 256 * i:256 * (i + 1)], AF.Copy)
                          for i in range(2)], waits=[e1])
        if it == 0:
            cut("pv_x")
        e_t1 = p.emit(V, [stt(tm_t[i][:], ps_p1[:, 256 * i:256 * (i + 1)], 1.0 / 7.0,
                              diagI[:, 256 * i:256 * (i + 1)], ALU.mult, ALU.subtract)
                          for i in range(2)], waits=[e1, e_xz])
        if it == 0:
            cut("pv_a")
        e_xzT = tr2(xzT_t, xz_t, [e_xz])
        if it == 0:
            cut("pv_b")
        e2 = mm256(ps_p1, xzT_t, tm_t, [e_xzT, e_t1])
        e_t3 = p.emit(V, [stt(tm2_t[i][:], ps_p1[:, 256 * i:256 * (i + 1)], -7.0 / 15.0,
                              diagI[:, 256 * i:256 * (i + 1)], ALU.mult, ALU.subtract)
                          for i in range(2)], waits=[e2])
        if it == 0:
            cut("pv_c")
        e3 = mm256(ps_p1, xzT_t, tm2_t, [e_t3])
        e_t5 = p.emit(V, [stt(tm_t[i][:], ps_p1[:, 256 * i:256 * (i + 1)], -15.0 / 13.0,
                              diagI[:, 256 * i:256 * (i + 1)], ALU.mult, ALU.subtract)
                          for i in range(2)], waits=[e3])
        e4 = mm256(ps_p1, zT_t, tm_t, [e_t5, ev_zT])
        if it == 0:
            cut("pv_d")
        ev_z = p.emit(A, [act(z_t[i][:], ps_p1[:, 256 * i:256 * (i + 1)], AF.Copy, scale=cconst[:, 3:4])
                          for i in range(2)], waits=[e4])
        if it == 0:
            cut("pv_e")
        ev_zT = tr2(zT_t, z_t, [ev_z])
        if it == 0:
            cut("pv_it0")
    ev_u_mm = p.emit(TE, [mm(ps_p1[:, 64 * m2:64 * (m2 + 1)], zT_t[kc][:, 128 * m2:128 * (m2 + 1)],
                             avb[kc][:], kc == 0, kc == 1)
                          for m2 in range(2) for kc in range(2)],
                     waits=[ev_zT, ev_av[-1]])
    ev_u_e = p.emit(V, [cp(Uloc[m2][:], ps_p1[:, 64 * m2:64 * (m2 + 1)]) for m2 in range(2)],
                    waits=[ev_u_mm])
    ev_u_st = p.emit(SY, [dmaf(ag_u_in[128 * m2:128 * (m2 + 1), :], Uloc[m2][:]) for m2 in range(2)],
                     waits=[ev_u_e], dma=True)
    cut("pv_u")
    ev_ag_u = coll("AllGather", ALU.bypass, ag_u_in[:, :], ag_u_out[:, :, :], [ev_u_st])
    PV.close()
    cut("pinv")
    dump("av", avb[0][:], [ev_av[-1]])
    dump("U", Uloc[0][:], [ev_u_e])

    A1S = ExitStack()
    ps_a1 = A1S.enter_context(nc.psum_tensor("ps_a1", [128, T], F32))
    a1max = A1S.enter_context(nc.sbuf_tensor("a1max", [128, 8], F32))
    a1sum = A1S.enter_context(nc.sbuf_tensor("a1sum", [128, 8], F32))
    ev_a1 = []
    for mt in range(2):
        e_m0 = p.emit(TE, [mm(ps_a1[:, 256 * h:256 * (h + 1)],
                              q0[h // 2][0:64, 128 * mt:128 * (mt + 1)],
                              k_l0[h // 2][0:64, :],
                              True, True) for h in (0, 2, 4, 6)],
                      waits=[ev_kl0, ev_u_e] + ([ev_a1[-1]] if ev_a1 else []))
        e_m = p.emit(TE, [mm(ps_a1[:, 256 * h:256 * (h + 1)],
                             q0[h // 2][64:128, 128 * mt:128 * (mt + 1)],
                             k_l0[h // 2][64:128, :],
                             True, True) for h in (1, 3, 5, 7)],
                     waits=[e_m0])
        if mt == 0:
            cut("a1_m")
        e_nm = p.emit(V, red(a1max[:, :], ps_a1[:, :].rearrange("p (h l) -> p h l", h=8),
                             ALU.max, neg=True), waits=[e_m])
        if mt == 0:
            cut("a1_r")
        e_sub = p.emit(V, tt(sq_s[:], ps_a1[:, :],
                             a1max[:, :].unsqueeze(2).broadcast_to((128, 8, 256)).rearrange("p h l -> p (h l)")
                             if False else a1max[:, :], ALU.add), waits=[e_nm]) if False else None
        e_sub = p.emit(V, tt(sq_s[:].rearrange("p (h l) -> p h l", h=8),
                             ps_a1[:, :].rearrange("p (h l) -> p h l", h=8),
                             a1max[:].unsqueeze(2).broadcast_to([128, 8, 256]),
                             ALU.add), waits=[e_nm])
        if mt == 0:
            cut("a1_s")
        e_ex = p.emit(A, act(attn1[mt][:], sq_s[:], AF.Exp),
                      waits=[e_sub] + ([ev_1t[-1]] if mt == 1 else []))
        if mt == 0:
            cut("a1_e")
        e_rs = p.emit(V, red(a1sum[:, :], attn1[mt][:].rearrange("p (h l) -> p h l", h=8),
                             ALU.add), waits=[e_ex])
        e_rc = p.emit(V, recip(a1sum[:, :], a1sum[:, :]), waits=[e_rs])
        e_n = p.emit(V, tt(attn1[mt][:].rearrange("p (h l) -> p h l", h=8),
                           attn1[mt][:].rearrange("p (h l) -> p h l", h=8),
                           a1sum[:].unsqueeze(2).broadcast_to([128, 8, 256]), ALU.mult),
                     waits=[e_rc])
        if mt == 0:
            cut("a1_n")
        ev_a1.append(e_n)
        if mt == 0:
            A1T = ExitStack()
            ps_1t = A1T.enter_context(nc.psum_tensor("ps_1t", [128, 1024], BF16))
            ev_1t = []
        for grp in range(2):
            fns = []
            hs = range(grp * 4, grp * 4 + 4)
            for hi, h in enumerate(hs):
                for j in range(2):
                    fns.append(trb(ps_1t[:, 256 * hi + 128 * j:256 * hi + 128 * (j + 1)],
                                   attn1[mt][:, 256 * h + 128 * j:256 * h + 128 * (j + 1)]))
            e_t = p.emit(TE, fns, waits=[e_n] + ([ev_1t[-1]] if ev_1t else []))
            e_e = p.emit(A, [act(attn1T[j][:, 256 * h + 128 * mt:256 * h + 128 * (mt + 1)],
                                 ps_1t[:, 256 * hi + 128 * j:256 * hi + 128 * (j + 1)],
                                 AF.Copy)
                             for hi, h in enumerate(hs) for j in range(2)],
                         waits=[e_t])
            ev_1t.append(e_e)
    A1T.close()
    A1S.close()
    cut("attn1")

    UL = ExitStack()
    ev_ul = p.emit(SY, [dmaf(xring[0][:, 0:512].rearrange("p (h c) -> p h c", h=8),
                             ag_u_out[:, 0:128, :].transpose([1, 0, 2])),
                        dmaf(xring[0][:, 512:1024].rearrange("p (h c) -> p h c", h=8),
                             ag_u_out[:, 128:256, :].transpose([1, 0, 2]))],
                   waits=[ev_ag_u], dma=True)
    ev_ub = p.emit(V, [cp(U_bf[0][:], xring[0][:, 0:512]), cp(U_bf[1][:], xring[0][:, 512:1024])],
                   waits=[ev_ul])
    UL.close()

    CH = ExitStack()
    ps_ch = CH.enter_context(nc.psum_tensor("ps_ch", [128, 1024], F32))
    fns = []
    for h in range(8):
        for kc in range(2):
            fns.append(mm(ps_ch[64 * (h % 2):64 * (h % 2) + 64, 256 * (h // 2):256 * (h // 2 + 1)],
                          U_bf[kc][:, 64 * h:64 * (h + 1)],
                          attn1T[kc][:, 256 * h:256 * (h + 1)],
                          (h in (0, 1) and kc == 0) if False else (h % 2 == 0 and kc == 0) if False else (h < 2 and kc == 0),
                          h >= 6 and kc == 1))
    fns = []
    for pair in range(4):
        for h in (2 * pair, 2 * pair + 1):
            for kc in range(2):
                first = (h == 2 * pair) and kc == 0
                fns.append(mm(ps_ch[64 * (h % 2):64 * (h % 2) + 64, 256 * (h // 2):256 * (h // 2 + 1)],
                              U_bf[kc][:, 64 * h:64 * (h + 1)],
                              attn1T[kc][:, 256 * h:256 * (h + 1)],
                              first and (pair % 2 == 0 or True) and (h % 2 == 0),
                              (h % 2 == 1) and kc == 1))
    def ch_fns(hs):
        return [mm(ps_ch[64 * (h % 2):64 * (h % 2) + 64,
                         256 * (h // 2):256 * (h // 2 + 1)],
                   U_bf[kc][:, 64 * h:64 * (h + 1)],
                   attn1T[kc][:, 256 * h:256 * (h + 1)], kc == 0, kc == 1)
                for h in hs for kc in range(2)]
    ev_ch0 = p.emit(TE, ch_fns((0, 2, 4, 6)), waits=[ev_ub, ev_1t[-1]])
    ev_ch = p.emit(TE, ch_fns((1, 3, 5, 7)), waits=[ev_ch0])
    ev_xgp = p.emit(V, [tt(xgp[j][:], ps_ch[:, 256 * j:256 * (j + 1)], res_fm[j][:], ALU.add)
                        for j in range(4)], waits=[ev_ch, ev_conv[-1]])
    CH.close()
    cut("chain")
    dump("xgp", xgp[0][:], [ev_xgp])

    OP = ExitStack()
    ps_op = OP.enter_context(nc.psum_tensor("ps_op", [128, 512], F32))
    ev_xg0 = []
    ev_oww = [ev_outw, ev_outw]
    ev_stg = None
    for m in range(DC):
        r = m % 2
        evl = p.emit(SY, dmaf(stage[:, 0:512].rearrange("p (k c) -> p k c", k=4),
                              out_w[:, 128 * m:128 * (m + 1)]
                              .rearrange("(k p) c -> k p c", p=128).transpose([1, 0, 2])),
                     waits=[ev_oww[r], ev_stg], dma=True)
        evc = p.emit(V, cp(oww[r][:], stage[:, 0:512]), waits=[evl])
        ev_stg = evc
        e_m = p.emit(TE, [mm(ps_op[:, 0:S], oww[r][:, 128 * kc:128 * (kc + 1)], xgp[kc][:],
                             kc == 0, kc == 3) for kc in range(4)],
                     waits=[evc, ev_xgp] + ([ev_xg0[-1]] if ev_xg0 else []))
        e_e = p.emit(V, ts(xg0[m][:], ps_op[:, 0:S], biasc[:, 8 + m:9 + m], None, ALU.add),
                     waits=[e_m, ev_bias])
        ev_xg0.append(e_e)
        ev_oww[r] = e_m
    OP.close()
    cut("outproj")
    dump("xg0", xg0[0][:], [ev_xg0[0]])

    ev_enc = []
    ev_xr = [ev_x2_ok[0], ev_x2_ok[1]]
    for k in range(DC):
        r = k % 2
        evl = p.emit(SY, dmaf(xring[r][:], x_fm[128 * k:128 * (k + 1), :]),
                     waits=[ev_xr[r]] if not ev_enc else [ev_xr[r]], dma=True)
        e1 = p.emit(V, tt(misc[:, 0:S], b0ap(xring[r]), b0ap(m_b), ALU.subtract),
                    waits=[evl, ev_mb, ev_hxn[-1], c_vh] + ([ev_enc[-1]] if ev_enc else []))
        e2 = p.emit(V, tt(misc[:, 0:S], misc[:, 0:S], b0ap(istd_b), ALU.mult), waits=[e1])
        e3 = p.emit(V, tt(misc[:, 256:256 + S], misc[:, 0:S], xg0[k][:], ALU.add),
                    waits=[e2, ev_xg0[k]])
        e4 = p.emit(V, cp(enc0b[k][:], misc[:, 256:256 + S]), waits=[e3])
        ev_enc.append(e4)
        ev_xr[r] = e1
    dump("enc0", enc0b[0][:], [ev_enc[0]])

    GK = ExitStack()
    ps_g1 = GK.enter_context(nc.psum_tensor("ps_g1", [128, 512], F32))
    ev_qg = []
    for m2 in range(2):
        e_m = p.emit(TE, [mm(ps_g1[:, 0:S], wqg_b[k][:, 128 * m2:128 * (m2 + 1)], enc0b[k][:],
                             k == 0, k == DC - 1) for k in range(DC)],
                     waits=[ev_enc[-1], ev_wg] + ([ev_qg[-1]] if ev_qg else []))
        e_e = p.emit(V, cp(q_g[m2][:], ps_g1[:, 0:S]), waits=[e_m])
        ev_qg.append(e_e)
    ev_kg_st = []
    for mt in range(2):
        e_m = p.emit(TE, [mm(ps_g1[:, 0:S], enc0b[k][:, 128 * mt:128 * (mt + 1)], wkg_b[k][:],
                             k == 0, k == DC - 1) for k in range(DC)],
                     waits=[ev_qg[-1]] + ([ev_kg_st[-1]] if ev_kg_st else []))
        e_e = p.emit(V, cp(convtmp[:, 0:S], ps_g1[:, 0:S]), waits=[e_m])
        e_s = p.emit(SY, dmaf(ag_kg_in[128 * mt:128 * (mt + 1), :], convtmp[:, 0:S]),
                     waits=[e_e], dma=True)
        ev_kg_st.append(e_s)
    ev_ag_kg = coll("AllGather", ALU.bypass, ag_kg_in[:, :], ag_kg_out[:, :, :], list(ev_kg_st))
    GK.close()

    GM = ExitStack()
    ps_M = GM.enter_context(nc.psum_tensor("ps_M", [128, 512], F32))
    ps_A = GM.enter_context(nc.psum_tensor("ps_A", [1, 256], F32))
    ev_gmm = []
    ci = 0
    for jc in range(16):
        r = jc % 2
        evk = p.emit(SY, dmaf(kgring[r][:], ag_kg_out[jc // 2, 128 * (jc % 2):128 * (jc % 2) + 128, :]),
                     waits=[ev_ag_kg] + ([ev_gmm[-1]] if ev_gmm else []), dma=True)
        for d in range(Dr):
            rc2 = ci % 2
            ci += 1
            evc = p.emit(SY, dmaf(canring[rc2][:], wt_can[d][128 * jc:128 * (jc + 1), :]),
                         waits=([ev_gmm[-1]] if ev_gmm else []), dma=True)
            e_m = p.emit(TE, [mm(ps_M[:, 256 * m2:256 * (m2 + 1)], kgring[r][:, 128 * m2:128 * (m2 + 1)],
                                 canring[rc2][:],
                                 jc == 0 and d == 0 and m2 == 0,
                                 jc == 15 and d == Dr - 1 and m2 == 1)
                              for m2 in range(2)],
                         waits=[evk, evc])
            ev_gmm.append(e_m)
    e_pr = p.emit(V, [tt(misc[:, 256 * m2:256 * (m2 + 1)], ps_M[:, 256 * m2:256 * (m2 + 1)], q_g[m2][:], ALU.mult)
                      for m2 in range(2)], waits=[ev_gmm[-1], ev_qg[-1], ev_enc[-1]])
    e_am = p.emit(TE, [mm(ps_A[:, :], ones128[:, 0:1], misc[:, 256 * m2:256 * (m2 + 1)], m2 == 0, m2 == 1)
                       for m2 in range(2)], waits=[e_pr])
    e_ar = p.emit(V, ts(arow[0:1, 256:512], ps_A[:, :], 1.0 / 16.0), waits=[e_am])
    GM.close()
    cut("graph")
    e_amx = p.emit(V, red(srow[0:1, 104:105], arow[0:1, 256:512], ALU.max), waits=[e_ar])
    e_sta = p.emit(G, dmaf(ar_a_in[:, :], srow[0:1, 104:105]), waits=[e_amx], dma=True)
    ev_ar_a = coll("AllReduce", ALU.max, ar_a_in[:, :], ar_a_out[:, :], [e_sta])
    e_lda = p.emit(G, dmaf(srow[0:1, 105:106], ar_a_out[:, :]), waits=[ev_ar_a], dma=True)
    e_ng = p.emit(V, ts(srow[0:1, 106:107], srow[0:1, 105:106], -1.0), waits=[e_lda])
    e_ea = p.emit(A, act(arow[0:1, 256:512], arow[0:1, 256:512], AF.Exp, bias=srow[0:1, 106:107]),
                  waits=[e_ng, e_amx])
    e_asm = p.emit(V, red(srow[0:1, 107:108], arow[0:1, 256:512], ALU.add), waits=[e_ea])
    e_sts = p.emit(G, dmaf(ar_s_in[:, :], srow[0:1, 107:108]), waits=[e_asm], dma=True)
    ev_ar_s = coll("AllReduce", ALU.add, ar_s_in[:, :], ar_s_out[:, :], [e_sts])
    e_lds = p.emit(G, dmaf(srow[0:1, 108:109], ar_s_out[:, :]), waits=[ev_ar_s], dma=True)
    e_rss = p.emit(V, recip(srow[0:1, 109:110], srow[0:1, 108:109]), waits=[e_lds])
    e_alr = p.emit(V, ts(arow[0:1, 256:512], arow[0:1, 256:512], srow[0:1, 109:110]), waits=[e_rss])
    ev_alpha = bcast_row(alpha_b, arow[0:1, 256:512], S, [e_alr])
    dump("alpha", alpha_b[:], [ev_alpha])
    NY.close()
    QW.close()
    cut("ny")

    MW = ExitStack()
    mwst = [MW.enter_context(nc.sbuf_tensor(f"mwst{j}", [128, 1024], F32)) for j in range(2)]
    wst2 = [MW.enter_context(nc.sbuf_tensor(f"wst2{j}", [128, 1024], BF16)) for j in range(2)]
    mw_free = [e_pr, e_pr]
    mw_stok = [None, None]
    mws = 0
    for e_i in range(8):
        for h in range(4):
            r = mws % 2
            evl = p.emit(G, dmaf(mwst[r][:, 0:1024].rearrange("p (k c) -> p k c", k=2),
                                 moe_w1[e_i, :, :].rearrange("(k p) c -> p k c", p=128)
                                 [:, 2 * h:2 * h + 2, :]),
                         waits=[mw_free[r]], dma=True)
            evc = p.emit(V, cp(wst2[r][:, 0:1024], mwst[r][:, 0:1024]),
                         waits=[evl, mw_stok[r]])
            evs = p.emit(G, [dmaf(w1bf_dram[e_i, 2 * h + k, :, :],
                                  wst2[r][:, 512 * k:512 * (k + 1)]) for k in range(2)],
                         waits=[evc], dma=True)
            mw_free[r], mw_stok[r] = evc, evs
            mw_w1st.setdefault(e_i, []).append(evs)
            mws += 1
    for e_i in range(8):
        for kc in range(4):
            r = mws % 2
            evl = p.emit(G, dmaf(mwst[r][:, 0:1024], moe_w2[e_i, 128 * kc:128 * (kc + 1), :]),
                         waits=[mw_free[r]], dma=True)
            evc = p.emit(V, cp(wst2[r][:, 0:1024], mwst[r][:, 0:1024]),
                         waits=[evl, mw_stok[r]])
            evs = p.emit(G, [dmaf(w2bf_dram[m, e_i, kc, :, :],
                                  wst2[r][:, 128 * m:128 * m + 128]) for m in range(8)],
                         waits=[evc], dma=True)
            mw_free[r], mw_stok[r] = evc, evs
            mw_w2st[r] = evs
            mws += 1

    RT = ExitStack()
    cnT = [RT.enter_context(nc.sbuf_tensor(f"cnT{k}", [128, 8], BF16)) for k in range(DC)]
    rtT = [RT.enter_context(nc.sbuf_tensor(f"rtT{k}", [128, 8], BF16)) for k in range(DC)]
    logit = RT.enter_context(nc.sbuf_tensor("logit", [8, T], F32))
    gat_tm = RT.enter_context(nc.sbuf_tensor("gat_tm", [128, 128], F32))
    gat_sc = RT.enter_context(nc.sbuf_tensor("gat_sc", [128, 128], F32))
    g16 = RT.enter_context(nc.sbuf_tensor("g16", [128, 16], F32))
    ninv = RT.enter_context(nc.sbuf_tensor("ninv", [8, T], F32))
    gates_fm = RT.enter_context(nc.sbuf_tensor("gates_fm", [8, T], BF16))
    RTP = ExitStack()
    ps_r = RTP.enter_context(nc.psum_tensor("ps_r", [128, 512], F32))
    ps_r2 = RTP.enter_context(nc.psum_tensor("ps_r2", [8, T], F32))

    e_cl = p.emit(SY, dmaf(sq_s[0:8, 1024:2048], cent[:, :]), waits=[ev_alpha], dma=True)
    e_cs = p.emit(A, act(sq_s[0:8, 0:1024], sq_s[0:8, 1024:2048],
                         AF.Square), waits=[e_cl])
    e_cr = p.emit(V, red(scol[0:8, 13:14], sq_s[0:8, 0:1024], ALU.add), waits=[e_cs])
    e_cq = p.emit(A, act(scol[0:8, 14:15], scol[0:8, 13:14], AF.Sqrt), waits=[e_cr])
    e_ci = p.emit(V, recip(scol[0:8, 14:15], scol[0:8, 14:15]), waits=[e_cq])
    e_ct = p.emit(TE, trf(ps_r[0:8, 0:8] if False else ps_r[0:128, 0:8],
                          scol[:, 14:15].broadcast_to([128, 8]) if False else scol[:, 14:15]),
                  waits=[e_ci]) if False else None
    e_ct = p.emit(TE, trf(ps_r[0:1, 0:128], scol[:, 14:15]), waits=[e_ci, ev_cld])
    e_cre = p.emit(V, cp(srow[0:1, 110:118], ps_r[0:1, 0:8]), waits=[e_ct])
    e_crb = p.emit(TE, mm(ps_r[:, 0:8], ones1[:, 0:128], srow[0:1, 110:118], True, True),
                   waits=[e_cre])
    e_crc = p.emit(V, cp(misc[:, 256:264], ps_r[:, 0:8]), waits=[e_crb])
    ev_ctl = []
    for k in range(DC):
        evl = p.emit(SY, [dmaf(misc[:, 264:272], cent_T[128 * k:128 * (k + 1), :]),
                          dmaf(misc[:, 272:280], route_T[128 * k:128 * (k + 1), :])],
                     waits=[ev_ctl[-1]] if ev_ctl else [e_crc], dma=True)
        e_c1 = p.emit(V, tt(cnT[k][:], misc[:, 264:272], misc[:, 256:264], ALU.mult), waits=[evl])
        e_c2 = p.emit(V, cp(rtT[k][:], misc[:, 272:280]), waits=[evl, e_c1])
        ev_ctl.append(e_c2)
    ev_ct_ok = ev_ctl[-1]
    e_n1 = p.emit(V, ts(sq_s[0:8, 0:T], var_b[0:8, :], float(D)), waits=[ev_var])
    e_n2 = p.emit(V, recip(sq_s[0:8, 0:T], sq_s[0:8, 0:T]), waits=[e_n1])
    e_n3 = p.emit(V, ts(ninv[:, :], var_b[0:8, :], 1.0, EPS_LN, ALU.mult, ALU.add), waits=[e_n1])
    e_n4 = p.emit(V, tt(ninv[:, :], ninv[:, :], sq_s[0:8, 0:T], ALU.mult), waits=[e_n3, e_n2])
    e_n5 = p.emit(A, act(ninv[:, :], ninv[:, :], AF.Sqrt), waits=[e_n4])
    ev_lg = []
    for nt in range(4):
        e_m1 = p.emit(TE, [mm(ps_r2[:, 512 * nt:512 * (nt + 1)], cnT[k][:, :],
                              xn_bf[k][:, 512 * nt:512 * (nt + 1)], k == 0, k == DC - 1)
                           for k in range(DC)],
                      waits=[ev_ct_ok, ev_xn[-1]] + ([ev_lg[-1]] if ev_lg else []))
        e_l1 = p.emit(V, tt(logit[:, 512 * nt:512 * (nt + 1)], ps_r2[:, 512 * nt:512 * (nt + 1)],
                            ninv[:, 512 * nt:512 * (nt + 1)], ALU.mult), waits=[e_m1, e_n5])
        e_m2 = p.emit(TE, [mm(ps_r2[:, 512 * nt:512 * (nt + 1)], rtT[k][:, :],
                              xn_bf[k][:, 512 * nt:512 * (nt + 1)], k == 0, k == DC - 1)
                           for k in range(DC)],
                      waits=[e_l1])
        e_l2 = p.emit(V, tt(logit[:, 512 * nt:512 * (nt + 1)], logit[:, 512 * nt:512 * (nt + 1)],
                            ps_r2[:, 512 * nt:512 * (nt + 1)], ALU.add), waits=[e_m2])
        ev_lg.append(e_l2)
    dump("logits", logit[:, :], [ev_lg[-1]])
    ev_gf = []
    for blk in range(16):
        e_t = p.emit(TE, trf(ps_r[:, 0:8] if blk == 0 else ps_r[:, 0:8],
                             logit[0:8, 128 * blk:128 * (blk + 1)]),
                     waits=[ev_lg[-1]] + ([ev_gf[-1]] if ev_gf else []))
        e_e = p.emit(V, cp(gat_tm[:, 8 * (blk % 16):8 * (blk % 16) + 8], ps_r[0:128, 0:8]),
                     waits=[e_t])
        ev_gf.append(e_e)
    g3 = gat_tm[:].rearrange("p (g e) -> p g e", e=8)
    e_m1g = p.emit(V, red(g16[:, :], g3, ALU.max, neg=True), waits=[ev_gf[-1]])
    gat3 = gat_tm[:].rearrange("p (g e) -> p g e", e=8)
    gsc3 = gat_sc[:].rearrange("p (g e) -> p g e", e=8)
    e_z = p.emit(V, tt(gat3, gat3,
                       g16[:].unsqueeze(2).broadcast_to([128, 16, 8]), ALU.add), waits=[e_m1g])
    e_msk = p.emit(V, ts(gat_sc[:], gat_tm[:], 0.0, None, ALU.is_ge), waits=[e_z])
    e_zt = p.emit(V, stt(gat_sc[:], gat_sc[:], NEG / 2, gat_tm[:], ALU.mult, ALU.add), waits=[e_msk])
    e_m2g = p.emit(V, red(g16[:, :], gat_sc[:].rearrange("p (g e) -> p g e", e=8), ALU.max),
                   waits=[e_zt])
    e_kp = p.emit(V, tt(gsc3, gat3,
                        g16[:].unsqueeze(2).broadcast_to([128, 16, 8]), ALU.is_ge), waits=[e_m2g])
    e_eg = p.emit(A, act(gat_tm[:], gat_tm[:], AF.Exp), waits=[e_kp])
    e_egm = p.emit(V, tt(gat_tm[:], gat_tm[:], gat_sc[:], ALU.mult), waits=[e_eg])
    e_den = p.emit(V, red(g16[:, :], gat_tm[:].rearrange("p (g e) -> p g e", e=8), ALU.add),
                   waits=[e_egm])
    e_rcg = p.emit(V, recip(g16[:, :], g16[:, :]), waits=[e_den])
    e_gn = p.emit(V, tt(gat3, gat3,
                        g16[:].unsqueeze(2).broadcast_to([128, 16, 8]), ALU.mult), waits=[e_rcg])
    ev_gfm = []
    for blk in range(16):
        e_t = p.emit(TE, trf(ps_r2[0:8, 128 * blk:128 * (blk + 1)],
                             gat_tm[:, 8 * blk:8 * blk + 8]),
                     waits=[e_gn] + ([ev_gfm[-1]] if ev_gfm else []))
        ev_gfm.append(e_t)
    e_gfe = p.emit(V, cp(gates_fm[:, :], ps_r2[0:8, :]), waits=[ev_gfm[-1]])
    e_grw = p.emit(SY, dmaf(grow_dram[0:1, :], gates_fm[:, :]), waits=[e_gfe], dma=True)
    dump("gates", gates_fm[:, :], [e_gfe])
    RTP.close()
    cut("route")

    MO = ExitStack()
    w1r = [MO.enter_context(nc.sbuf_tensor(f"w1r{j}", [128, 8 * 512], BF16)) for j in range(2)]
    hp = [MO.enter_context(nc.sbuf_tensor(f"hp{e}", [128, 4 * 512], BF16)) for e in range(8)]
    w2r = [MO.enter_context(nc.sbuf_tensor(f"w2r{j}", [128, 4096], BF16)) for j in range(2)]
    gtile = MO.enter_context(nc.sbuf_tensor("gtile", [128, 512], BF16))
    grow_sb = MO.enter_context(nc.sbuf_tensor("grow_sb", [1, 512], BF16))
    ev_gb_prev = None
    hstg = MO.enter_context(nc.sbuf_tensor("hstg", [128, 512], BF16))
    gstg = MO.enter_context(nc.sbuf_tensor("gstg", [128, 512], F32))
    b1c = MO.enter_context(nc.sbuf_tensor("b1c", [128, 32], F32))
    b2b = MO.enter_context(nc.sbuf_tensor("b2b", [8, D], BF16))
    ps_mh = MO.enter_context(nc.psum_tensor("ps_mh", [128, 512], F32))
    ps_mo = MO.enter_context(nc.psum_tensor("ps_mo", [128, 512], F32))
    ps_gb = MO.enter_context(nc.psum_tensor("ps_gb", [128, 512], F32))

    e_b1 = p.emit(SY, dmaf(b1c[:, :], moe_b1[:, :]), dma=True)
    e_b2l = p.emit(SY, dmaf(misc[0:8, 0:128], moe_b2[:, 0:128]), waits=[e_am], dma=True)
    ev_b2c = []
    for kk in range(8):
        evl = p.emit(SY, dmaf(misc[0:8, 128:256], moe_b2[:, 128 * kk:128 * (kk + 1)]),
                     waits=[ev_b2c[-1]] if ev_b2c else [], dma=True)
        ev_b2c.append(p.emit(V, cp(b2b[:, 128 * kk:128 * (kk + 1)], misc[0:8, 128:256]),
                             waits=[evl]))
    ev_b2 = ev_b2c[-1]

    w1view = [w1r[j][:].rearrange("p (k c) -> p k c", k=8) for j in range(2)]
    w2view = [w2r[j][:].rearrange("p (e k c) -> p e k c", e=8, k=4) for j in range(2)]
    ev_w1use = [None, None]
    ev_w2use = [None, None]
    ev_ge_prev = [None]
    ev_gm_prev = [None]
    ev_sil_prev = [None]
    ev_hp_use = [[None] * 8, [None] * 8]
    hp_ready = [None] * 8
    ev_g_st = []
    ev_gstg_ok = None
    for tti in range(4):
        tt0 = 512 * tti
        for e_i in range(8):
            r = e_i % 2
            evw = p.emit(SY, dmaf(w1r[r][:].rearrange("p (k c) -> p k c", k=8),
                                  w1bf_dram[e_i, :, :, :].transpose([1, 0, 2])),
                         waits=[ev_w1use[r]] + mw_w1st[e_i], dma=True)
            e_gl = p.emit(G, dmaf(grow_sb[0:1, :], grow_dram[0:1, T * e_i + tt0:T * e_i + tt0 + 512]),
                          waits=[e_grw, ev_gb_prev], dma=True)
            e_gb = p.emit(TE, mm(ps_gb[:, :], ones1b[:, 0:128], grow_sb[0:1, :], True, True),
                          waits=[e_gl])
            e_gt = p.emit(V, cp(gtile[:], ps_gb[:, :]), waits=[e_gb, ev_gm_prev[0]])
            ev_gb_prev = e_gt
            last_gm = None
            for ci in range(4):
                e_h = p.emit(TE, [mm(ps_mh[:, :], w1view[r][:, k, 128 * ci:128 * (ci + 1)],
                                     xn_bf[k][:, tt0:tt0 + 512], k == 0, k == DC - 1)
                                  for k in range(DC)],
                             waits=[evw, ev_xn[-1], hp_ready[e_i], ev_ge_prev[0]])
                e_ge = p.emit(A, act(hstg[:], ps_mh[:, :], AF.Gelu, bias=b1c[:, 4 * e_i + ci:4 * e_i + ci + 1]),
                              waits=[e_h, e_b1, ev_gm_prev[0]])
                ev_ge_prev[0] = e_ge
                e_gm = p.emit(V, tt(hp[e_i][:, 512 * ci:512 * (ci + 1)], hstg[:], gtile[:], ALU.mult),
                              waits=[e_ge, e_gt])
                ev_gm_prev[0] = e_gm
                last_gm = e_gm
            ev_w1use[r] = last_gm
            hp_ready[e_i] = last_gm
        for m in range(8):
            r2 = m % 2
            evw2 = p.emit(SY, dmaf(w2r[r2][:].rearrange("p (e k c) -> p e k c", e=8, k=4),
                                   w2bf_dram[m, :, :, :, :].transpose([2, 0, 1, 3])),
                          waits=[ev_w2use[r2]] + mw_w2st, dma=True)
            fns = []
            for e_i in range(8):
                for kc in range(4):
                    fns.append(mm(ps_mo[:, :], w2view[r2][:, e_i, kc, :],
                                  hp[e_i][:, 512 * kc:512 * kc + 512],
                                  e_i == 0 and kc == 0, False))
            fns.append(mm(ps_mo[:, :], b2b[:, 128 * m:128 * (m + 1)],
                          gates_fm[:, tt0:tt0 + 512], False, True))
            e_o = p.emit(TE, fns, waits=[evw2, ev_b2, e_gfe, ev_sil_prev[0]] +
                         [hp_ready[e_i] for e_i in range(8)])
            e_sil = p.emit(A, act(gstg[:], ps_mo[:, :], AF.Silu), waits=[e_o, ev_gstg_ok])
            ev_sil_prev[0] = e_sil
            e_gs = p.emit(SY, dmaf(g_dram[128 * m:128 * (m + 1), tt0:tt0 + 512], gstg[:]),
                          waits=[e_sil], dma=True)
            ev_gstg_ok = e_gs
            ev_w2use[r2] = e_o
            ev_g_st.append(e_gs)
        if tti == 3:
            break
        for e_i in range(8):
            hp_ready[e_i] = ev_g_st[-1]
    ev_moe_done = ev_g_st[-1]
    MO.close()
    RT.close()
    MW.close()
    cut("moe")
    dump("g0", gstg[:], [ev_moe_done])

    VAL = ExitStack()
    val_bf = [VAL.enter_context(nc.sbuf_tensor(f"val_bf{k}", [128, T], BF16)) for k in range(DC)]
    VL = ExitStack()
    wvb2 = [VL.enter_context(nc.sbuf_tensor(f"wvb2_{k}", [128, D], BF16)) for k in range(DC)]
    ps_vl = VL.enter_context(nc.psum_tensor("ps_vl", [128, 512], F32))
    ev_wv = None
    for k in range(DC):
        evl = p.emit(SY, dmaf(xring[0][:, 0:D], wv_w[128 * k:128 * (k + 1), :]),
                     waits=[ev_wv] if ev_wv else [ev_xr[0]], dma=True)
        ev_wv = p.emit(V, cp(wvb2[k][:], xring[0][:, 0:D]), waits=[evl])
    ev_val = []
    for m in range(DC):
        for nt in range(4):
            e_m = p.emit(TE, [mm(ps_vl[:, :], wvb2[k][:, 128 * m:128 * (m + 1)],
                                 xn_bf[k][:, 512 * nt:512 * (nt + 1)], k == 0, k == DC - 1)
                              for k in range(DC)],
                         waits=[ev_wv, ev_xn[-1]] + ([ev_val[-1]] if ev_val else []))
            e_e = p.emit(V, ts(val_bf[m][:, 512 * nt:512 * (nt + 1)], ps_vl[:, :],
                               biasc[:, m:m + 1], None, ALU.add), waits=[e_m, ev_bias])
            ev_val.append(e_e)
    VL.close()
    cut("val")
    dump("value", val_bf[0][:], [ev_val[3]])

    CB = ExitStack()
    t_xl = CB.enter_context(nc.sbuf_tensor("t_xl", [128, T], F32))
    t_w = CB.enter_context(nc.sbuf_tensor("t_w", [128, T], F32))
    t_sw = CB.enter_context(nc.sbuf_tensor("t_sw", [128, T], F32))
    t_enc = CB.enter_context(nc.sbuf_tensor("t_enc", [128, T], F32))
    gring = [CB.enter_context(nc.sbuf_tensor(f"gring{j}", [128, T], F32)) for j in range(2)]
    alpha3 = alpha_b[:].unsqueeze(2).broadcast_to([128, S, 8])
    ev_xnew_st = []
    ev_xr2 = [ev_wv, ev_wv]
    ev_gr = [ev_moe_done, ev_moe_done]
    ev_cb_prev = None
    for k in range(DC):
        r = k % 2
        evl = p.emit(SY, dmaf(xring[r][:], x_fm[128 * k:128 * (k + 1), :]),
                     waits=[ev_xr2[r]], dma=True)
        evg = p.emit(SY, dmaf(gring[r][:], g_dram[128 * k:128 * (k + 1), :]),
                     waits=[ev_gr[r]], dma=True)
        w0 = [ev_alpha, ev_val[-1], ev_cb_prev]
        e1 = p.emit(V, tt(t_xl[:].rearrange("p (s b) -> p s b", b=8),
                          val_bf[k][:].rearrange("p (s b) -> p s b", b=8), alpha3, ALU.mult),
                    waits=w0)
        e2 = p.emit(A, act(t_w[:], t_xl[:], AF.Sigmoid, scale=cconst[:, 4:5]), waits=[e1])
        e3 = p.emit(A, act(t_sw[:], t_w[:], AF.Square), waits=[e2])
        e4 = p.emit(V, tt(t_w[:], t_xl[:], t_sw[:], ALU.mult), waits=[e3])
        e5 = p.emit(V, ts(t_xl[:], t_sw[:], -1.0, 1.0, ALU.mult, ALU.add), waits=[e4])
        e6 = p.emit(V, tt(t_enc[:], xring[r][:], m_b[:], ALU.subtract), waits=[evl, e5])
        e7 = p.emit(V, tt(t_enc[:], t_enc[:], istd_b[:], ALU.mult), waits=[e6])
        e8 = p.emit(V, tt(t_enc[:].rearrange("p (s b) -> p s b", b=8),
                          t_enc[:].rearrange("p (s b) -> p s b", b=8),
                          xg0[k][:].unsqueeze(2).broadcast_to([128, S, 8]), ALU.add),
                    waits=[e7, ev_xg0[k]])
        e9 = p.emit(V, tt(t_xl[:], t_xl[:], t_enc[:], ALU.mult), waits=[e8])
        e10 = p.emit(V, tt(t_w[:], t_w[:], t_xl[:], ALU.add), waits=[e9])
        e11 = p.emit(V, tt(t_w[:], t_w[:], gring[r][:], ALU.mult), waits=[e10, evg])
        e12 = p.emit(V, stt(xring[r][:], t_w[:], 2.0, xring[r][:], ALU.mult, ALU.add),
                     waits=[e11])
        evst = p.emit(SY, dmaf(xnew_dram[128 * k:128 * (k + 1), :], xring[r][:]),
                      waits=[e12], dma=True)
        ev_xnew_st.append(evst)
        ev_xr2[r] = evst
        ev_gr[r] = e11
        ev_cb_prev = e12
    dump("xnew", xring[1][:], [ev_xnew_st[-1]])
    CB.close()
    VAL.close()
    L0.close()
    cut("combine")

    L1 = ExitStack()

    def sb1(name, shape, dt=F32):
        return L1.enter_context(nc.sbuf_tensor(name, list(shape), dt))

    xq_bf = [sb1(f"xq{k}", [128, T], BF16) for k in range(DC)]
    clip2 = sb1("clip2", [1, 2])
    l1r2 = sb1("l1r2", [1, I_FF])
    bc1_ps = None

    bc1_last = [None]

    def bcast1(dst_tile, row_ap, n, waits, scale=None):
        evs = []
        nt = (n + 511) // 512
        for j in range(nt):
            w = min(512, n - 512 * j)
            ev_m = p.emit(TE, mm(bc1_ps[:, 0:w], ones1[:, 0:128], row_ap[0:1, 512 * j:512 * j + w],
                                 True, True),
                          waits=list(waits) + [evs[-1] if evs else bc1_last[0]])
            if scale is None:
                ev_c = p.emit(V, cp(dst_tile[:, 512 * j:512 * j + w], bc1_ps[:, 0:w]), waits=[ev_m])
            else:
                ev_c = p.emit(V, ts(dst_tile[:, 512 * j:512 * j + w], bc1_ps[:, 0:w], scale), waits=[ev_m])
            evs.append(ev_c)
        bc1_last[0] = evs[-1]
        return evs[-1]

    LA = ExitStack()

    def sba(name, shape, dt=F32):
        return LA.enter_context(nc.sbuf_tensor(name, list(shape), dt))

    xn1_bf = [sba(f"xn1_{k}", [128, T], BF16) for k in range(DC)]
    absa = sba("absa", [128, 2])
    LSB = ExitStack()
    m1_b = LSB.enter_context(nc.sbuf_tensor("m1_b", [128, T], F32))
    i1_b = LSB.enter_context(nc.sbuf_tensor("i1_b", [128, T], F32))

    RW1 = ExitStack()
    l1row = RW1.enter_context(nc.sbuf_tensor("l1row", [1, 2 * T], F32))
    LN1 = ExitStack()
    ps_s1 = LN1.enter_context(nc.psum_tensor("ps_s1", [1, T], F32))
    ps_s2 = LN1.enter_context(nc.psum_tensor("ps_s2", [1, T], F32))
    mm_s1, mm_s2 = [], []
    ev_xn_ok = [[ev_xnew_st[-2]], [ev_xnew_st[-1]]]
    for k in range(DC):
        r = k % 2
        evl = p.emit(SY, dmaf(xring[r][:], xnew_dram[128 * k:128 * (k + 1), :]),
                     waits=ev_xn_ok[r] + [ev_xnew_st[k]], dma=True)
        evsq = p.emit(A, act(sq_s[:], xring[r][:], AF.Square),
                      waits=[evl] + ([mm_s2[-1]] if mm_s2 else []))

        def mk1(dst, src, k=k):
            return [mm(dst[:, 512 * j:512 * (j + 1)], ones128[:, 0:1],
                       src[:, 512 * j:512 * (j + 1)], k == 0, k == DC - 1) for j in range(4)]

        m1_ = p.emit(TE, mk1(ps_s1, xring[r]), waits=[evl])
        mm_s1.append(m1_)
        m2_ = p.emit(TE, mk1(ps_s2, sq_s), waits=[evsq])
        mm_s2.append(m2_)
        ev_xn_ok[r] = [m1_, evsq]
    ev_m1r = p.emit(V, ts(l1row[0:1, 0:T], ps_s1[:, :], 1.0 / D), waits=[mm_s1[-1]])
    ev_q1r = p.emit(V, ts(l1row[0:1, T:2 * T], ps_s2[:, :], 1.0 / D), waits=[mm_s2[-1]])
    LN1.close()
    bc1_ps = L1.enter_context(nc.psum_tensor("bc1_ps", [128, 512], F32))
    ev_m1b = bcast1(m1_b, l1row[0:1, 0:T], T, [ev_m1r])
    ev_v1b = bcast1(i1_b, l1row[0:1, T:2 * T], T, [ev_q1r])
    RW1.close()
    ev_m12 = p.emit(V, tt(sq_s[:], m1_b[:], m1_b[:], ALU.mult), waits=[ev_m1b, ev_q1r])
    ev_var1 = p.emit(V, tt(i1_b[:], i1_b[:], sq_s[:], ALU.subtract), waits=[ev_m12, ev_v1b])
    ev_sd1 = p.emit(A, act(sq_s[:], i1_b[:], AF.Sqrt, bias=cconst[:, 0:1]), waits=[ev_var1])
    ev_i1 = p.emit(V, recip(i1_b[:], sq_s[:]), waits=[ev_sd1])

    ev_xn1 = []
    ev_ab = None
    ev_sq_last = None
    ev_xr3 = [ev_i1, ev_i1]
    for k in range(DC):
        r = k % 2
        evl = p.emit(SY, dmaf(xring[r][:], xnew_dram[128 * k:128 * (k + 1), :]),
                     waits=[ev_xr3[r]], dma=True)
        e1 = p.emit(V, tt(sq_s[:], xring[r][:], m1_b[:], ALU.subtract),
                    waits=[evl, ev_i1, ev_ab, ev_sq_last])
        e2 = p.emit(V, tt(sq_s[:], sq_s[:], i1_b[:], ALU.mult), waits=[e1])
        e3 = p.emit(V, cp(xn1_bf[k][:], sq_s[:]), waits=[e2])
        if ev_ab is None:
            ev_ab = p.emit(V, red(absa[:, 0:1], sq_s[:], ALU.max, absv=True), waits=[e2])
        else:
            e4 = p.emit(V, red(absa[:, 1:2], sq_s[:], ALU.max, absv=True), waits=[e2])
            ev_ab = p.emit(V, tt(absa[:, 0:1], absa[:, 0:1], absa[:, 1:2], ALU.max), waits=[e4, ev_ab])
        ev_xn1.append(e3)
        ev_sq_last = e3
        ev_xr3[r] = e2
    dump("xn1", xn1_bf[0][:], [ev_xn1[0]])
    LSB.close()
    PMX = ExitStack()
    ps_px = PMX.enter_context(nc.psum_tensor("ps_px", [128, 128], F32))
    e_tp = p.emit(TE, trf(ps_px[0:1, :], absa[:, 0:1]), waits=[ev_ab])
    e_te = p.emit(V, cp(misc[0:1, 0:128], ps_px[0:1, :]), waits=[e_tp])
    e_tm = p.emit(V, red(srow[0:1, 120:121], misc[0:1, 0:128], ALU.max), waits=[e_te])
    e_sx = p.emit(G, dmaf(ar_x_in[:, :], srow[0:1, 120:121]), waits=[e_tm], dma=True)
    ev_ar_x = coll("AllReduce", ALU.max, ar_x_in[:, :], ar_x_out[:, :], [e_sx])
    e_lx = p.emit(G, dmaf(srow[0:1, 121:122], ar_x_out[:, :]), waits=[ev_ar_x], dma=True)
    PMX.close()
    cut("ln1")

    AT = ExitStack()

    def sbat(name, shape, dt=F32):
        return AT.enter_context(nc.sbuf_tensor(name, list(shape), dt))

    q_il = [sbat(f"qil{k}", [128, T], BF16) for k in range(DC)]
    k_il = [sbat(f"kil{k}", [128, T], BF16) for k in range(DC)]
    wm_bf = [sbat(f"wmbf{j}", [128, D], BF16) for j in range(2)]
    wstage = sbat("wstage", [128, D])
    wvT_b = [sbat(f"wvT{k}", [128, D], BF16) for k in range(DC)]
    bqk = sbat("bqk", [128, 16])
    bv_bc = sbat("bv_bc", [128, D])
    nk2_b = sbat("nk2_b", [128, T])
    vt_r = [sbat(f"vt{j}", [128, D], BF16) for j in range(2)]
    e_til = sbat("e_til", [128, 128], BF16)
    s_til = sbat("s_til", [128, 128], BF16)
    st_til = sbat("st_til", [128, 128], BF16)
    ostg = sbat("ostg", [128, D], BF16)

    ev_bqk = p.emit(SY, dmaf(bqk[:, :], l1_bqk[:, :]), dma=True)
    ev_bvl = p.emit(SY, dmaf(l1r2[0:1, 0:D], l1_bv_row[:, :]), dma=True)
    ev_bvb = bcast1(bv_bc, l1r2[0:1, 0:D], D, [ev_bvl])
    ev_wc = None
    for k in range(DC):
        evl = p.emit(SY, dmaf(wstage[:, 0:D], l1_w_T[2, 128 * k:128 * (k + 1), :]),
                     waits=[ev_wc] if ev_wc else [ev_xr3[0]], dma=True)
        ev_wc = p.emit(V, cp(wvT_b[k][:], wstage[:, 0:D]), waits=[evl])
    QK1 = ExitStack()
    ps_qk = QK1.enter_context(nc.psum_tensor("ps_qk", [128, 512], F32))
    ev_q1 = []
    ev_wm = [ev_wc, ev_wc]
    ev_wstg = ev_wc
    for w_i, dstl, bcol in ((0, q_il, 0), (1, k_il, 8)):
        for m in range(8):
            r = m % 2
            evl = p.emit(SY, dmaf(wstage[:, 0:D].rearrange("p (k c) -> p k c", k=8),
                                  l1_w_T[w_i, :, 128 * m:128 * (m + 1)]
                                  .rearrange("(k p) c -> k p c", p=128).transpose([1, 0, 2])),
                         waits=[ev_wm[r], ev_wstg], dma=True)
            evc = p.emit(V, cp(wm_bf[r][:], wstage[:, 0:D]), waits=[evl])
            ev_wstg = evc
            for nt in range(4):
                e_m = p.emit(TE, [mm(ps_qk[:, :], wm_bf[r][:, 128 * k:128 * (k + 1)],
                                     xn1_bf[k][:, 512 * nt:512 * (nt + 1)], k == 0, k == DC - 1)
                                  for k in range(DC)],
                             waits=[evc, ev_xn1[-1]] + ([ev_q1[-1]] if ev_q1 else []))
                e_e = p.emit(V, ts(dstl[m][:, 512 * nt:512 * (nt + 1)], ps_qk[:, :],
                                   bqk[:, bcol + m:bcol + m + 1], None, ALU.add),
                             waits=[e_m, ev_bqk])
                ev_q1.append(e_e)
            ev_wm[r] = ev_q1[-1]
    QK1.close()
    K2 = ExitStack()
    ps_k2 = K2.enter_context(nc.psum_tensor("ps_k2", [1, T], F32))
    mm_k2 = []
    for k in range(DC):
        e_sq = p.emit(A, act(sq_s[:], k_il[k][:], AF.Square),
                      waits=[ev_q1[-1]] + ([mm_k2[-1]] if mm_k2 else []))
        e_m = p.emit(TE, [mm(ps_k2[:, 512 * j:512 * (j + 1)], ones128[:, 0:1],
                             sq_s[:, 512 * j:512 * (j + 1)], k == 0, k == DC - 1) for j in range(4)],
                     waits=[e_sq])
        mm_k2.append(e_m)
    e_k2r = p.emit(V, ts(l1r2[0:1, 0:T], ps_k2[:, :], -0.5), waits=[mm_k2[-1], ev_bvb])
    K2.close()
    ev_nk2 = bcast1(nk2_b, l1r2[0:1, 0:T], T, [e_k2r])

    GR = ExitStack()
    ps_v2 = GR.enter_context(nc.psum_tensor("ps_v2", [128, 1024], F32))
    ps_gr = GR.enter_context(nc.psum_tensor("ps_gr", [128, 512], F32))
    ps_st = GR.enter_context(nc.psum_tensor("ps_st", [128, 1024], BF16))
    ps_o1 = GR.enter_context(nc.psum_tensor("ps_o1", [128, 1024], F32))
    ev_ost = []
    ev_vt = [None, None]
    ev_grprev = None
    for t in range(16):
        r = t % 2
        e_vm = p.emit(TE, [mm(ps_v2[:, 512 * n2:512 * (n2 + 1)],
                              xn1_bf[k][:, 128 * t:128 * (t + 1)],
                              wvT_b[k][:, 512 * n2:512 * (n2 + 1)], k == 0, k == DC - 1)
                           for n2 in range(2) for k in range(DC)],
                      waits=[ev_wc, ev_xn1[-1], ev_vt[r]])
        e_ve = p.emit(V, [tt(vt_r[r][:, 512 * n2:512 * (n2 + 1)], ps_v2[:, 512 * n2:512 * (n2 + 1)],
                             bv_bc[:, 512 * n2:512 * (n2 + 1)], ALU.add) for n2 in range(2)],
                      waits=[e_vm, ev_bvb])
        e_gm = p.emit(TE, [mm(ps_gr[:, 0:128], q_il[k][:, 128 * t:128 * (t + 1)],
                              k_il[k][:, 128 * t:128 * (t + 1)], k == 0, k == DC - 1)
                           for k in range(DC)],
                      waits=[ev_q1[-1]] + ([ev_grprev] if ev_grprev else []))
        e_z1 = p.emit(V, tt(misc[:, 0:128], ps_gr[:, 0:128], nk2_b[:, 128 * t:128 * (t + 1)],
                            ALU.add), waits=[e_gm, ev_nk2])
        e_z2 = p.emit(V, tt(misc[:, 128:256], misc[:, 0:128], maskb[:], ALU.add), waits=[e_z1])
        ev_grprev = e_z1
        e_nm = p.emit(V, red(scol[:, 2:3], misc[:, 128:256], ALU.max, neg=True), waits=[e_z2])
        e_ex = p.emit(A, act(e_til[:], misc[:, 128:256], AF.Exp, bias=scol[:, 2:3]), waits=[e_nm])
        e_dn = p.emit(V, red(scol[:, 3:4], e_til[:], ALU.add), waits=[e_ex])
        e_rc = p.emit(V, recip(scol[:, 4:5], scol[:, 3:4]), waits=[e_dn])
        e_s = p.emit(V, ts(s_til[:], e_til[:], scol[:, 4:5]), waits=[e_rc])
        e_tr = p.emit(TE, trb(ps_st[:, 0:128], s_til[:]), waits=[e_s])
        e_ste = p.emit(A, act(st_til[:], ps_st[:, 0:128], AF.Copy), waits=[e_tr])
        e_om = p.emit(TE, [mm(ps_o1[:, 512 * n2:512 * (n2 + 1)], st_til[:],
                              vt_r[r][:, 512 * n2:512 * (n2 + 1)], True, True) for n2 in range(2)],
                      waits=[e_ste, e_ve])
        e_oe = p.emit(A, [act(ostg[:, 512 * n2:512 * (n2 + 1)], ps_o1[:, 512 * n2:512 * (n2 + 1)],
                              AF.Copy) for n2 in range(2)],
                      waits=[e_om] + ([ev_ost[-1]] if ev_ost else []))
        e_os = p.emit(SY, dmaf(o_dram[t, :, :], ostg[:, :]), waits=[e_oe], dma=True)
        ev_ost.append(e_os)
        ev_vt[r] = e_om
    GR.close()
    dump("o_l1", ostg[:, :], [ev_ost[-1]])
    AT.close()
    XQ = ExitStack()
    e_cll = p.emit(SY, dmaf(clip2[:, :], l1_clip[:, :]), dma=True)
    e_sig = p.emit(A, act(clip2[:, :], clip2[:, :], AF.Sigmoid), waits=[e_cll])
    e_t1 = p.emit(V, tt(srow[0:1, 122:123], clip2[0:1, 0:1], srow[0:1, 121:122], ALU.mult),
                  waits=[e_sig, e_lx])
    e_r1 = p.emit(V, recip(srow[0:1, 123:124], srow[0:1, 122:123]), waits=[e_t1])
    e_i1 = p.emit(V, ts(srow[0:1, 124:125], srow[0:1, 123:124], 127.0), waits=[e_r1])
    e_s1 = p.emit(V, ts(srow[0:1, 125:126], srow[0:1, 122:123], 1.0 / 127.0), waits=[e_t1])
    ps_xx = XQ.enter_context(nc.psum_tensor("ps_xx", [128, 128], F32))
    e_bx = p.emit(TE, mm(ps_xx[:, 0:2], ones1[:, 0:128], srow[0:1, 124:126], True, True),
                  waits=[e_i1, e_s1])
    e_bxe = p.emit(V, cp(scol[:, 5:7], ps_xx[:, 0:2]), waits=[e_bx])
    ev_xq = []
    for k in range(DC):
        e1 = p.emit(V, ts(sq_s[:], xn1_bf[k][:], scol[:, 5:6], MAGIC, ALU.mult, ALU.add),
                    waits=[e_bxe, ev_xn1[k], ev_sq_last] + ([ev_xq[-1]] if ev_xq else []))
        e2 = p.emit(V, ts(sq_s[:], sq_s[:], MAGIC, None, ALU.subtract), waits=[e1])
        e3 = p.emit(V, ts(xq_bf[k][:], sq_s[:], -128.0, 127.0, ALU.max, ALU.min), waits=[e2])
        ev_xq.append(e3)
        ev_sq_last = e3
    dump("xq", xq_bf[0][:], [ev_xq[0]])
    XQ.close()
    LA.close()
    cut("l1attn")

    FF = ExitStack()

    def sbf(name, shape, dt=F32):
        return FF.enter_context(nc.sbuf_tensor(name, list(shape), dt))

    rinv_g = sbf("rinv_g", [128, IC])
    rinv_u = sbf("rinv_u", [128, IC])
    comb_g = sbf("comb_g", [128, IC])
    comb_u = sbf("comb_u", [128, IC])
    hp_t = [sbf(f"hpre{m}", [128, T], BF16) for m in range(IC)]
    habs = sbf("habs", [128, 2])
    GQ = ExitStack()
    rb_t = GQ.enter_context(nc.sbuf_tensor("rb_t", [128, I_FF], F32))
    wscr = GQ.enter_context(nc.sbuf_tensor("wscr", [128, 1408], F32))
    wbst = GQ.enter_context(nc.sbuf_tensor("wbst", [128, 1408], BF16))
    PSW = ExitStack()
    ps_w = PSW.enter_context(nc.psum_tensor("ps_w", [128, 128], F32))

    def wquant(nat, wT, qdram, rinv, combv, ev_after):
        evs = []
        for k in range(IC):
            r = k % 2
            evl = p.emit(SY, dmaf(xring[r][:, 0:D], nat[128 * k:128 * (k + 1), :]),
                         waits=[evs[-2][1] if len(evs) >= 2 else ev_after], dma=True)
            e_rm = p.emit(V, red(scol[:, 8:9], xring[r][:, 0:D], ALU.max, absv=True),
                          waits=[evl] + ([evs[-1][0], evs[-1][1]] if evs else []))
            e_rc = p.emit(V, recip(scol[:, 9:10], scol[:, 8:9]), waits=[e_rm])
            e_ri = p.emit(V, ts(rinv[:, k:k + 1], scol[:, 9:10], 127.0), waits=[e_rc])
            e_sw = p.emit(V, ts(combv[:, k:k + 1], scol[:, 8:9], 1.0 / 127.0), waits=[e_rm])
            evs.append((e_ri, e_sw))
        e_cmb = p.emit(V, tt(combv[:, :], combv[:, :],
                             scol[:, 6:7].broadcast_to([128, IC]), ALU.mult),
                       waits=[evs[-1][1], evs[-2][1], e_bxe])
        e_tp = p.emit(TE, trf(ps_w[0:IC, :], rinv[:, :]), waits=[evs[-1][0], evs[-2][0]])
        e_te = p.emit(V, cp(misc[0:IC, 0:128], ps_w[0:IC, :]), waits=[e_tp])
        e_rw = p.emit(SY, dmaf(l1r2[0:1, 0:I_FF], misc[0:IC, 0:128]),
                      waits=[e_te, ev_nk2], dma=True)
        e_rb = bcast1(rb_t, l1r2[0:1, 0:I_FF], I_FF, [e_rw])
        ev_q = []
        for k in range(DC):
            for hf in range(2):
                r = (2 * k + hf) % 2
                evl = p.emit(SY, dmaf(xring[r][:, 0:1408], wT[128 * k:128 * (k + 1),
                                                              1408 * hf:1408 * (hf + 1)]),
                             waits=[ev_q[-2] if len(ev_q) >= 2 else e_rb], dma=True)
                e_v = p.emit(V, tt(wscr[:], xring[r][:, 0:1408],
                                   rb_t[:, 1408 * hf:1408 * (hf + 1)], ALU.mult),
                             waits=[evl, e_rb] + ([ev_q[-1]] if ev_q else []))
                e_r = p.emit(V, ts(wbst[:], wscr[:], MAGIC, MAGIC, ALU.add, ALU.subtract),
                             waits=[e_v])
                e_st = p.emit(SY, dmaf(qdram[k, :, 1408 * hf:1408 * (hf + 1)], wbst[:]),
                              waits=[e_r], dma=True)
                ev_q.append(e_st)
        return e_cmb, ev_q[-1], ev_q[-2]

    ev_cg, ev_qg1, ev_qg2 = wquant(wg_nat, wg_T, wgq_dram, rinv_g, comb_g, ev_xq[-1])
    ev_cu, ev_qu1, ev_qu2 = wquant(wu_nat, wu_T, wuq_dram, rinv_u, comb_u, ev_qg1)
    PSW.close()
    GQ.close()
    cut("gq")

    GU = ExitStack()
    wmr = [GU.enter_context(nc.sbuf_tensor(f"wmr{j}", [128, D], BF16)) for j in range(2)]
    gsil = GU.enter_context(nc.sbuf_tensor("gsil", [128, T], BF16))
    PSG = ExitStack()
    ps_g = PSG.enter_context(nc.psum_tensor("ps_g", [128, 512], F32))
    ev_hp = []
    ev_wmr = [ev_qg1, ev_qg2]
    ev_hab = None
    for m in range(IC):
        r = m % 2
        evlg = p.emit(SY, dmaf(wmr[r][:].rearrange("p (k c) -> p k c", k=8),
                               wgq_dram[:, :, 128 * m:128 * (m + 1)].transpose([1, 0, 2])),
                      waits=[ev_wmr[r], ev_qg1, ev_qu1], dma=True)
        ev_gm = []
        for nt in range(4):
            e_m = p.emit(TE, [mm(ps_g[:, :], wmr[r][:, 128 * k:128 * (k + 1)],
                                 xq_bf[k][:, 512 * nt:512 * (nt + 1)], k == 0, k == DC - 1)
                              for k in range(DC)],
                         waits=[evlg, ev_xq[-1]] + ([ev_gm[-1]] if ev_gm else
                                                    ([ev_hp[-1]] if ev_hp else [])))
            e_e = p.emit(A, act(gsil[:, 512 * nt:512 * (nt + 1)], ps_g[:, :], AF.Silu,
                                scale=comb_g[:, m:m + 1]), waits=[e_m, ev_cg])
            ev_gm.append(e_e)
        evlu = p.emit(SY, dmaf(wmr[r][:].rearrange("p (k c) -> p k c", k=8),
                               wuq_dram[:, :, 128 * m:128 * (m + 1)].transpose([1, 0, 2])),
                      waits=[ev_gm[-1]], dma=True)
        ev_um = []
        for nt in range(4):
            e_m = p.emit(TE, [mm(ps_g[:, :], wmr[r][:, 128 * k:128 * (k + 1)],
                                 xq_bf[k][:, 512 * nt:512 * (nt + 1)], k == 0, k == DC - 1)
                              for k in range(DC)],
                         waits=[evlu] + ([ev_um[-1]] if ev_um else []))
            e_e = p.emit(V, ts(misc[:, 0:512], ps_g[:, :], comb_u[:, m:m + 1]),
                         waits=[e_m, ev_cu])
            e_h = p.emit(V, tt(hp_t[m][:, 512 * nt:512 * (nt + 1)],
                               gsil[:, 512 * nt:512 * (nt + 1)], misc[:, 0:512], ALU.mult),
                         waits=[e_e])
            ev_um.append(e_h)
        ev_wmr[r] = ev_um[-1]
        ev_hp.append(ev_um[-1])
        if ev_hab is None:
            ev_hab = p.emit(V, red(habs[:, 0:1], hp_t[m][:], ALU.max, absv=True), waits=[ev_um[-1]])
        else:
            e4 = p.emit(V, red(habs[:, 1:2], hp_t[m][:], ALU.max, absv=True), waits=[ev_um[-1]])
            ev_hab = p.emit(V, tt(habs[:, 0:1], habs[:, 0:1], habs[:, 1:2], ALU.max),
                            waits=[e4, ev_hab])
    PSG.close()
    GU.close()
    dump("hpre", hp_t[0][:], [ev_hp[0]])
    PMH = ExitStack()
    ps_ph = PMH.enter_context(nc.psum_tensor("ps_ph", [128, 128], F32))
    e_tp = p.emit(TE, trf(ps_ph[0:1, :], habs[:, 0:1]), waits=[ev_hab])
    e_te = p.emit(V, cp(misc[0:1, 0:128], ps_ph[0:1, :]), waits=[e_tp])
    e_tm = p.emit(V, red(srow[0:1, 126:127], misc[0:1, 0:128], ALU.max), waits=[e_te])
    e_sh = p.emit(G, dmaf(ar_h_in[:, :], srow[0:1, 126:127]), waits=[e_tm], dma=True)
    ev_ar_h = coll("AllReduce", ALU.max, ar_h_in[:, :], ar_h_out[:, :], [e_sh])
    e_lh = p.emit(G, dmaf(srow[0:1, 127:128], ar_h_out[:, :]), waits=[ev_ar_h], dma=True)
    e_t2 = p.emit(V, tt(srow[0:1, 128:129], clip2[0:1, 1:2], srow[0:1, 127:128], ALU.mult),
                  waits=[e_lh, e_sig])
    e_r2 = p.emit(V, recip(srow[0:1, 129:130], srow[0:1, 128:129]), waits=[e_t2])
    e_i2 = p.emit(V, ts(srow[0:1, 130:131], srow[0:1, 129:130], 127.0), waits=[e_r2])
    e_s2 = p.emit(V, ts(srow[0:1, 131:132], srow[0:1, 128:129], 1.0 / 127.0), waits=[e_t2])
    e_bh = p.emit(TE, mm(ps_ph[:, 0:2], ones1[:, 0:128], srow[0:1, 130:132], True, True),
                  waits=[e_i2, e_s2])
    e_bhe = p.emit(V, cp(scol[:, 8:10], ps_ph[:, 0:2]), waits=[e_bh])
    PMH.close()
    cut("gu")
    ev_hq = []
    for m in range(IC):
        e1 = p.emit(V, ts(sq_s[:], hp_t[m][:], scol[:, 8:9], MAGIC, ALU.mult, ALU.add),
                    waits=[e_bhe, ev_hp[m]] + ([ev_hq[-1]] if ev_hq else []))
        e2 = p.emit(V, ts(sq_s[:], sq_s[:], MAGIC, None, ALU.subtract), waits=[e1])
        e3 = p.emit(V, ts(hp_t[m][:], sq_s[:], -128.0, 127.0, ALU.max, ALU.min), waits=[e2])
        ev_hq.append(e3)

    sd_c = sbf("sd_c", [128, 8])
    rinv_d = sbf("rinv_d", [128, 8])
    rbH = sbf("rbH", [128, D])
    sdh_b = sbf("sdh_b", [128, D])
    PSD = ExitStack()
    ps_d0 = PSD.enter_context(nc.psum_tensor("ps_d0", [128, 128], F32))
    ev_ds = []
    for k in range(DC):
        r = k % 2
        evl = p.emit(SY, [dmaf(xring[r][:, 0:1408], wd_nat[128 * k:128 * (k + 1), 0:1408]),
                          dmaf(xring[r][:, 1408:2016],
                               wd_nat[128 * k:128 * (k + 1), 1408:2016])],
                     waits=[ev_ds[-2] if len(ev_ds) >= 2 else ev_hq[-1]], dma=True)
        evl2 = p.emit(SY, dmaf(sq_s[:, 0:800], wd_nat[128 * k:128 * (k + 1), 2016:2816]),
                      waits=[evl], dma=True)
        e_r1 = p.emit(V, red(scol[:, 10:11], xring[r][:, 0:2016], ALU.max, absv=True),
                      waits=[evl])
        e_r2 = p.emit(V, red(scol[:, 11:12], sq_s[:, 0:800], ALU.max, absv=True), waits=[evl2])
        e_mx = p.emit(V, tt(scol[:, 10:11], scol[:, 10:11], scol[:, 11:12], ALU.max),
                      waits=[e_r1, e_r2])
        e_sd = p.emit(V, ts(sd_c[:, k:k + 1], scol[:, 10:11], 1.0 / 127.0), waits=[e_mx])
        e_rc = p.emit(V, recip(scol[:, 12:13], scol[:, 10:11]), waits=[e_mx])
        e_ri = p.emit(V, ts(rinv_d[:, k:k + 1], scol[:, 12:13], 127.0), waits=[e_rc])
        ev_ds.append(e_ri if k % 2 else e_sd)
        ev_ds.append(e_sd if k % 2 else e_ri)
    e_tp1 = p.emit(TE, trf(ps_d0[0:8, :], rinv_d[:, :]), waits=[ev_ds[-1], ev_ds[-2]])
    e_te1 = p.emit(V, cp(misc[0:8, 0:128], ps_d0[0:8, :]), waits=[e_tp1])
    e_rw1 = p.emit(SY, dmaf(l1r2[0:1, 0:D], misc[0:8, 0:128]), waits=[e_te1], dma=True)
    e_rb1 = bcast1(rbH, l1r2[0:1, 0:D], D, [e_rw1])
    e_tp2 = p.emit(TE, trf(ps_d0[0:8, :], sd_c[:, :]), waits=[e_rb1, ev_ds[-1]])
    e_te2 = p.emit(V, cp(misc[0:8, 0:128], ps_d0[0:8, :]), waits=[e_tp2])
    e_rw2 = p.emit(SY, dmaf(l1r2[0:1, 0:D], misc[0:8, 0:128]), waits=[e_te2], dma=True)
    e_sdr = p.emit(V, ts(l1r2[0:1, D:2 * D], l1r2[0:1, 0:D], srow[0:1, 131:132]),
                   waits=[e_rw2, e_s2])
    e_sdb = bcast1(sdh_b, l1r2[0:1, D:2 * D], D, [e_sdr])
    PSD.close()
    cut("sdown")

    wdq = [sbf(f"wdq{k}", [128, 512], BF16) for k in range(IC)]
    or_t = sbf("or_t", [128, D], BF16)
    gh_t = sbf("gh_t", [128, 512], BF16)
    sgf = sbf("sgf", [128, 512])
    xnb = sbf("xnb", [128, 128])
    xo_st = sbf("xo_st", [128, 128])
    PDD = ExitStack()
    ps_dd = PDD.enter_context(nc.psum_tensor("ps_dd", [128, 512], F32))
    ps_tt = PDD.enter_context(nc.psum_tensor("ps_tt", [128, 512], F32))
    ev_out_st = []
    ev_last_sgtr = None
    for hf in range(2):
        ev_wq = []
        for k in range(IC):
            r = k % 2
            evl = p.emit(SY, dmaf(xring[r][:, 0:512],
                                  wd_T[128 * k:128 * (k + 1), 512 * hf:512 * (hf + 1)]),
                         waits=[ev_wq[-2] if len(ev_wq) >= 2 else (e_sdb if hf == 0 else ev_out_st[-1])],
                         dma=True)
            e_v = p.emit(V, tt(misc[:, 0:512], xring[r][:, 0:512],
                               rbH[:, 512 * hf:512 * (hf + 1)], ALU.mult),
                         waits=[evl, e_rb1] + ([ev_wq[-1]] if ev_wq else []))
            e_r = p.emit(V, ts(wdq[k][:], misc[:, 0:512], MAGIC, MAGIC, ALU.add, ALU.subtract),
                         waits=[e_v])
            ev_wq.append(e_r)
        for m in range(16):
            e_ol = p.emit(SY, dmaf(or_t[:, :], o_dram[m, :, :]),
                          waits=[ev_ost[m]] + ([ev_out_st[-1]] if ev_out_st else []), dma=True)
            e_dm = p.emit(TE, [mm(ps_dd[:, :], hp_t[k][:, 128 * m:128 * (m + 1)], wdq[k][:],
                                  k == 0, k == IC - 1) for k in range(IC)],
                          waits=[ev_wq[-1], ev_hq[-1]] + ([ev_last_sgtr] if ev_last_sgtr else []))
            e_sc = p.emit(V, tt(misc[:, 0:512], ps_dd[:, :], sdh_b[:, 512 * hf:512 * (hf + 1)],
                                ALU.mult), waits=[e_dm, e_sdb])
            e_g = p.emit(A, act(gh_t[:], misc[:, 0:512], AF.Silu), waits=[e_sc])
            e_sg = p.emit(V, tt(sgf[:], gh_t[:], or_t[:, 512 * hf:512 * (hf + 1)], ALU.mult),
                          waits=[e_g, e_ol])
            for j in range(4):
                e_tr = p.emit(TE, trf(ps_tt[:, 128 * j:128 * (j + 1)], sgf[:, 128 * j:128 * (j + 1)]),
                              waits=[e_sg] + ([ev_last_sgtr] if ev_last_sgtr else []))
                kk = 4 * hf + j
                e_xl = p.emit(SY, dmaf(xnb[:, :], xnew_dram[128 * kk:128 * (kk + 1),
                                                            128 * m:128 * (m + 1)]),
                              waits=[ev_xnew_st[kk]] + ([ev_out_st[-1]] if ev_out_st else []),
                              dma=True)
                e_ad = p.emit(V, tt(xo_st[:, :], ps_tt[:, 128 * j:128 * (j + 1)], xnb[:, :],
                                    ALU.add), waits=[e_tr, e_xl] +
                              ([ev_out_st[-1]] if ev_out_st else []))
                e_st = p.emit(SY, dmaf(out_t[128 * kk:128 * (kk + 1), 128 * m:128 * (m + 1)],
                                       xo_st[:, :]), waits=[e_ad], dma=True)
                ev_out_st.append(e_st)
                ev_last_sgtr = e_ad
    PDD.close()
    FF.close()
    L1.close()

    _finish(nc, p, out_t, sq_s)
    return nc


def _finish(nc, p, out_t, sq_s):
    plan_sems = p.sems

    def final_wait(e):
        for eng in ("sync", "gpsimd", "scalar"):
            for j in range(Plan.RING[eng]):
                sname = f"d{eng}{j}"
                if p.cnt[sname] > 0:
                    e.wait_ge(plan_sems[sname], p.cnt[sname])

    p.streams["sync"].append(final_wait)
    p.play()
    nv = p.psum_report()
    if nv and not os.environ.get("KPSUM_OK"):
        raise RuntimeError(f"{nv} unordered cross-engine PSUM bank access pairs")


def _to_il(a):
    return np.ascontiguousarray(a.transpose(2, 1, 0).reshape(a.shape[2], -1))


def build_safe(Dr, dbg=(), stop_after=None):
    if stop_after is None:
        return build_program(Dr, dbg)
    try:
        build_program(Dr, dbg, stop_after)
    except StopPlan as e:
        return e.nc
    raise RuntimeError(f"stop_after tag {stop_after!r} not reached")


def kernel(**inp):
    X = np.asarray(inp["L0_attn_dense"], np.float32)
    rows_i = np.asarray(inp["L0_attn_adj_rows"]).astype(np.int64)
    cols_i = np.asarray(inp["L0_attn_adj_cols"]).astype(np.int64)
    vals = np.asarray(inp["L0_attn_adj_vals"], np.float32)

    keep = rows_i < 2048
    rows_k = rows_i[keep]
    cols_k = np.minimum(cols_i[keep], 2047)
    vals_k = vals[keep]
    key = rows_k * 2048 + cols_k
    order = np.argsort(key, kind="stable")
    sk = key[order]
    first = np.r_[True, sk[1:] != sk[:-1]]
    idx = np.arange(len(sk))
    grp_start = np.maximum.accumulate(np.where(first, idx, 0))
    occ = idx - grp_start
    Dr = int(occ.max()) + 1 if len(occ) else 1
    canv = np.zeros((Dr, 2048, 2048), np.float32)
    canv[occ, cols_k[order], rows_k[order]] = vals_k[order]

    qkv_w = np.asarray(inp["L0_attn_qkv_w"], np.float32)
    wv_w = np.asarray(inp["L0_attn_wv_w"], np.float32)
    wv_b = np.asarray(inp["L0_attn_wv_b"], np.float32)
    out_w = np.asarray(inp["L0_attn_out_w"], np.float32)
    out_b = np.asarray(inp["L0_attn_out_b"], np.float32)
    wq_g = np.asarray(inp["L0_attn_wq"], np.float32)
    wk_g = np.asarray(inp["L0_attn_wk"], np.float32)
    conv_w = np.asarray(inp["L0_attn_conv_w"], np.float32)
    w1 = np.asarray(inp["L0_ffn_w1"], np.float32)
    b1 = np.asarray(inp["L0_ffn_b1"], np.float32)
    w2 = np.asarray(inp["L0_ffn_w2"], np.float32)
    b2 = np.asarray(inp["L0_ffn_b2"], np.float32)
    cent = np.asarray(inp["L0_ffn_centroids"], np.float32)
    w_route = np.asarray(inp["L0_ffn_w_route"], np.float32)
    Wq1 = np.asarray(inp["L1_attn_Wq"], np.float32)
    bq1 = np.asarray(inp["L1_attn_bq"], np.float32)
    Wk1 = np.asarray(inp["L1_attn_Wk"], np.float32)
    bk1 = np.asarray(inp["L1_attn_bk"], np.float32)
    Wv1 = np.asarray(inp["L1_attn_Wv"], np.float32)
    bv1 = np.asarray(inp["L1_attn_bv"], np.float32)
    clip_x = np.asarray(inp["L1_ffn_clip_x"], np.float32)
    clip_d = np.asarray(inp["L1_ffn_clip_down"], np.float32)
    w_gate = np.asarray(inp["L1_ffn_w_gate"], np.float32)
    w_up = np.asarray(inp["L1_ffn_w_up"], np.float32)
    w_down = np.asarray(inp["L1_ffn_w_down"], np.float32)

    taps = np.ascontiguousarray(np.repeat(conv_w[:, 0, :, 0], 64, axis=0))
    l1wT = np.ascontiguousarray(np.stack([Wq1.T, Wk1.T, Wv1.T], 0))
    l1bqk = np.ascontiguousarray(np.concatenate([bq1.reshape(8, 128).T,
                                                 bk1.reshape(8, 128).T], axis=1))
    b1cols = np.ascontiguousarray(b1.reshape(8, 4, 128).transpose(2, 0, 1).reshape(128, 32))

    shared = {
        "qkv_w": qkv_w, "wv_w": wv_w,
        "wv_b": np.ascontiguousarray(wv_b.reshape(8, 128).T),
        "out_w": out_w,
        "out_b": np.ascontiguousarray(out_b.reshape(8, 128).T),
        "wq_g": wq_g, "wk_g": wk_g, "conv_taps": taps,
        "moe_w1": w1, "moe_b1": b1cols, "moe_w2": w2, "moe_b2": b2,
        "cent": cent, "cent_T": np.ascontiguousarray(cent.T),
        "route_T": np.ascontiguousarray(w_route.T),
        "l1_w_T": l1wT, "l1_bqk": l1bqk,
        "l1_bv_row": np.ascontiguousarray(bv1.reshape(1, 1024)),
        "l1_clip": np.ascontiguousarray(np.stack([clip_x[0], clip_d[0]]).reshape(1, 2)),
        "wg_nat": w_gate, "wu_nat": w_up, "wd_nat": w_down,
        "wg_T": np.ascontiguousarray(w_gate.T),
        "wu_T": np.ascontiguousarray(w_up.T),
        "wd_T": np.ascontiguousarray(w_down.T),
    }

    in_maps = []
    for c in range(NCORES):
        im = dict(shared)
        lo = S * c
        im["x_fm"] = _to_il(X[:, lo:lo + S, :])
        left = X[0, lo - 16:lo, :].T if c > 0 else np.zeros((D, 16), np.float32)
        right = X[0, lo + S:lo + S + 16, :].T if c < NCORES - 1 else np.zeros((D, 16), np.float32)
        im["x_halo"] = np.ascontiguousarray(np.concatenate([left, right], axis=1))
        im["qkvw_head"] = np.ascontiguousarray(
            np.concatenate([qkv_w[:, 64 * c:64 * (c + 1)],
                            qkv_w[:, 512 + 64 * c:512 + 64 * (c + 1)]], axis=1))
        sel = np.zeros((512, 64), np.float32)
        sel[64 * c:64 * (c + 1)] = np.eye(64, dtype=np.float32)
        im["sel_head"] = sel.astype(bfd).astype(np.float32) if False else sel
        for d in range(Dr):
            im[f"wt_can{d}"] = np.ascontiguousarray(canv[d][:, lo:lo + S])
        in_maps.append(im)

    nc = build_safe(Dr, stop_after=os.environ.get("KSTOP") or None)
    res = run_bass_kernel_spmd(nc, in_maps, core_ids=list(range(NCORES)))
    out = np.empty((8, 2048, 1024), np.float32)
    for c in range(NCORES):
        o = res.results[c]["out_il"]
        out[:, S * c:S * (c + 1), :] = o.reshape(D, S, 8).transpose(2, 1, 0)
    return out
